# version 12 (final)
# speedup vs baseline: 1.0555x; 1.0270x over previous
import numpy as np
import ml_dtypes
import concourse.bass as bass
import concourse.mybir as mybir
from concourse.bass_utils import run_bass_kernel_spmd
from concourse.alu_op_type import AluOpType as ALU
from contextlib import ExitStack

F32 = mybir.dt.float32
BF16 = mybir.dt.bfloat16
AF = mybir.ActivationFunctionType
AX = mybir.AxisListType

BF = ml_dtypes.bfloat16
N_CORES = 8
B = 16
L = 2048
D = 1024
SEG = 512
OWN = 256
NSLOT = 4

ENGINES = ("sync", "scalar", "vector", "tensor", "gpsimd")
SEM_ENG = {"ld": "sync", "ld2": "scalar", "gld": "gpsimd", "gw": "gpsimd", "st": "gpsimd",
           "pe": "tensor", "act": "scalar", "dve": "vector", "pool": "gpsimd"}
DMA_SEMS = {"ld", "ld2", "gld", "gw", "st"}


class Ctx:

    def __init__(self):
        self.q = {e: [] for e in ENGINES}
        self.cnt = {s: 0 for s in SEM_ENG}
        self.waited = {e: {s: 0 for s in SEM_ENG} for e in ENGINES}

    def emit(self, eng, fn, deps=(), inc=None):
        step = 16 if inc in DMA_SEMS else 1
        tok = None
        if inc is not None:
            self.cnt[inc] += step
            tok = (inc, self.cnt[inc])
        need = []
        w = self.waited[eng]
        best = {}
        for (s, c) in deps:
            if c is None or c <= 0:
                continue
            if eng == "tensor" and SEM_ENG[s] == "tensor":
                continue
            if c <= w[s]:
                continue
            if c > best.get(s, 0):
                best[s] = c
        for s, c in best.items():
            w[s] = c
            need.append((s, c))
        self.q[eng].append((tuple(need), fn, inc, step))
        return tok

    def replay(self, eng, api, sems):
        for need, fn, inc, step in self.q[eng]:
            for (s, c) in need:
                api.wait_ge(sems[s], c)
            ins = fn(api, sems)
            if inc is not None:
                ins.then_inc(sems[inc], step)


def tmax(*toks):
    out = {}
    def add(t):
        if t is None:
            return
        if isinstance(t, list):
            for u in t:
                add(u)
            return
        s, c = t
        if c is not None and c > out.get(s, 0):
            out[s] = c
    for t in toks:
        add(t)
    return [(s, c) for s, c in out.items()]



def alloc_p1(sb, ps):
    H = {}
    H["wqb"] = sb("wqb", [128, 8 * 1024], BF16)
    H["wkb"] = sb("wkb", [128, 8 * 1024], BF16)
    H["wvb"] = sb("wvb", [128, 8 * 1024], BF16)
    H["wob"] = sb("wob", [128, 8 * 1024], BF16)
    H["bqc"] = sb("bqc", [128, 8], F32)
    H["bkc"] = sb("bkc", [128, 8], F32)
    H["repbv"] = sb("repbv", [128, 1024], F32)
    H["bobf"] = sb("bobf", [1, 1024], BF16)
    H["onebf"] = sb("onebf", [1, 128], BF16)
    H["onesf"] = sb("onesf_s", [1, 128], F32)
    H["wmaskr"] = sb("wmaskr", [1, 4], F32)
    H["repwm"] = sb("repwm", [128, 4], F32)
    H["identb"] = sb("identb", [128, 128], BF16)
    H["xfm"] = sb("xfm", [128, 2 * (8 * 512)], BF16)
    H["xnfm"] = sb("xnfm", [128, 2 * (8 * 512)], BF16)
    H["qfm"] = sb("qfm", [128, 2 * (8 * 256)], BF16)
    H["kfm"] = sb("kfm", [128, 2 * (8 * 512)], BF16)
    H["kdfm"] = sb("kdfm", [128, 2 * (8 * 512)], BF16)
    H["vsb"] = sb("vsb", [128, 2 * (4 * 1024)], BF16)
    H["vdsb"] = sb("vdsb", [128, 2 * (4 * 1024)], BF16)
    H["exf"] = sb("exf", [128, 2048], F32)
    H["smx"] = sb("smx", [128, 64], F32)
    H["abf"] = sb("abf", [128, 2048], BF16)
    H["aTl"] = sb("aTl", [128, 2 * 1024], BF16)
    H["aTc"] = sb("aTc", [128, 2 * 1024], BF16)
    H["att"] = sb("att", [128, 8 * 256], BF16)
    H["r32"] = sb("r32", [128, 2 * 1024], F32)
    H["mvar"] = sb("mvar", [128, 32], F32)
    H["epsc"] = sb("epsc", [128, 1], F32)
    for nm in ("psA", "psB", "psC", "psD", "psE", "psF", "psG"):
        H[nm] = ps(nm, [128, 512], F32)
    H["ptr"] = ps("ptr", [128, 1024], BF16)
    H["ptr2"] = H["psG"][:].bitcast(BF16)
    return H


def emit_p1(ctx, H, IN, x1_out, dbg=None):
    E = ctx.emit
    idb = H["identb"]

    t_wld = []
    for nm, src in (("wqb", "wq"), ("wkb", "wk"), ("wvb", "wv"), ("wob", "wo")):
        for k in range(8):
            t_wld.append(E("gpsimd", lambda g, s, nm=nm, src=src, k=k: g.dma_start(
                out=H[nm][:, k * 1024:(k + 1) * 1024], in_=IN[src][k * 128:(k + 1) * 128, :]),
                inc="gw"))
    t_b = []
    for nm, src in (("bqc", "bq"), ("bkc", "bk")):
        for m in range(8):
            t_b.append(E("sync", lambda s_, s, nm=nm, src=src, m=m: s_.dma_start(
                out=H[nm][:, m:m + 1],
                in_=IN[src][m * 128:(m + 1) * 128].rearrange("(p x) -> p x", x=1)),
                inc="ld"))
    t_b.append(E("sync", lambda s_, s: s_.dma_start(out=H["r32"][0:1, 0:1024], in_=IN["bv"][:]), inc="ld"))
    t_b.append(E("sync", lambda s_, s: s_.dma_start(out=H["onesf"][0:1, :], in_=IN["onesf"][:]), inc="ld"))
    t_b.append(E("sync", lambda s_, s: s_.dma_start(out=H["wmaskr"][0:1, :], in_=IN["wmask"][:]), inc="ld"))
    t_b.append(E("sync", lambda s_, s: s_.dma_start(out=H["identb"][:], in_=IN["ident"][:]), inc="ld"))
    t_bld = t_b[-1]

    jobs = [("psA", "repbv", 0), ("psB", "repbv", 512)]
    t_rep = []
    for i, (pq, dst, off) in enumerate(jobs):
        t = E("tensor", lambda te, s, pq=pq, off=off: te.matmul(
            out=H[pq][:, 0:512], lhsT=H["onesf"][0:1, :],
            rhs=H["r32"][0:1, off:off + 512], start=True, stop=True),
            deps=tmax(t_bld), inc="pe")
        t_rep.append(E("vector", lambda v, s, pq=pq, dst=dst, off=off: v.tensor_copy(
            out=H[dst][:, off % 1024:off % 1024 + 512], in_=H[pq][:, 0:512]),
            deps=tmax(t), inc="dve"))
    t_rep.append(E("gpsimd", lambda g, s: g.dma_start(out=H["bobf"][0:1, :], in_=IN["bo"][:]), inc="gw"))
    t_rep.append(E("vector", lambda v, s: v.memset(H["onebf"][0:1, :], 1.0), inc="dve"))
    t_rep = t_rep + [t_rep[-1]]
    t = E("tensor", lambda te, s: te.matmul(
        out=H["psA"][:, 0:4], lhsT=H["onesf"][0:1, :], rhs=H["wmaskr"][0:1, :],
        start=True, stop=True), deps=tmax(t_bld, t_rep[0], t_rep[1]), inc="pe")
    t_rep.append(E("vector", lambda v, s: v.tensor_copy(
        out=H["repwm"][:], in_=H["psA"][:, 0:4]), deps=tmax(t), inc="dve"))
    t_wready = tmax(t_wld[-1], t_b[7], t_b[15], *t_rep)
    t_eps = E("vector", lambda v, s: v.memset(H["epsc"][:], 1e-5), inc="dve")

    last_xof_use = [None]
    last_xdf_use = [None]
    prev_ptr_user = [None, None]
    prev_aTl_rd = [None, None]
    prev_aTc_rd = [None, None]
    prev_store = {0: None, 1: None}
    prev_mvar_act = {0: None, 1: None}
    prev_abf_user = {0: None, 1: None}
    prev_exf_user = {0: None, 1: None}
    prev_smx_user = {0: None, 1: None}
    prev_exp = {}
    t_avdone = {"psF": None, "psG": None}
    psq = ("psA", "psB")

    QW, KW, VW = 8 * 256, 8 * 512, 8 * 512
    XW = 8 * 512
    VSW = 4 * 1024

    def load_x(b, dep_tok):
        bp = b % 2
        toks = []
        for src_nm, dst, W in (("xofm", "xfm", XW), ("xdfm", "xnfm", XW)):
            for k in range(8):
                toks.append(E("gpsimd", lambda g, s, b=b, k=k, src_nm=src_nm, dst=dst, W=W, bp=bp: g.dma_start(
                    out=H[dst][:, bp * W + k * 512:bp * W + (k + 1) * 512],
                    in_=IN[src_nm][b, k * 128:(k + 1) * 128, :]),
                    deps=tmax(dep_tok) if not toks else (), inc="gld"))
        return toks[-1]

    def stage_B(b, t_xready, first):
        bp = b % 2
        t_q = []
        psq = ("psA", "psB")
        for m in range(8):
            pq = psq[m % 2]
            dep = tmax(t_xready, t_q[m - 2] if m >= 2 else projB_war[m % 2])
            if first and m == 0:
                dep = dep + t_wready
            mm = None
            for k in range(8):
                mm = E("tensor", lambda te, s, m=m, k=k, pq=pq, bp=bp: te.matmul(
                    out=H[pq][:, 0:256],
                    lhsT=H["wqb"][:, k * 1024 + m * 128:k * 1024 + (m + 1) * 128],
                    rhs=H["xfm"][:, bp * XW + k * 512:bp * XW + k * 512 + 256],
                    start=(k == 0), stop=(k == 7)),
                    deps=(dep if k == 0 else ()), inc=("pe" if k == 7 else None))
            t_q.append(E("vector", lambda v, s, m=m, pq=pq, bp=bp: v.tensor_scalar(
                out=H["qfm"][:, bp * QW + m * 256:bp * QW + (m + 1) * 256], in0=H[pq][:, 0:256],
                scalar1=H["bqc"][:, m:m + 1], scalar2=0.125, op0=ALU.add, op1=ALU.mult),
                deps=tmax(mm), inc="dve"))
            projB_war[m % 2] = t_q[-1]
            yield None
        t_k = []
        for m in range(8):
            pq = psq[m % 2]
            dep = tmax(t_k[m - 2] if m >= 2 else projB_war[m % 2])
            mm = None
            for k in range(8):
                mm = E("tensor", lambda te, s, m=m, k=k, pq=pq, bp=bp: te.matmul(
                    out=H[pq][:, 0:512],
                    lhsT=H["wkb"][:, k * 1024 + m * 128:k * 1024 + (m + 1) * 128],
                    rhs=H["xfm"][:, bp * XW + k * 512:bp * XW + (k + 1) * 512],
                    start=(k == 0), stop=(k == 7)),
                    deps=(dep if k == 0 else ()), inc=("pe" if k == 7 else None))
            t_k.append(E("vector", lambda v, s, m=m, pq=pq, bp=bp: v.tensor_scalar(
                out=H["kfm"][:, bp * KW + m * 512:bp * KW + (m + 1) * 512], in0=H[pq][:, 0:512],
                scalar1=H["bkc"][:, m:m + 1], scalar2=None, op0=ALU.add),
                deps=tmax(mm), inc="dve"))
            projB_war[m % 2] = t_k[-1]
            yield None
        t_kd = []
        for m in range(8):
            pq = psq[m % 2]
            dep = tmax(t_kd[m - 2] if m >= 2 else projB_war[m % 2])
            mm = None
            for k in range(8):
                mm = E("tensor", lambda te, s, m=m, k=k, pq=pq, bp=bp: te.matmul(
                    out=H[pq][:, 0:512],
                    lhsT=H["wkb"][:, k * 1024 + m * 128:k * 1024 + (m + 1) * 128],
                    rhs=H["xnfm"][:, bp * XW + k * 512:bp * XW + (k + 1) * 512],
                    start=(k == 0), stop=(k == 7)),
                    deps=(dep if k == 0 else ()), inc=("pe" if k == 7 else None))
            t_kd.append(E("vector", lambda v, s, m=m, pq=pq, bp=bp: v.tensor_scalar(
                out=H["kdfm"][:, bp * KW + m * 512:bp * KW + (m + 1) * 512], in0=H[pq][:, 0:512],
                scalar1=H["bkc"][:, m:m + 1], scalar2=None, op0=ALU.add),
                deps=tmax(mm), inc="dve"))
            projB_war[m % 2] = t_kd[-1]
            yield None
        t_v = []
        for srcb, dst in (("xfm", "vsb"), ("xnfm", "vdsb")):
            for tc in range(4):
                for dh in range(2):
                    i = len(t_v)
                    pq = psq[i % 2]
                    dep = tmax(t_v[i - 2] if i >= 2 else projB_war[i % 2])
                    mm = None
                    for k in range(8):
                        mm = E("tensor", lambda te, s, srcb=srcb, tc=tc, dh=dh, k=k, pq=pq, bp=bp: te.matmul(
                            out=H[pq][:, 0:512],
                            lhsT=H[srcb][:, bp * XW + k * 512 + tc * 128:bp * XW + k * 512 + (tc + 1) * 128],
                            rhs=H["wvb"][:, k * 1024 + dh * 512:k * 1024 + (dh + 1) * 512],
                            start=(k == 0), stop=(k == 7)),
                            deps=(dep if k == 0 else ()), inc=("pe" if k == 7 else None))
                    t_v.append(E("vector", lambda v, s, dst=dst, tc=tc, dh=dh, pq=pq, bp=bp: v.tensor_tensor(
                        out=H[dst][:, bp * VSW + tc * 1024 + dh * 512:bp * VSW + tc * 1024 + (dh + 1) * 512],
                        in0=H[pq][:, 0:512], in1=H["repbv"][:, dh * 512:(dh + 1) * 512],
                        op=ALU.add), deps=tmax(mm), inc="dve"))
                    projB_war[i % 2] = t_v[-1]
                    yield None
        proj_done[b % 2] = tmax(t_q[-1], t_k[-1], t_kd[-1], *t_v)

    projB_war = [None, None]
    proj_done = [None, None]
    prev_exp = {}
    t_avdone = {"psF": None}
    t_lb_tok = load_x(0, None)
    genB = stage_B(0, tmax(t_lb_tok), True)
    for _ in genB:
        pass
    next_gen = None
    for b in range(B):
        bp = b % 2
        t_ld = []
        for i in range(2):
            t_ld.append(E("sync", lambda s_, s, b=b, i=i: s_.dma_start(
                out=H["r32"][:, i * 1024:(i + 1) * 1024],
                in_=IN["xo"][b, i * 128:(i + 1) * 128, :]),
                deps=tmax(*last_xof_use, t_rep[0], t_rep[1],
                          prev_store[0], prev_store[1]), inc="ld"))
        if b + 1 < B:
            tok = load_x(b + 1, proj_done[(b + 1) % 2])
            next_gen = stage_B(b + 1, tmax(tok), False)
        else:
            next_gen = iter(())
        t_proj_done = proj_done[bp]

        t_att_evac = []
        for h in range(16):
            hp, ho = h // 2, (h % 2) * 64
            par = h % 2
            eB, aB, sB = par * 1024, par * 1024, par * 32
            pav = "psF"
            ptrh = ("ptr", "ptr2")[par]
            pl01 = (("psD", "psE") if par == 0 else ("psD", "psE"))
            pl01 = (("psD", "psE"))
            for _ in range(2):
                try:
                    next(next_gen)
                except StopIteration:
                    pass
            t_sc = []
            for qc, pl in ((0, pl01[0]), (1, pl01[1])):
                dep = tmax(t_proj_done, prev_exp.get(("c", pl)), prev_exp.get(("l", pl)))
                t_sc.append(E("tensor", lambda te, s, hp=hp, ho=ho, qc=qc, pl=pl, bp=bp: te.matmul(
                    out=H[pl][:, 0:512],
                    lhsT=H["qfm"][ho:ho + 64, bp * QW + hp * 256 + qc * 128:bp * QW + hp * 256 + (qc + 1) * 128],
                    rhs=H["kfm"][ho:ho + 64, bp * KW + hp * 512:bp * KW + (hp + 1) * 512],
                    start=True, stop=True), deps=dep, inc="pe"))
            t_abf = []
            for qc, pl in ((0, pl01[0]), (1, pl01[1])):
                t1 = E("vector", lambda v, s, qc=qc, pl=pl, sB=sB: v.reduce_max(
                    out=H["smx"][:, sB + qc:sB + qc + 1], in_=H[pl][:, 0:512],
                    axis=AX.X, negate=True),
                    deps=tmax(t_sc[qc], prev_smx_user[par]), inc="dve")
                t2 = E("scalar", lambda sc, s, qc=qc, pl=pl, sB=sB, eB=eB: sc.activation(
                    out=H["exf"][:, eB + qc * 512:eB + (qc + 1) * 512], in_=H[pl][:, 0:512],
                    func=AF.Exp, bias=H["smx"][:, sB + qc:sB + qc + 1], scale=1.0),
                    deps=tmax(t1, prev_exf_user[par]), inc="act")
                prev_exp[("l", pl)] = t2
                t2b = E("vector", lambda v, s, qc=qc, sB=sB, eB=eB: v.reduce_sum(
                    out=H["smx"][:, sB + 2 + qc:sB + 3 + qc],
                    in_=H["exf"][:, eB + qc * 512:eB + (qc + 1) * 512], axis=AX.X),
                    deps=tmax(t2), inc="dve")
                t3 = E("vector", lambda v, s, qc=qc, sB=sB: v.reciprocal(
                    out=H["smx"][:, sB + 4 + qc:sB + 5 + qc],
                    in_=H["smx"][:, sB + 2 + qc:sB + 3 + qc]),
                    deps=tmax(t2b), inc="dve")
                t_abf.append(E("scalar", lambda sc, s, qc=qc, sB=sB, eB=eB, aB=aB: sc.activation(
                    out=H["abf"][:, aB + qc * 512:aB + (qc + 1) * 512],
                    in_=H["exf"][:, eB + qc * 512:eB + (qc + 1) * 512],
                    func=AF.Copy, scale=H["smx"][:, sB + 4 + qc:sB + 5 + qc], bias=0.0),
                    deps=tmax(t2, t3, prev_abf_user[par]), inc="act"))
            for _ in range(1):
                try:
                    next(next_gen)
                except StopIteration:
                    pass
            tt = None
            for qc in range(2):
                for kc in range(4):
                    dep = tmax(t_abf[0], t_abf[1], prev_ptr_user[par]) if (qc == 0 and kc == 0) else ()
                    tt = E("tensor", lambda te, s, qc=qc, kc=kc, aB=aB, ptrh=ptrh: te.transpose(
                        out=H[ptrh][:, kc * 256 + qc * 128:kc * 256 + (qc + 1) * 128],
                        in_=H["abf"][:, aB + qc * 512 + kc * 128:aB + qc * 512 + (kc + 1) * 128],
                        identity=idb[:]),
                        deps=dep, inc=("pe" if (qc == 1 and kc == 3) else None))
            t_aTl = E("vector", lambda v, s, ptrh=ptrh, aB=aB: v.tensor_copy(
                out=H["aTl"][:, aB:aB + 1024], in_=H[ptrh][:]),
                deps=tmax(tt, prev_aTl_rd[par]), inc="dve")
            prev_ptr_user[par] = t_aTl
            mmav = None
            for kc in range(4):
                dep = tmax(t_aTl, t_avdone[pav]) if kc == 0 else ()
                mmav = E("tensor", lambda te, s, h=h, ho=ho, kc=kc, pav=pav, aB=aB, bp=bp: te.matmul(
                    out=H[pav][ho:ho + 64, 0:256],
                    lhsT=H["vsb"][:, bp * VSW + kc * 1024 + h * 64:bp * VSW + kc * 1024 + (h + 1) * 64],
                    rhs=H["aTl"][:, aB + kc * 256:aB + (kc + 1) * 256],
                    start=(kc == 0), stop=False), deps=dep, inc=None)
            for qc, pl in ((0, pl01[0]), (1, pl01[1])):
                mm = None
                dep0 = tmax(prev_exp.get(("l", pl)))
                for j in range(4):
                    mm = E("tensor", lambda te, s, hp=hp, ho=ho, qc=qc, j=j, pl=pl, bp=bp: te.matmul(
                        out=H[pl][:, j * 128:(j + 1) * 128],
                        lhsT=H["qfm"][ho:ho + 64, bp * QW + hp * 256 + qc * 128:bp * QW + hp * 256 + (qc + 1) * 128],
                        rhs=H["kdfm"][ho:ho + 64, bp * KW + hp * 512 + j * 128:bp * KW + hp * 512 + (j + 1) * 128],
                        start=True, stop=True),
                        deps=(dep0 if j == 0 else ()), inc=("pe" if j == 3 else None))
                t1 = E("vector", lambda v, s, qc=qc, pl=pl, sB=sB: v.tensor_reduce(
                    out=H["smx"][:, sB + 8 + qc * 4:sB + 12 + qc * 4],
                    in_=H[pl][:, 0:512].rearrange("p (j x) -> p j x", j=4),
                    axis=AX.X, op=ALU.max, negate=True),
                    deps=tmax(mm), inc="dve")
                texp = None
                for j in range(4):
                    texp = E("scalar", lambda sc, s, qc=qc, j=j, pl=pl, sB=sB, eB=eB: sc.activation(
                        out=H["exf"][:, eB + qc * 512 + j * 128:eB + qc * 512 + (j + 1) * 128],
                        in_=H[pl][:, j * 128:(j + 1) * 128], func=AF.Exp,
                        bias=H["smx"][:, sB + 8 + qc * 4 + j:sB + 9 + qc * 4 + j], scale=1.0),
                        deps=tmax(t1, t_abf[qc]) if j == 0 else (), inc=("act" if j == 3 else None))
                prev_exp[("c", pl)] = texp
                t2s = E("vector", lambda v, s, qc=qc, sB=sB, eB=eB: v.tensor_reduce(
                    out=H["smx"][:, sB + 16 + qc * 4:sB + 20 + qc * 4],
                    in_=H["exf"][:, eB + qc * 512:eB + (qc + 1) * 512].rearrange("p (j x) -> p j x", j=4),
                    axis=AX.X, op=ALU.add),
                    deps=tmax(texp), inc="dve")
                t2 = E("vector", lambda v, s, qc=qc, sB=sB: v.reciprocal(
                    out=H["smx"][:, sB + 24 + qc * 4:sB + 28 + qc * 4],
                    in_=H["smx"][:, sB + 16 + qc * 4:sB + 20 + qc * 4]),
                    deps=tmax(t2s), inc="dve")
                t3 = E("vector", lambda v, s, qc=qc, sB=sB: v.tensor_tensor(
                    out=H["smx"][:, sB + 24 + qc * 4:sB + 28 + qc * 4],
                    in0=H["smx"][:, sB + 24 + qc * 4:sB + 28 + qc * 4], in1=H["repwm"][:, 0:4],
                    op=ALU.mult), deps=tmax(t2), inc="dve")
                for j in range(4):
                    E("scalar", lambda sc, s, qc=qc, j=j, sB=sB, eB=eB, aB=aB: sc.activation(
                        out=H["abf"][:, aB + qc * 512 + j * 128:aB + qc * 512 + (j + 1) * 128],
                        in_=H["exf"][:, eB + qc * 512 + j * 128:eB + qc * 512 + (j + 1) * 128],
                        func=AF.Copy, scale=H["smx"][:, sB + 24 + qc * 4 + j:sB + 25 + qc * 4 + j],
                        bias=0.0),
                        deps=tmax(t3, texp, t_aTl), inc="act")
            t_cabf = ("act", ctx.cnt["act"])
            for _ in range(1):
                try:
                    next(next_gen)
                except StopIteration:
                    pass
            tt = None
            for qc in range(2):
                for j in range(4):
                    dep = tmax(t_cabf, prev_ptr_user[par]) if (qc == 0 and j == 0) else ()
                    tt = E("tensor", lambda te, s, qc=qc, j=j, aB=aB, ptrh=ptrh: te.transpose(
                        out=H[ptrh][:, j * 256 + qc * 128:j * 256 + (qc + 1) * 128],
                        in_=H["abf"][:, aB + qc * 512 + j * 128:aB + qc * 512 + (j + 1) * 128],
                        identity=idb[:]),
                        deps=dep, inc=("pe" if (qc == 1 and j == 3) else None))
            t_aTc = E("vector", lambda v, s, ptrh=ptrh, aB=aB: v.tensor_copy(
                out=H["aTc"][:, aB:aB + 1024], in_=H[ptrh][:]),
                deps=tmax(tt, prev_aTc_rd[par]), inc="dve")
            prev_ptr_user[par] = t_aTc
            prev_abf_user[par] = t_aTc
            prev_exf_user[par] = t_cabf
            prev_smx_user[par] = t_cabf
            for j in range(4):
                mmav = E("tensor", lambda te, s, h=h, ho=ho, j=j, pav=pav, aB=aB, bp=bp: te.matmul(
                    out=H[pav][ho:ho + 64, 0:256],
                    lhsT=H["vdsb"][:, bp * VSW + j * 1024 + h * 64:bp * VSW + j * 1024 + (h + 1) * 64],
                    rhs=H["aTc"][:, aB + j * 256:aB + (j + 1) * 256],
                    start=False, stop=(j == 3)),
                    deps=tmax(t_aTc) if j == 0 else (), inc=("pe" if j == 3 else None))
            prev_aTl_rd[par] = mmav
            prev_aTc_rd[par] = mmav
            if h % 2 == 1:
                tev = E("vector", lambda v, s, h=h, pav=pav: v.tensor_copy(
                    out=H["att"][:, (h // 2) * 256:(h // 2 + 1) * 256], in_=H[pav][:, 0:256]),
                    deps=tmax(mmav), inc="dve")
                t_att_evac.append(tev)
                t_avdone[pav] = tev

        for _ in next_gen:
            pass

        t_r = []
        psd = ("psD", "psE")
        for tc in range(2):
            for dh in range(2):
                gi = tc * 2 + dh
                pq = psd[gi % 2]
                dep = tmax(*t_att_evac, t_r[-2] if len(t_r) >= 2 else
                           tmax(prev_exp.get(("l", "psD")), prev_exp.get(("l", "psE")),
                                prev_exp.get(("c", "psD")), prev_exp.get(("c", "psE"))))
                for k in range(8):
                    E("tensor", lambda te, s, tc=tc, dh=dh, k=k, pq=pq: te.matmul(
                        out=H[pq][:, 0:512],
                        lhsT=H["att"][:, k * 256 + tc * 128:k * 256 + (tc + 1) * 128],
                        rhs=H["wob"][:, k * 1024 + dh * 512:k * 1024 + (dh + 1) * 512],
                        start=(k == 0), stop=False),
                        deps=(dep if k == 0 else ()), inc=None)
                mm = E("tensor", lambda te, s, dh=dh, pq=pq: te.matmul(
                    out=H[pq][:, 0:512], lhsT=H["onebf"][0:1, :],
                    rhs=H["bobf"][0:1, dh * 512:(dh + 1) * 512],
                    start=False, stop=True), deps=(), inc="pe")
                t_r.append(E("vector", lambda v, s, tc=tc, dh=dh, pq=pq: v.tensor_tensor(
                    out=H["r32"][:, tc * 1024 + dh * 512:tc * 1024 + (dh + 1) * 512],
                    in0=H[pq][:, 0:512],
                    in1=H["r32"][:, tc * 1024 + dh * 512:tc * 1024 + (dh + 1) * 512],
                    op=ALU.add), deps=tmax(mm, t_ld[tc]), inc="dve"))
        prev_exp[("l", "psD")] = tmax(prev_exp.get(("l", "psD")), t_r[-2], t_r[-1])
        prev_exp[("l", "psE")] = tmax(prev_exp.get(("l", "psE")), t_r[-2], t_r[-1])
        t_store = []
        for tc in range(2):
            c0 = tc * 16
            t1 = E("vector", lambda v, s, tc=tc, c0=c0: v.bn_stats(
                out=H["mvar"][:, c0:c0 + 6], in_=H["r32"][:, tc * 1024:tc * 1024 + 512]),
                deps=tmax(t_r[tc * 2], prev_mvar_act[tc]), inc="dve")
            t1b = E("vector", lambda v, s, tc=tc, c0=c0: v.bn_stats(
                out=H["mvar"][:, c0 + 6:c0 + 12], in_=H["r32"][:, tc * 1024 + 512:tc * 1024 + 1024]),
                deps=tmax(t_r[tc * 2 + 1], t1), inc="dve")
            t2 = E("vector", lambda v, s, tc=tc, c0=c0: v.bn_aggr(
                out=H["mvar"][:, c0 + 12:c0 + 14], in_=H["mvar"][:, c0:c0 + 12]),
                deps=tmax(t1b), inc="dve")
            t3 = E("scalar", lambda sc, s, c0=c0: sc.activation(
                out=H["mvar"][:, c0 + 14:c0 + 15], in_=H["mvar"][:, c0 + 13:c0 + 14],
                func=AF.Sqrt, bias=H["epsc"][:, 0:1], scale=1.0), deps=tmax(t2, t_eps), inc="act")
            prev_mvar_act[tc] = t3
            t4 = E("vector", lambda v, s, c0=c0: v.reciprocal(
                out=H["mvar"][:, c0 + 15:c0 + 16], in_=H["mvar"][:, c0 + 14:c0 + 15]),
                deps=tmax(t3), inc="dve")
            t5 = E("vector", lambda v, s, tc=tc, c0=c0: v.tensor_scalar(
                out=H["r32"][:, tc * 1024:(tc + 1) * 1024],
                in0=H["r32"][:, tc * 1024:(tc + 1) * 1024],
                scalar1=H["mvar"][:, c0 + 12:c0 + 13], scalar2=H["mvar"][:, c0 + 15:c0 + 16],
                op0=ALU.subtract, op1=ALU.mult),
                deps=tmax(t4), inc="dve")
            last_xof_use.append(t5)
            t_store.append(E("gpsimd", lambda g, s, b=b, tc=tc: g.dma_start(
                out=x1_out[tc * 128:(tc + 1) * 128, b, :],
                in_=H["r32"][:, tc * 1024:(tc + 1) * 1024]),
                deps=tmax(t5), inc="st"))
            prev_store[tc] = t_store[tc]
        last_xof_use = [t_r[-1]] + last_xof_use[-2:]

    E("gpsimd", lambda g, s: g.nop(), deps=[("st", ctx.cnt["st"])], inc=None)



NEXP = 15
NST = 4
SLOT1 = 16384
SLOT2 = 8192


def alloc_p2(sb, ps):
    H = {}
    H["yacc"] = sb("yacc", [128, 8 * 1024], F32)
    H["x1fm"] = sb("x1fm", [128, 8 * 1024], BF16)
    H["xtb"] = sb("xtb", [128, 2 * 1024], BF16)
    H["fc1T"] = sb("fc1T", [128, 2 * SLOT1], BF16)
    H["fc2T"] = sb("fc2T", [128, 2 * SLOT2], BF16)
    H["abf"] = sb("abf2", [128, 16 * 1024], BF16)
    H["gel"] = sb("gel", [128, 2 * 512], F32)
    H["wstg"] = sb("wstg", [128, 3 * 1024], F32)
    H["comb"] = sb("comb", [128, 8 * 16], F32)
    H["combT"] = sb("combT", [15, 1024], BF16)
    H["cbt"] = sb("cbt", [128, 16], BF16)
    H["pbuf"] = sb("pbuf", [128, 64], F32)
    H["rstat"] = sb("rstat", [128, 8], F32)
    H["gwT"] = sb("gwT", [128, 8 * 16], F32)
    H["identf"] = sb("identf_s", [128, 128], F32)
    H["fc1b"] = sb("fc1b", [128, 240], F32)
    H["geglu"] = sb("geglu", [128, 120], F32)
    H["fc2brow"] = sb("fc2brow", [15, 1024], BF16)
    H["sfc1b"] = sb("sfc1b", [128, 32], F32)
    H["sgeglu"] = sb("sgeglu", [128, 16], F32)
    H["repsb"] = sb("repsb", [128, 1024], F32)
    H["identb"] = sb("identb", [128, 128], BF16)
    H["mvar"] = sb("mvar", [128, 32], F32)
    H["epsc"] = sb("epsc", [128, 1], F32)
    for nm in ("psA", "psB", "psC", "psD", "psE", "psF", "psG"):
        H[nm] = ps(nm, [128, 512], F32)
    H["ptr"] = ps("ptr", [128, 1024], BF16)
    H["ptr2"] = H["psG"][:].bitcast(BF16)
    return H


def emit_p2(ctx, H, IN, x2_out):
    E = ctx.emit
    idb = H["identb"]
    H = dict(H)
    H["x2t"] = H["gel"]
    state = {"xtb": 0, "xtb_rd": [None, None], "ptr_rd": None, "cv_prev": [None, None]}

    def ct_tile(load_tok, src_sb_ap_fn, evac_out_fn, war_deps=()):
        sl = state["xtb"]
        state["xtb"] = 1 - sl
        cv = E("scalar", lambda sc, s, sl=sl: sc.copy(
            out=H["xtb"][:, sl * 1024:(sl + 1) * 1024], in_=src_sb_ap_fn()),
            deps=tmax(load_tok, state["xtb_rd"][sl]), inc="act")
        tt = None
        for k in range(8):
            dep = tmax(cv, state["ptr_rd"], *war_deps) if k == 0 else ()
            tt = E("tensor", lambda te, s, sl=sl, k=k: te.transpose(
                out=H["ptr"][:, k * 128:(k + 1) * 128],
                in_=H["xtb"][:, sl * 1024 + k * 128:sl * 1024 + (k + 1) * 128],
                identity=idb[:]), deps=dep, inc=("pe" if k == 7 else None))
        state["xtb_rd"][sl] = tt
        ev = E("vector", lambda v, s: v.tensor_copy(
            out=evac_out_fn(), in_=H["ptr"][:].rearrange("p (k x) -> p k x", k=8)),
            deps=tmax(tt), inc="dve")
        state["ptr_rd"] = ev
        return cv, tt, ev

    t_c0 = E("vector", lambda v, s: v.memset(H["gel"][0:1, 0:128], 1.0), inc="dve")
    t_c1 = E("sync", lambda s_, s: s_.dma_start(out=H["identb"][:], in_=IN["ident"][:]), inc="ld")
    t_c1f = E("sync", lambda s_, s: s_.dma_start(out=H["identf"][:], in_=IN["identf"][:]), inc="ld")
    t_c2 = E("sync", lambda s_, s: s_.dma_start(out=H["wstg"][0:1, 2048:3072], in_=IN["s_fc2_b"][:]), inc="ld")
    t_rep = []
    for half in range(2):
        t = E("tensor", lambda te, s, half=half: te.matmul(
            out=H["psA"][:, 0:512], lhsT=H["gel"][0:1, 0:128],
            rhs=H["wstg"][0:1, 2048 + half * 512:2048 + (half + 1) * 512], start=True, stop=True),
            deps=tmax(t_c0, t_c2, t_rep[-1] if t_rep else None), inc="pe")
        t_rep.append(E("vector", lambda v, s, half=half: v.tensor_copy(
            out=H["repsb"][:, half * 512:(half + 1) * 512], in_=H["psA"][:, 0:512]),
            deps=tmax(t), inc="dve"))
    t_c3 = E("sync", lambda s_, s: s_.dma_start(out=H["wstg"][0:15, 2048:3072], in_=IN["gate_w"][:, :]),
             deps=tmax(t_rep[-1], ("pe", ctx.cnt["pe"])), inc="ld")
    t_c4 = E("sync", lambda s_, s: s_.dma_start(out=H["wstg"][0:15, 0:1024], in_=IN["fc2_b"][:, :]), inc="ld")
    t_cv4 = E("scalar", lambda sc, s: sc.copy(out=H["fc2brow"][0:15, :], in_=H["wstg"][0:15, 0:1024]),
              deps=tmax(t_c4), inc="act")
    t_bc = []
    for e in range(NEXP):
        for c in range(16):
            t_bc.append(E("sync", lambda s_, s, e=e, c=c: s_.dma_start(
                out=H["fc1b"][:, e * 16 + c:e * 16 + c + 1],
                in_=IN["fc1_b"][e, c * 128:(c + 1) * 128].rearrange("(p x) -> p x", x=1)), inc="ld"))
        for c in range(8):
            t_bc.append(E("sync", lambda s_, s, e=e, c=c: s_.dma_start(
                out=H["geglu"][:, e * 8 + c:e * 8 + c + 1],
                in_=IN["geglu_mult"][e, c * 128:(c + 1) * 128].rearrange("(p x) -> p x", x=1)), inc="ld"))
    for c in range(32):
        t_bc.append(E("sync", lambda s_, s, c=c: s_.dma_start(
            out=H["sfc1b"][:, c:c + 1],
            in_=IN["s_fc1_b"][c * 128:(c + 1) * 128].rearrange("(p x) -> p x", x=1)), inc="ld"))
    for c in range(16):
        t_bc.append(E("sync", lambda s_, s, c=c: s_.dma_start(
            out=H["sgeglu"][:, c:c + 1],
            in_=IN["s_geglu_mult"][c * 128:(c + 1) * 128].rearrange("(p x) -> p x", x=1)), inc="ld"))
    t_biases = t_bc[-1]
    t_eps = E("vector", lambda v, s: v.memset(H["epsc"][:], 1e-5), inc="dve")
    t_gwTl = []
    for k in range(8):
        t = E("tensor", lambda te, s, k=k: te.transpose(
            out=H["psG"][:, 0:15], in_=H["wstg"][0:15, 2048 + k * 128:2048 + (k + 1) * 128],
            identity=H["identf"][0:15, 0:15]),
            deps=tmax(t_c3, t_c1f, t_rep[-1], t_gwTl[-1] if t_gwTl else None), inc="pe")
        t_gwTl.append(E("vector", lambda v, s, k=k: v.tensor_copy(
            out=H["gwT"][:, k * 16:k * 16 + 15], in_=H["psG"][:, 0:15]),
            deps=tmax(t), inc="dve"))
    t_gwT = tmax(*t_gwTl)

    prev_store = [None, None]
    prev_yacc_rd = {i: None for i in range(8)}
    prev_mvar_act = [None, None]
    slot_free = {("fc1T", 0): None, ("fc1T", 1): None, ("fc2T", 0): None, ("fc2T", 1): None}
    slot_stg = {("fc1T", 0): None, ("fc1T", 1): None, ("fc2T", 0): None, ("fc2T", 1): None}
    prev_comb_rd = None
    prev_combT_rd = None
    prev_pG_rd = None
    prev_fc1_tail = None
    abf_rd = None
    state["cv_prev"][0] = tmax(t_cv4)

    def load_weight_slot(kind, e_or_none, ST, sl, nrows, src_fn, stage_ap_fn, slot_cols,
                         extra_war=None):
        base = sl * (SLOT1 if kind == "fc1T" else SLOT2)
        war0 = extra_war if extra_war is not None else tmax(
            slot_free[(kind, sl)], slot_stg[(kind, sl)])
        dst = H[kind]
        if ST == 0:
            evs = []
            ld_prev = [None, None]
            for fr in range(nrows):
                wsl = fr % 2
                tl = E("sync", lambda s_, s, fr=fr: s_.dma_start(
                    out=H["wstg"][:, (fr % 2) * 1024:(fr % 2) * 1024 + 1024],
                    in_=src_fn(fr)),
                    deps=tmax(state["cv_prev"][wsl]), inc="ld")
                cv, tr, ev = ct_tile(
                    tl,
                    (lambda fr=fr: H["wstg"][:, (fr % 2) * 1024:(fr % 2) * 1024 + 1024]),
                    (lambda base=base, fr=fr: dst[:, base:base + slot_cols].rearrange(
                        "p (k x) -> p k x", k=8)[:, :, fr * 128:(fr + 1) * 128]),
                    war_deps=(war0 if fr == 0 else ()))
                state["cv_prev"][wsl] = cv
                evs.append(ev)
            ready = tmax(*evs)
            stg = E("scalar", lambda sc, s, base=base: sc.dma_start(
                out=stage_ap_fn(), in_=dst[:, base:base + slot_cols]),
                deps=ready, inc="ld2")
            return ready, stg
        else:
            t = E("sync", lambda s_, s, base=base: s_.dma_start(
                out=dst[:, base:base + slot_cols], in_=stage_ap_fn()),
                deps=war0, inc="ld")
            return tmax(t), None

    for ST in range(NST):
        T0 = ST * 8
        t_fm = []
        t_yinit = []
        t_x1l = []
        for i in range(8):
            tl = E("sync", lambda s_, s, T0=T0, i=i: s_.dma_start(
                out=H["yacc"][:, i * 1024:(i + 1) * 1024],
                in_=IN["x1"][(T0 + i) * 128:(T0 + i + 1) * 128, :]),
                deps=tmax(prev_yacc_rd[i]), inc="ld")
            t_x1l.append(tl)
            cv, tr, ev = ct_tile(
                tl, (lambda i=i: H["yacc"][:, i * 1024:(i + 1) * 1024]),
                (lambda i=i: H["x1fm"][:].rearrange("p (k x) -> p k x", k=8)[:, :, i * 128:(i + 1) * 128]))
            t_fm.append(ev)
        t_fm_done = tmax(*t_fm)

        t_comb = []
        t_combT_ev = []
        prev_xf32_rd = None
        for tc in range(8):
            tts = []
            for k in range(8):
                dep = tmax(t_x1l[tc], prev_pG_rd, prev_xf32_rd,
                           t_gwT) if k == 0 else (tmax(tts[-1]) if k == 4 else ())
                t = E("tensor", lambda te, s, tc=tc, k=k: te.transpose(
                    out=H["psG"][:, (k % 4) * 128:(k % 4 + 1) * 128],
                    in_=H["yacc"][:, tc * 1024 + k * 128:tc * 1024 + (k + 1) * 128],
                    identity=H["identf"][:]),
                    deps=dep, inc=("pe" if (k % 4) == 3 else None))
                if (k % 4) == 3:
                    tts.append(E("vector", lambda v, s, k=k: v.tensor_copy(
                        out=H["wstg"][:, 2048 + (k - 3) * 128:2048 + (k + 1) * 128],
                        in_=H["psG"][:, 0:512]),
                        deps=tmax(t), inc="dve"))
            t_xf = tmax(*tts)
            t_yinit.append(E("vector", lambda v, s, tc=tc: v.tensor_tensor(
                out=H["yacc"][:, tc * 1024:(tc + 1) * 1024],
                in0=H["yacc"][:, tc * 1024:(tc + 1) * 1024],
                in1=H["repsb"][:, 0:1024], op=ALU.add),
                deps=tmax(t_xf, t_rep[-1]), inc="dve"))
            mm = None
            for k in range(8):
                dep = tmax(t_xf) if k == 0 else ()
                mm = E("tensor", lambda te, s, tc=tc, k=k: te.matmul(
                    out=H["psG"][:, 0:15],
                    lhsT=H["wstg"][:, 2048 + k * 128:2048 + (k + 1) * 128],
                    rhs=H["gwT"][:, k * 16:k * 16 + 15],
                    start=(k == 0), stop=(k == 7)),
                    deps=dep, inc=("pe" if k == 7 else None))
            prev_xf32_rd = mm
            r = H["rstat"]
            prv = t_comb[-1] if t_comb else None
            t1 = E("vector", lambda v, s: v.reduce_max(
                out=r[:, 0:1], in_=H["psG"][:, 0:15], axis=AX.X, negate=True),
                deps=tmax(mm, prv), inc="dve")
            t2 = E("scalar", lambda sc, s: sc.activation(
                out=H["pbuf"][:, 0:15], in_=H["psG"][:, 0:15], func=AF.Exp,
                bias=r[:, 0:1], scale=1.0), deps=tmax(t1), inc="act")
            prev_pG_rd = t2
            t3 = E("vector", lambda v, s: v.reduce_max(
                out=r[:, 1:2], in_=H["pbuf"][:, 0:15], axis=AX.X), deps=tmax(t2), inc="dve")
            t4 = E("vector", lambda v, s: v.tensor_scalar(
                out=H["pbuf"][:, 16:31], in0=H["pbuf"][:, 0:15], scalar1=r[:, 1:2],
                scalar2=None, op0=ALU.is_ge), deps=tmax(t3), inc="dve")
            t5 = E("vector", lambda v, s: v.scalar_tensor_tensor(
                out=H["pbuf"][:, 16:31], in0=H["pbuf"][:, 16:31], scalar=-2.0,
                in1=H["pbuf"][:, 0:15], op0=ALU.mult, op1=ALU.add), deps=tmax(t4), inc="dve")
            t6 = E("vector", lambda v, s: v.reduce_max(
                out=r[:, 2:3], in_=H["pbuf"][:, 16:31], axis=AX.X), deps=tmax(t5), inc="dve")
            t7 = E("vector", lambda v, s: v.tensor_scalar(
                out=H["pbuf"][:, 32:47], in0=H["pbuf"][:, 16:31], scalar1=r[:, 2:3],
                scalar2=None, op0=ALU.is_ge), deps=tmax(t6), inc="dve")
            t8 = E("vector", lambda v, s: v.scalar_tensor_tensor(
                out=H["pbuf"][:, 32:47], in0=H["pbuf"][:, 32:47], scalar=-2.0,
                in1=H["pbuf"][:, 16:31], op0=ALU.mult, op1=ALU.add), deps=tmax(t7), inc="dve")
            t9 = E("vector", lambda v, s: v.reduce_max(
                out=r[:, 3:4], in_=H["pbuf"][:, 32:47], axis=AX.X), deps=tmax(t8), inc="dve")
            t10 = E("vector", lambda v, s: v.tensor_scalar(
                out=H["pbuf"][:, 48:63], in0=H["pbuf"][:, 0:15], scalar1=r[:, 3:4],
                scalar2=None, op0=ALU.is_ge), deps=tmax(t9), inc="dve")
            t11 = E("vector", lambda v, s: v.tensor_tensor(
                out=H["pbuf"][:, 48:63], in0=H["pbuf"][:, 48:63], in1=H["pbuf"][:, 0:15],
                op=ALU.mult), deps=tmax(t10), inc="dve")
            t12 = E("vector", lambda v, s: v.reduce_sum(
                out=r[:, 4:5], in_=H["pbuf"][:, 48:63], axis=AX.X), deps=tmax(t11), inc="dve")
            t13 = E("vector", lambda v, s: v.reciprocal(
                out=r[:, 5:6], in_=r[:, 4:5]), deps=tmax(t12), inc="dve")
            t14 = E("vector", lambda v, s, tc=tc: v.tensor_scalar(
                out=H["comb"][:, tc * 16:tc * 16 + 15], in0=H["pbuf"][:, 48:63],
                scalar1=r[:, 5:6], scalar2=None, op0=ALU.mult),
                deps=tmax(t13, prev_comb_rd), inc="dve")
            t_comb.append(t14)
            c1 = E("scalar", lambda sc, s, tc=tc: sc.copy(
                out=H["cbt"][:, 0:15], in_=H["comb"][:, tc * 16:tc * 16 + 15]),
                deps=tmax(t14, t_combT_ev[-1] if t_combT_ev else None), inc="act")
            c2 = E("tensor", lambda te, s: te.transpose(
                out=H["ptr"][0:15, 0:128], in_=H["cbt"][:, 0:15], identity=idb[:]),
                deps=tmax(c1, state["ptr_rd"], prev_combT_rd), inc="pe")
            ev = E("vector", lambda v, s, tc=tc: v.tensor_copy(
                out=H["combT"][0:15, tc * 128:(tc + 1) * 128], in_=H["ptr"][0:15, 0:128]),
                deps=tmax(c2), inc="dve")
            state["ptr_rd"] = ev
            t_combT_ev.append(ev)
        t_comb_done = tmax(*t_comb)
        t_combT_done = tmax(*t_combT_ev)

        t_y_prev2 = [prev_yacc_rd[6], prev_yacc_rd[7]]
        t_y_last = None
        for e in range(NEXP):
            sl = e % 2
            w1_ready, w1_stg = load_weight_slot(
                "fc1T", e, ST, sl, 16,
                (lambda fr, e=e: IN["fc1_w"][e, fr * 128:(fr + 1) * 128, :]),
                (lambda e=e: IN["fc1T_st"][e]), SLOT1)
            w2_ready, w2_stg = load_weight_slot(
                "fc2T", e, ST, sl, 8,
                (lambda dr, e=e: IN["fc2_w"][e, dr * 128:(dr + 1) * 128, :]),
                (lambda e=e: IN["fc2T_st"][e]), SLOT2)
            b1, b2 = sl * SLOT1, sl * SLOT2
            t_a = []
            for fc in range(8):
                for th in range(2):
                    gi = fc * 2 + th
                    pg = ("psA", "psC")[gi % 2]
                    px = ("psB", "psD")[gi % 2]
                    mmg = None
                    for k in range(8):
                        dep = tmax(w1_ready, t_fm_done,
                                   t_a[gi - 2][2] if gi >= 2 else prev_fc1_tail) if k == 0 else ()
                        mmg = E("tensor", lambda te, s, fc=fc, th=th, k=k, pg=pg, b1=b1: te.matmul(
                            out=H[pg][:, 0:512],
                            lhsT=H["fc1T"][:, b1 + k * 2048 + 1024 + fc * 128:b1 + k * 2048 + 1024 + (fc + 1) * 128],
                            rhs=H["x1fm"][:, k * 1024 + th * 512:k * 1024 + (th + 1) * 512],
                            start=(k == 0), stop=(k == 7)),
                            deps=dep, inc=("pe" if k == 7 else None))
                    tg = E("scalar", lambda sc, s, th=th, pg=pg, e=e, fc=fc: sc.activation(
                        out=H["gel"][:, th * 512:(th + 1) * 512], in_=H[pg][:, 0:512],
                        func=AF.Gelu, bias=H["fc1b"][:, e * 16 + 8 + fc:e * 16 + 9 + fc], scale=1.0),
                        deps=tmax(mmg, t_a[gi - 2][1] if gi >= 2 else None), inc="act")
                    mmx = None
                    for k in range(8):
                        dep = tmax(t_a[gi - 2][1] if gi >= 2 else prev_fc1_tail) if k == 0 else ()
                        mmx = E("tensor", lambda te, s, fc=fc, th=th, k=k, px=px, b1=b1: te.matmul(
                            out=H[px][:, 0:512],
                            lhsT=H["fc1T"][:, b1 + k * 2048 + fc * 128:b1 + k * 2048 + (fc + 1) * 128],
                            rhs=H["x1fm"][:, k * 1024 + th * 512:k * 1024 + (th + 1) * 512],
                            start=(k == 0), stop=(k == 7)),
                            deps=dep, inc=("pe" if k == 7 else None))
                    ta1 = E("vector", lambda v, s, th=th, px=px, e=e, fc=fc: v.scalar_tensor_tensor(
                        out=H["gel"][:, th * 512:(th + 1) * 512], in0=H[px][:, 0:512],
                        scalar=H["fc1b"][:, e * 16 + fc:e * 16 + 1 + fc],
                        in1=H["gel"][:, th * 512:(th + 1) * 512],
                        op0=ALU.add, op1=ALU.mult),
                        deps=tmax(mmx, tg, t_biases), inc="dve")
                    ta2 = E("vector", lambda v, s, fc=fc, th=th, e=e: v.tensor_scalar(
                        out=H["abf"][:, fc * 1024 + th * 512:fc * 1024 + (th + 1) * 512],
                        in0=H["gel"][:, th * 512:(th + 1) * 512],
                        scalar1=H["geglu"][:, e * 8 + fc:e * 8 + 1 + fc], scalar2=None,
                        op0=ALU.mult), deps=tmax(ta1, abf_rd), inc="dve")
                    t_a.append((ta2, ta1, tg))
            t_a_done = tmax(*[t[0] for t in t_a])
            for tc in range(8):
                for dh in range(2):
                    gi = tc * 2 + dh
                    pp = ("psE", "psF")[gi % 2]
                    mm = None
                    for fc in range(8):
                        dep = tmax(w2_ready, t_a_done, t_y_prev2[gi % 2]) if fc == 0 else ()
                        mm = E("tensor", lambda te, s, tc=tc, dh=dh, fc=fc, pp=pp, b2=b2: te.matmul(
                            out=H[pp][:, 0:512],
                            lhsT=H["abf"][:, fc * 1024 + tc * 128:fc * 1024 + (tc + 1) * 128],
                            rhs=H["fc2T"][:, b2 + fc * 1024 + dh * 512:b2 + fc * 1024 + (dh + 1) * 512],
                            start=(fc == 0), stop=(fc == 7)),
                            deps=dep, inc=("pe" if fc == 7 else None))
                    ev = E("vector", lambda v, s, tc=tc, dh=dh, pp=pp, e=e: v.scalar_tensor_tensor(
                        out=H["yacc"][:, tc * 1024 + dh * 512:tc * 1024 + (dh + 1) * 512],
                        in0=H[pp][:, 0:512], scalar=H["comb"][:, tc * 16 + e:tc * 16 + e + 1],
                        in1=H["yacc"][:, tc * 1024 + dh * 512:tc * 1024 + (dh + 1) * 512],
                        op0=ALU.mult, op1=ALU.add),
                        deps=tmax(mm, t_comb_done, t_yinit[tc]), inc="dve")
                    t_y_prev2[gi % 2] = ev
                    t_y_last = ev
            prev_comb_rd = t_y_last
            slot_free[("fc1T", sl)] = t_a[-1][0]
            slot_free[("fc2T", sl)] = t_y_last
            slot_stg[("fc1T", sl)] = w1_stg
            slot_stg[("fc2T", sl)] = w2_stg
            abf_rd = t_y_last
            prev_fc1_tail = tmax(t_a[-2][1], t_a[-1][1], t_a[-2][2], t_a[-1][2])

        sl = NEXP % 2
        ws1_ready, ws1_stg = load_weight_slot(
            "fc1T", None, ST, 0, 32,
            (lambda fr: IN["s_fc1_w"][fr * 128:(fr + 1) * 128, :]),
            (lambda: IN["sfc1T_st"][:]), 2 * SLOT1,
            extra_war=tmax(slot_free[("fc1T", 0)], slot_free[("fc1T", 1)],
                           slot_stg[("fc1T", 0)], slot_stg[("fc1T", 1)]))
        b1 = 0
        t_a = []
        for fc in range(16):
            for th in range(2):
                gi = fc * 2 + th
                pg = ("psA", "psC")[gi % 2]
                px = ("psB", "psD")[gi % 2]
                mmg = None
                for k in range(8):
                    dep = tmax(ws1_ready, t_fm_done,
                               t_a[gi - 2][2] if gi >= 2 else prev_fc1_tail) if k == 0 else ()
                    mmg = E("tensor", lambda te, s, fc=fc, th=th, k=k, pg=pg, b1=b1: te.matmul(
                        out=H[pg][:, 0:512],
                        lhsT=H["fc1T"][:, b1 + k * 4096 + 2048 + fc * 128:b1 + k * 4096 + 2048 + (fc + 1) * 128],
                        rhs=H["x1fm"][:, k * 1024 + th * 512:k * 1024 + (th + 1) * 512],
                        start=(k == 0), stop=(k == 7)),
                        deps=dep, inc=("pe" if k == 7 else None))
                tg = E("scalar", lambda sc, s, th=th, pg=pg, fc=fc: sc.activation(
                    out=H["gel"][:, th * 512:(th + 1) * 512], in_=H[pg][:, 0:512],
                    func=AF.Gelu, bias=H["sfc1b"][:, 16 + fc:17 + fc], scale=1.0),
                    deps=tmax(mmg, t_a[gi - 2][1] if gi >= 2 else None), inc="act")
                mmx = None
                for k in range(8):
                    dep = tmax(t_a[gi - 2][1] if gi >= 2 else prev_fc1_tail) if k == 0 else ()
                    mmx = E("tensor", lambda te, s, fc=fc, th=th, k=k, px=px, b1=b1: te.matmul(
                        out=H[px][:, 0:512],
                        lhsT=H["fc1T"][:, b1 + k * 4096 + fc * 128:b1 + k * 4096 + (fc + 1) * 128],
                        rhs=H["x1fm"][:, k * 1024 + th * 512:k * 1024 + (th + 1) * 512],
                        start=(k == 0), stop=(k == 7)),
                        deps=dep, inc=("pe" if k == 7 else None))
                ta1 = E("vector", lambda v, s, th=th, px=px, fc=fc: v.scalar_tensor_tensor(
                    out=H["gel"][:, th * 512:(th + 1) * 512], in0=H[px][:, 0:512],
                    scalar=H["sfc1b"][:, fc:fc + 1],
                    in1=H["gel"][:, th * 512:(th + 1) * 512],
                    op0=ALU.add, op1=ALU.mult), deps=tmax(mmx, tg), inc="dve")
                ta2 = E("vector", lambda v, s, fc=fc, th=th: v.tensor_scalar(
                    out=H["abf"][:, fc * 1024 + th * 512:fc * 1024 + (th + 1) * 512],
                    in0=H["gel"][:, th * 512:(th + 1) * 512],
                    scalar1=H["sgeglu"][:, fc:fc + 1], scalar2=None,
                    op0=ALU.mult), deps=tmax(ta1, abf_rd), inc="dve")
                t_a.append((ta2, ta1, tg))
        t_a_done = tmax(*[t[0] for t in t_a])
        ws2_ready, ws2_stg = None, None
        if ST == 0:
            evs = []
            for dr in range(8):
                for half in range(2):
                    wsl = (dr * 2 + half) % 2
                    tl = E("sync", lambda s_, s, dr=dr, half=half, wsl=wsl: s_.dma_start(
                        out=H["wstg"][:, wsl * 1024:wsl * 1024 + 1024],
                        in_=IN["s_fc2_w"][dr * 128:(dr + 1) * 128, half * 1024:(half + 1) * 1024]),
                        deps=tmax(state["cv_prev"][wsl]), inc="ld")
                    war = (tmax(slot_free[("fc2T", 0)], slot_free[("fc2T", 1)],
                                slot_stg[("fc2T", 0)], slot_stg[("fc2T", 1)])
                           if (dr == 0 and half == 0) else ())
                    cv, tr, ev = ct_tile(
                        tl,
                        (lambda wsl=wsl: H["wstg"][:, wsl * 1024:wsl * 1024 + 1024]),
                        (lambda dr=dr, half=half: H["fc2T"][:].rearrange(
                            "p (k x) -> p k x", k=16)[:, half * 8:(half + 1) * 8, dr * 128:(dr + 1) * 128]),
                        war_deps=war)
                    state["cv_prev"][wsl] = cv
                    evs.append(ev)
            ws2_ready = tmax(*evs)
            ws2_stg = E("scalar", lambda sc, s: sc.dma_start(
                out=IN["sfc2T_st"][:], in_=H["fc2T"][:, 0:2 * SLOT2]),
                deps=ws2_ready, inc="ld2")
        else:
            t = E("sync", lambda s_, s: s_.dma_start(
                out=H["fc2T"][:, 0:2 * SLOT2], in_=IN["sfc2T_st"][:]),
                deps=tmax(slot_free[("fc2T", 0)], slot_free[("fc2T", 1)],
                          slot_stg[("fc2T", 0)], slot_stg[("fc2T", 1)]), inc="ld")
            ws2_ready = tmax(t)
        t_y_final = {}
        for tc in range(8):
            for dh in range(2):
                gi = tc * 2 + dh
                pp = ("psE", "psF")[gi % 2]
                mm = None
                for fc in range(16):
                    dep = tmax(ws2_ready, t_a_done, t_y_prev2[gi % 2], t_combT_done, t_cv4) if fc == 0 else ()
                    mm = E("tensor", lambda te, s, tc=tc, dh=dh, fc=fc, pp=pp: te.matmul(
                        out=H[pp][:, 0:512],
                        lhsT=H["abf"][:, fc * 1024 + tc * 128:fc * 1024 + (tc + 1) * 128],
                        rhs=H["fc2T"][:, fc * 1024 + dh * 512:fc * 1024 + (dh + 1) * 512],
                        start=(fc == 0), stop=False),
                        deps=dep, inc=None)
                mm = E("tensor", lambda te, s, tc=tc, dh=dh, pp=pp: te.matmul(
                    out=H[pp][:, 0:512],
                    lhsT=H["combT"][0:15, tc * 128:(tc + 1) * 128],
                    rhs=H["fc2brow"][0:15, dh * 512:(dh + 1) * 512],
                    start=False, stop=True), deps=(), inc="pe")
                ev = E("vector", lambda v, s, tc=tc, dh=dh, pp=pp: v.tensor_tensor(
                    out=H["yacc"][:, tc * 1024 + dh * 512:tc * 1024 + (dh + 1) * 512],
                    in0=H[pp][:, 0:512],
                    in1=H["yacc"][:, tc * 1024 + dh * 512:tc * 1024 + (dh + 1) * 512],
                    op=ALU.add), deps=tmax(mm), inc="dve")
                t_y_prev2[gi % 2] = ev
                t_y_final[tc] = ev
        prev_combT_rd = ("pe", ctx.cnt["pe"])
        slot_free[("fc1T", 0)] = t_a[-1][0]
        slot_free[("fc1T", 1)] = t_a[-1][0]
        slot_free[("fc2T", 0)] = t_y_final[7]
        slot_free[("fc2T", 1)] = t_y_final[7]
        slot_stg[("fc1T", 0)] = ws1_stg
        slot_stg[("fc1T", 1)] = ws1_stg
        slot_stg[("fc2T", 0)] = ws2_stg
        slot_stg[("fc2T", 1)] = ws2_stg
        abf_rd = ("pe", ctx.cnt["pe"])
        prev_fc1_tail = tmax(t_a[-2][1], t_a[-1][1], t_a[-2][2], t_a[-1][2])

        for tc in range(8):
            c0 = (tc % 2) * 16
            xsl = tc % 2
            t1 = E("vector", lambda v, s, tc=tc, c0=c0: v.bn_stats(
                out=H["mvar"][:, c0:c0 + 6], in_=H["yacc"][:, tc * 1024:tc * 1024 + 512]),
                deps=tmax(t_y_final[tc], prev_mvar_act[tc % 2]), inc="dve")
            t1b = E("vector", lambda v, s, tc=tc, c0=c0: v.bn_stats(
                out=H["mvar"][:, c0 + 6:c0 + 12], in_=H["yacc"][:, tc * 1024 + 512:tc * 1024 + 1024]),
                deps=tmax(t1), inc="dve")
            t2 = E("vector", lambda v, s, c0=c0: v.bn_aggr(
                out=H["mvar"][:, c0 + 12:c0 + 14], in_=H["mvar"][:, c0:c0 + 12]),
                deps=tmax(t1b), inc="dve")
            t3 = E("scalar", lambda sc, s, c0=c0: sc.activation(
                out=H["mvar"][:, c0 + 14:c0 + 15], in_=H["mvar"][:, c0 + 13:c0 + 14],
                func=AF.Sqrt, bias=H["epsc"][:, 0:1], scale=1.0),
                deps=tmax(t2, t_eps), inc="act")
            prev_mvar_act[tc % 2] = t3
            t4 = E("vector", lambda v, s, c0=c0: v.reciprocal(
                out=H["mvar"][:, c0 + 15:c0 + 16], in_=H["mvar"][:, c0 + 14:c0 + 15]),
                deps=tmax(t3), inc="dve")
            t5 = E("vector", lambda v, s, tc=tc, c0=c0: v.tensor_scalar(
                out=H["x2t"][:, 0:1024],
                in0=H["yacc"][:, tc * 1024:(tc + 1) * 1024],
                scalar1=H["mvar"][:, c0 + 12:c0 + 13], scalar2=H["mvar"][:, c0 + 15:c0 + 16],
                op0=ALU.subtract, op1=ALU.mult),
                deps=tmax(t4, prev_store[0]), inc="dve")
            st_t = E("gpsimd", lambda g, s, T0=T0, tc=tc: g.dma_start(
                out=x2_out[(T0 + tc) * 128:(T0 + tc + 1) * 128, :],
                in_=H["x2t"][:, 0:1024]),
                deps=tmax(t5), inc="st")
            prev_store[0] = st_t
            prev_yacc_rd[tc] = t5
        prev_fc1_tail = tmax(prev_fc1_tail, prev_store[0])

    E("gpsimd", lambda g, s: g.nop(), deps=[("st", ctx.cnt["st"])], inc=None)


def build_p2():
    nc = bass.Bass("TRN2", target_bir_lowering=False)
    IN = {}
    IN["x1"] = nc.dram_tensor("x1", [4096, D], F32, kind="ExternalInput").ap()
    IN["gate_w"] = nc.dram_tensor("gate_w", [NEXP, D], F32, kind="ExternalInput").ap()
    IN["fc1_w"] = nc.dram_tensor("fc1_w", [NEXP, 2048, D], F32, kind="ExternalInput").ap()
    IN["fc1_b"] = nc.dram_tensor("fc1_b", [NEXP, 2048], F32, kind="ExternalInput").ap()
    IN["geglu_mult"] = nc.dram_tensor("geglu_mult", [NEXP, D], F32, kind="ExternalInput").ap()
    IN["fc2_w"] = nc.dram_tensor("fc2_w", [NEXP, D, D], F32, kind="ExternalInput").ap()
    IN["fc2_b"] = nc.dram_tensor("fc2_b", [NEXP, D], F32, kind="ExternalInput").ap()
    IN["s_fc1_w"] = nc.dram_tensor("s_fc1_w", [4096, D], F32, kind="ExternalInput").ap()
    IN["s_fc1_b"] = nc.dram_tensor("s_fc1_b", [4096], F32, kind="ExternalInput").ap()
    IN["s_geglu_mult"] = nc.dram_tensor("s_geglu_mult", [2048], F32, kind="ExternalInput").ap()
    IN["s_fc2_w"] = nc.dram_tensor("s_fc2_w", [D, 2048], F32, kind="ExternalInput").ap()
    IN["s_fc2_b"] = nc.dram_tensor("s_fc2_b", [D], F32, kind="ExternalInput").ap()
    IN["onesf"] = nc.dram_tensor("onesf", [1, 128], F32, kind="ExternalInput").ap()
    IN["ident"] = nc.dram_tensor("ident", [128, 128], BF16, kind="ExternalInput").ap()
    IN["identf"] = nc.dram_tensor("identf", [128, 128], F32, kind="ExternalInput").ap()
    IN["fc1T_st"] = nc.dram_tensor("fc1T_st", [NEXP, 128, SLOT1], BF16, kind="Internal").ap()
    IN["fc2T_st"] = nc.dram_tensor("fc2T_st", [NEXP, 128, SLOT2], BF16, kind="Internal").ap()
    IN["sfc1T_st"] = nc.dram_tensor("sfc1T_st", [128, 2 * SLOT1], BF16, kind="Internal").ap()
    IN["sfc2T_st"] = nc.dram_tensor("sfc2T_st", [128, 2 * SLOT2], BF16, kind="Internal").ap()
    x2_out = nc.dram_tensor("x2o", [4096, D], F32, kind="ExternalOutput").ap()
    stack = ExitStack()
    with stack:
        H = alloc_p2(*real_allocators(nc, stack))
        ctx = Ctx()
        emit_p2(ctx, H, IN, x2_out)
        _finalize(ctx, nc, stack)
    return nc


def host_inputs_p2(x1_percore, inputs):
    ident = np.eye(128, dtype=np.float32).astype(BF)
    onesf = np.ones((1, 128), np.float32)
    maps = []
    for c in range(N_CORES):
        maps.append({
            "x1": np.ascontiguousarray(x1_percore[c].reshape(4096, D)),
            "gate_w": inputs["L0_ffn_gate_w"], "fc1_w": inputs["L0_ffn_fc1_w"],
            "fc1_b": inputs["L0_ffn_fc1_b"], "geglu_mult": inputs["L0_ffn_geglu_mult"],
            "fc2_w": inputs["L0_ffn_fc2_w"], "fc2_b": inputs["L0_ffn_fc2_b"],
            "s_fc1_w": inputs["L0_ffn_s_fc1_w"], "s_fc1_b": inputs["L0_ffn_s_fc1_b"],
            "s_geglu_mult": inputs["L0_ffn_s_geglu_mult"], "s_fc2_w": inputs["L0_ffn_s_fc2_w"],
            "s_fc2_b": inputs["L0_ffn_s_fc2_b"], "onesf": onesf, "ident": ident,
            "identf": np.eye(128, dtype=np.float32),
        })
    return maps




def alloc_p3(sb, ps):
    H = {}
    H["wqT"] = sb("wqT", [128, 8 * 1024], BF16)
    H["wkT"] = sb("wkT", [128, 8 * 1024], BF16)
    H["wvT"] = sb("wvT", [128, 8 * 1024], BF16)
    H["woutT"] = sb("woutT", [128, 8 * 1024], BF16)
    H["x2sb"] = sb("x2sb", [128, 4 * 1024], F32)
    H["xin"] = sb("xin", [128, 2 * 1024], F32)
    H["xtb"] = sb("xtb", [128, 2 * 1024], BF16)
    H["xinfm"] = sb("xinfm", [128, 8 * 512], BF16)
    H["qfm"] = sb("qfm", [128, 8 * 512], BF16)
    H["kfm"] = sb("kfm", [128, 2 * (8 * 512)], BF16)
    H["vsb"] = sb("vsb", [128, 2 * (4 * 1024)], BF16)
    H["smf"] = sb("smf", [128, 2048], F32)
    H["exf3"] = sb("exf3", [128, 2048], F32)
    H["aT"] = sb("aT", [128, 2048], BF16)
    H["aT2"] = sb("aT2", [128, 2048], BF16)
    H["attfm"] = sb("attfm", [128, 8 * 128], BF16)
    H["replnw"] = sb("replnw", [128, 1024], F32)
    H["replnb"] = sb("replnb", [128, 1024], F32)
    H["mask4"] = sb("mask4_s", [128, 512], F32)
    H["r32"] = sb("r32", [128, 2 * 1024], F32)
    H["x3t"] = sb("x3t", [128, 1024], F32)
    H["lrow"] = sb("lrow", [1, 2048], F32)
    H["onesf"] = sb("onesf_s", [1, 128], F32)
    H["identb"] = sb("identb", [128, 128], BF16)
    H["wstg"] = sb("wstg", [128, 2 * 1024], F32)
    H["mvar"] = sb("mvar", [128, 64], F32)
    H["smx"] = sb("smx", [128, 64], F32)
    H["epsc"] = sb("epsc", [128, 1], F32)
    for nm in ("psA", "psB", "psC", "psD", "psE", "psF", "psG"):
        H[nm] = ps(nm, [128, 512], F32)
    H["ptr"] = ps("ptr", [128, 1024], BF16)
    H["ptr2"] = H["psG"][:].bitcast(BF16)
    return H


def emit_p3(ctx, H, IN, x3_out, nsg=8, upto='full'):
    E = ctx.emit
    idb = H["identb"]
    state = {"xtb": 0, "xtb_rd": [None, None], "ptr_rd": None, "cv_prev": [None, None]}

    def ct_tile(load_tok, src_sb_ap_fn, evac_out_fn, war_deps=()):
        sl = state["xtb"]
        state["xtb"] = 1 - sl
        cv = E("scalar", lambda sc, s, sl=sl: sc.copy(
            out=H["xtb"][:, sl * 1024:(sl + 1) * 1024], in_=src_sb_ap_fn()),
            deps=tmax(load_tok, state["xtb_rd"][sl]), inc="act")
        tt = None
        for k in range(8):
            dep = tmax(cv, state["ptr_rd"], *war_deps) if k == 0 else ()
            tt = E("tensor", lambda te, s, sl=sl, k=k: te.transpose(
                out=H["ptr"][:, k * 128:(k + 1) * 128],
                in_=H["xtb"][:, sl * 1024 + k * 128:sl * 1024 + (k + 1) * 128],
                identity=idb[:]), deps=dep, inc=("pe" if k == 7 else None))
        state["xtb_rd"][sl] = tt
        ev = E("vector", lambda v, s: v.tensor_copy(
            out=evac_out_fn(), in_=H["ptr"][:].rearrange("p (k x) -> p k x", k=8)),
            deps=tmax(tt), inc="dve")
        state["ptr_rd"] = ev
        return cv, tt, ev

    t_c0 = E("sync", lambda s_, s: s_.dma_start(out=H["onesf"][0:1, :], in_=IN["onesf"][:]), inc="ld")
    t_c1 = E("sync", lambda s_, s: s_.dma_start(out=H["identb"][:], in_=IN["ident"][:]), inc="ld")
    t_c2 = E("sync", lambda s_, s: s_.dma_start(out=H["mask4"][:], in_=IN["mask4"][:]), inc="ld")
    t_c3 = E("sync", lambda s_, s: s_.dma_start(out=H["lrow"][0:1, 0:1024], in_=IN["ln_w"][:]), inc="ld")
    t_c4 = E("sync", lambda s_, s: s_.dma_start(out=H["lrow"][0:1, 1024:2048], in_=IN["ln_b"][:]), inc="ld")
    t_eps = E("vector", lambda v, s: v.memset(H["epsc"][:], 1e-5), inc="dve")
    t_rep = []
    for i, (dst, off) in enumerate((("replnw", 0), ("replnw", 512), ("replnb", 1024), ("replnb", 1536))):
        pq = ("psA", "psB")[i % 2]
        war = t_rep[i - 2] if i >= 2 else None
        t = E("tensor", lambda te, s, pq=pq, off=off: te.matmul(
            out=H[pq][:, 0:512], lhsT=H["onesf"][0:1, :],
            rhs=H["lrow"][0:1, off:off + 512], start=True, stop=True),
            deps=tmax(t_c0, t_c3, t_c4, war), inc="pe")
        t_rep.append(E("vector", lambda v, s, pq=pq, dst=dst, off=off: v.tensor_copy(
            out=H[dst][:, off % 1024:off % 1024 + 512], in_=H[pq][:, 0:512]),
            deps=tmax(t), inc="dve"))
    t_repd = tmax(*t_rep)
    w_ev = []
    DSTS = ("wqT", "wkT", "wvT")
    for r in range(24):
        tl = E("sync", lambda s_, s, r=r: s_.dma_start(
            out=H["wstg"][:, (r % 2) * 1024:(r % 2 + 1) * 1024],
            in_=IN["w_qkv"][r * 128:(r + 1) * 128, :]),
            deps=tmax(state["cv_prev"][r % 2]), inc="ld")
        sl = state["xtb"]
        state["xtb"] = 1 - sl
        cv = E("scalar", lambda sc, s, sl=sl, r=r: sc.copy(
            out=H["xtb"][:, sl * 1024:(sl + 1) * 1024],
            in_=H["wstg"][:, (r % 2) * 1024:(r % 2 + 1) * 1024]),
            deps=tmax(tl, state["xtb_rd"][sl]), inc="act")
        tt = None
        for k in range(8):
            dep = tmax(cv, state["ptr_rd"],
                       (tmax(t_c1, t_rep[-1]) if r == 0 else None)) if k == 0 else ()
            tt = E("tensor", lambda te, s, sl=sl, k=k: te.transpose(
                out=H["ptr"][:, k * 128:(k + 1) * 128],
                in_=H["xtb"][:, sl * 1024 + k * 128:sl * 1024 + (k + 1) * 128],
                identity=idb[:]), deps=dep, inc=("pe" if k == 7 else None))
        state["xtb_rd"][sl] = tt
        evl = None
        for half in range(2):
            j = 2 * r + half
            n, sel = j // 3, j % 3
            dst = DSTS[sel]
            evl = E("vector", lambda v, s, dst=dst, n=n, half=half: v.tensor_copy(
                out=H[dst][:].rearrange("p (k n c) -> p k n c", k=8, n=16)[:, :, n, :],
                in_=H["ptr"][:].rearrange("p (k x) -> p k x", k=8)[:, :, half * 64:(half + 1) * 64]),
                deps=tmax(tt) if half == 0 else (), inc="dve")
        state["ptr_rd"] = evl
        state["cv_prev"][r % 2] = cv
        w_ev.append(evl)
    for r in range(8):
        tl = E("sync", lambda s_, s, r=r: s_.dma_start(
            out=H["wstg"][:, (r % 2) * 1024:(r % 2 + 1) * 1024],
            in_=IN["w_out"][r * 128:(r + 1) * 128, :]),
            deps=tmax(state["cv_prev"][r % 2]), inc="ld")
        cv, tt, ev = ct_tile(
            tl, (lambda r=r: H["wstg"][:, (r % 2) * 1024:(r % 2 + 1) * 1024]),
            (lambda r=r: H["woutT"][:].rearrange("p (k x) -> p k x", k=8)[:, :, r * 128:(r + 1) * 128]))
        state["cv_prev"][r % 2] = cv
        w_ev.append(ev)
    t_w_done = tmax(*w_ev)

    prev_x2_rd = [None] * 4
    prev_qk_rd = None
    prev_sm_rd = {}
    prev_store = None
    prev_mvar_act = [None, None]
    prev_attfm_rd = None
    prev_att_ps_rd = [None, None]
    prev_psD_rd = [None] * 4
    prev_exf3_rd = [None] * 4
    prev_smf_rd = [None] * 4
    prev_aT_rd = [None] * 4
    prev_aT2_rd = [None] * 4
    prev_ptr3_rd = [None, None]
    prev_r32_rd = [None, None]
    prev_x3t_rd = None

    if upto == "w":
        E("gpsimd", lambda g, s: g.dma_start(out=x3_out[0:128, :], in_=H["wqT"][:, 0:1024].bitcast(F32) if False else H["replnw"][:, 0:1024]), deps=tmax(t_w_done, t_repd), inc="st")
        E("gpsimd", lambda g, s: g.nop(), deps=[("st", ctx.cnt["st"])], inc=None)
        return
    for sg in range(nsg):
        t_x2 = []
        t_fm = []
        for i in range(4):
            tl = E("sync", lambda s_, s, sg=sg, i=i: s_.dma_start(
                out=H["x2sb"][:, i * 1024:(i + 1) * 1024],
                in_=IN["x2"][(sg * 4 + i) * 128:(sg * 4 + i + 1) * 128, :]),
                deps=tmax(prev_x2_rd[i]), inc="ld")
            t_x2.append(tl)
            c0 = (i % 2) * 16
            t1 = E("vector", lambda v, s, i=i, c0=c0: v.bn_stats(
                out=H["mvar"][:, c0:c0 + 6], in_=H["x2sb"][:, i * 1024:i * 1024 + 512]),
                deps=tmax(tl, prev_mvar_act[i % 2]), inc="dve")
            t1b = E("vector", lambda v, s, i=i, c0=c0: v.bn_stats(
                out=H["mvar"][:, c0 + 6:c0 + 12], in_=H["x2sb"][:, i * 1024 + 512:i * 1024 + 1024]),
                deps=tmax(t1), inc="dve")
            t2 = E("vector", lambda v, s, c0=c0: v.bn_aggr(
                out=H["mvar"][:, c0 + 12:c0 + 14], in_=H["mvar"][:, c0:c0 + 12]),
                deps=tmax(t1b), inc="dve")
            t3 = E("scalar", lambda sc, s, c0=c0: sc.activation(
                out=H["mvar"][:, c0 + 14:c0 + 15], in_=H["mvar"][:, c0 + 13:c0 + 14],
                func=AF.Sqrt, bias=H["epsc"][:, 0:1], scale=1.0), deps=tmax(t2, t_eps), inc="act")
            prev_mvar_act[i % 2] = t3
            t4 = E("vector", lambda v, s, c0=c0: v.reciprocal(
                out=H["mvar"][:, c0 + 15:c0 + 16], in_=H["mvar"][:, c0 + 14:c0 + 15]),
                deps=tmax(t3), inc="dve")
            xsl = i % 2
            t5 = E("vector", lambda v, s, i=i, c0=c0, xsl=xsl: v.tensor_scalar(
                out=H["xin"][:, xsl * 1024:(xsl + 1) * 1024],
                in0=H["x2sb"][:, i * 1024:(i + 1) * 1024],
                scalar1=H["mvar"][:, c0 + 12:c0 + 13], scalar2=H["mvar"][:, c0 + 15:c0 + 16],
                op0=ALU.subtract, op1=ALU.mult), deps=tmax(t4), inc="dve")
            t6 = E("vector", lambda v, s, xsl=xsl: v.tensor_tensor(
                out=H["xin"][:, xsl * 1024:(xsl + 1) * 1024],
                in0=H["xin"][:, xsl * 1024:(xsl + 1) * 1024], in1=H["replnw"][:, 0:1024],
                op=ALU.mult), deps=tmax(t5, t_repd), inc="dve")
            t7 = E("vector", lambda v, s, xsl=xsl: v.tensor_tensor(
                out=H["xin"][:, xsl * 1024:(xsl + 1) * 1024],
                in0=H["xin"][:, xsl * 1024:(xsl + 1) * 1024], in1=H["replnb"][:, 0:1024],
                op=ALU.add), deps=tmax(t6), inc="dve")
            cv, tt, ev = ct_tile(
                t7, (lambda xsl=xsl: H["xin"][:, xsl * 1024:(xsl + 1) * 1024]),
                (lambda i=i: H["xinfm"][:].rearrange("p (k x) -> p k x", k=8)[:, :, i * 128:(i + 1) * 128]),
                war_deps=(tmax(t_w_done) if (sg == 0 and i == 0) else ()))
            t_fm.append(ev)
        t_fm_done = tmax(*t_fm)

        t_q = []
        t_k = []
        psq = ("psA", "psB")
        for np_ in range(8):
            for which, dst, tlist in (("q", "qfm", t_q), ("k", "kfm", t_k)):
                qsel = 0 if which == "q" else 1
                i = len(t_q) + len(t_k)
                pq = psq[i % 2]
                dep = tmax(t_fm_done, tlist[np_ - 1] if np_ >= 1 else prev_qk_rd,
                           t_q[np_ - 1] if (which == "k" and np_ >= 1) else None,
                           (t_q[-1] if (which == "k" and t_q) else None))
                wsrc = "wqT" if qsel == 0 else "wkT"
                mm = None
                for k in range(8):
                    mm = E("tensor", lambda te, s, np_=np_, k=k, pq=pq, wsrc=wsrc: te.matmul(
                        out=H[pq][:, 0:512],
                        lhsT=H[wsrc][:, k * 1024 + np_ * 128:k * 1024 + (np_ + 1) * 128],
                        rhs=H["xinfm"][:, k * 512:(k + 1) * 512],
                        start=(k == 0), stop=(k == 7)),
                        deps=(dep if k == 0 else ()), inc=("pe" if k == 7 else None))
                tlist.append(E("vector", lambda v, s, np_=np_, dst=dst, pq=pq, which=which: v.tensor_scalar(
                    out=H[dst][:, np_ * 512:(np_ + 1) * 512], in0=H[pq][:, 0:512],
                    scalar1=(0.03125 if which == "q" else 1.0), scalar2=None, op0=ALU.mult),
                    deps=tmax(mm), inc="dve"))
        t_v = []
        for tc in range(4):
            for vh in range(2):
                i = len(t_v)
                pq = psq[i % 2]
                dep = tmax(t_fm_done, t_v[i - 2] if i >= 2 else t_k[-1],
                           t_k[-2] if i < 2 else None)
                mm = None
                for k in range(8):
                    mm = E("tensor", lambda te, s, tc=tc, vh=vh, k=k, pq=pq: te.matmul(
                        out=H[pq][:, 0:512],
                        lhsT=H["xinfm"][:, k * 512 + tc * 128:k * 512 + (tc + 1) * 128],
                        rhs=H["wvT"][:, k * 1024 + vh * 512:k * 1024 + (vh + 1) * 512],
                        start=(k == 0), stop=(k == 7)),
                        deps=(dep if k == 0 else ()), inc=("pe" if k == 7 else None))
                t_v.append(E("vector", lambda v, s, tc=tc, vh=vh, pq=pq: v.tensor_copy(
                    out=H["vsb"][:, tc * 1024 + vh * 512:tc * 1024 + (vh + 1) * 512],
                    in_=H[pq][:, 0:512]), deps=tmax(mm), inc="dve"))
        t_qkv_done = tmax(t_q[-1], t_k[-1], *t_v)
        if upto == "qkv":
            E("gpsimd", lambda g, s: g.dma_start(out=x3_out[0:128, :], in_=H["x2sb"][:, 0:1024]), deps=t_qkv_done, inc="st")
            E("gpsimd", lambda g, s: g.nop(), deps=[("st", ctx.cnt["st"])], inc=None)
            return

        t_att_ev = []
        for g in range(4):
            SCB = ("psD", "psB", "psA", "psC")
            t_mm = [None] * 4
            t_msk = [None] * 4
            t_max = [None] * 4
            t_exp = [None] * 4
            t_rcp = [None] * 4
            t_ab = [None] * 4
            t_tr = [None] * 4
            t_ev = [None] * 4
            for pidx, js in enumerate(((0, 2), (1, 3))):
                if pidx == 1:
                    E("tensor", lambda te, s: te.transpose(
                        out=H["ptr"][:, 512:640], in_=idb[:], identity=idb[:]),
                        deps=(), inc=None)
                for hb in range(4):
                    psc = SCB[hb]
                    mm = None
                    for j in js:
                        n = hb * 4 + j
                        np_, ho = n // 2, (n % 2) * 64
                        dep = tmax(t_qkv_done, prev_psD_rd[hb]) if (pidx == 0 and j == js[0]) else ()
                        mm = E("tensor", lambda te, s, g=g, np_=np_, ho=ho, j=j, psc=psc: te.matmul(
                            out=H[psc][:, j * 128:(j + 1) * 128],
                            lhsT=H["qfm"][ho:ho + 64, np_ * 512 + g * 128:np_ * 512 + (g + 1) * 128],
                            rhs=H["kfm"][ho:ho + 64, np_ * 512 + g * 128:np_ * 512 + (g + 1) * 128],
                            start=True, stop=True),
                            deps=dep, inc=("pe" if (pidx == 1 and j == js[-1]) else None))
                    if pidx == 1:
                        t_mm[hb] = mm
            for hb in range(4):
                psc = SCB[hb]
                t_msk[hb] = E("vector", lambda v, s, psc=psc, hb=hb: v.tensor_tensor(
                    out=H["smf"][:, hb * 512:(hb + 1) * 512], in0=H[psc][:, 0:512],
                    in1=H["mask4"][:], op=ALU.add),
                    deps=tmax(t_mm[hb], t_c2, prev_smf_rd[hb]), inc="dve")
                prev_psD_rd[hb] = t_msk[hb]
            for hb in range(4):
                t_max[hb] = E("vector", lambda v, s, hb=hb: v.tensor_reduce(
                    out=H["smx"][:, hb * 4:hb * 4 + 4],
                    in_=H["smf"][:, hb * 512:(hb + 1) * 512].rearrange("p (j x) -> p j x", j=4),
                    axis=AX.X, op=ALU.max, negate=True), deps=tmax(t_msk[hb]), inc="dve")
            for hb in range(4):
                texp = None
                for j in range(4):
                    texp = E("scalar", lambda sc, s, hb=hb, j=j: sc.activation(
                        out=H["exf3"][:, hb * 512 + j * 128:hb * 512 + (j + 1) * 128],
                        in_=H["smf"][:, hb * 512 + j * 128:hb * 512 + (j + 1) * 128],
                        func=AF.Exp, bias=H["smx"][:, hb * 4 + j:hb * 4 + j + 1], scale=1.0,
                        accum_out=H["smx"][:, 16 + hb * 4 + j:17 + hb * 4 + j]),
                        deps=tmax(t_max[hb], prev_exf3_rd[hb]) if j == 0 else (),
                        inc=("act" if j == 3 else None))
                t_exp[hb] = texp
                prev_smf_rd[hb] = texp
            for hb in range(4):
                t_rcp[hb] = E("vector", lambda v, s, hb=hb: v.reciprocal(
                    out=H["smx"][:, 32 + hb * 4:36 + hb * 4],
                    in_=H["smx"][:, 16 + hb * 4:20 + hb * 4]), deps=tmax(t_exp[hb]), inc="dve")
            for hb in range(4):
                tab = None
                for j in range(4):
                    tab = E("scalar", lambda sc, s, hb=hb, j=j: sc.activation(
                        out=H["aT"][:, hb * 512 + j * 128:hb * 512 + (j + 1) * 128],
                        in_=H["exf3"][:, hb * 512 + j * 128:hb * 512 + (j + 1) * 128],
                        func=AF.Copy, scale=H["smx"][:, 32 + hb * 4 + j:33 + hb * 4 + j], bias=0.0),
                        deps=tmax(t_rcp[hb], prev_aT_rd[hb]) if j == 0 else (),
                        inc=("act" if j == 3 else None))
                t_ab[hb] = tab
            for hb in range(4):
                ptrh = ("ptr", "ptr2")[hb % 2]
                tt = None
                for j in range(4):
                    dep = tmax(t_ab[hb], prev_ptr3_rd[hb % 2]) if j == 0 else ()
                    tt = E("tensor", lambda te, s, hb=hb, j=j, ptrh=ptrh: te.transpose(
                        out=H[ptrh][:, j * 128:(j + 1) * 128],
                        in_=H["aT"][:, hb * 512 + j * 128:hb * 512 + (j + 1) * 128],
                        identity=idb[:]), deps=dep, inc=("pe" if j == 3 else None))
                t_tr[hb] = tt
                t_ev[hb] = E("vector", lambda v, s, hb=hb, ptrh=ptrh: v.tensor_copy(
                    out=H["aT2"][:, hb * 512:(hb + 1) * 512], in_=H[ptrh][:, 0:512]),
                    deps=tmax(tt, prev_aT2_rd[hb]), inc="dve")
                prev_ptr3_rd[hb % 2] = t_ev[hb]
            for hb in range(4):
                pb = ("psE", "psF")[hb // 2]
                mmav = None
                for j in range(4):
                    n = hb * 4 + j
                    ho = (n % 2) * 64
                    col = ((n // 2) % 4) * 128
                    dep = tmax(t_ev[hb], prev_att_ps_rd[hb // 2]) if j == 0 else ()
                    mmav = E("tensor", lambda te, s, g=g, n=n, j=j, ho=ho, col=col, pb=pb, hb=hb: te.matmul(
                        out=H[pb][ho:ho + 64, col:col + 128],
                        lhsT=H["vsb"][:, g * 1024 + n * 64:g * 1024 + (n + 1) * 64],
                        rhs=H["aT2"][:, hb * 512 + j * 128:hb * 512 + (j + 1) * 128],
                        start=True, stop=True),
                        deps=dep, inc=("pe" if j == 3 else None))
                prev_sm_rd[hb] = mmav
                prev_aT_rd[hb] = mmav
                prev_aT2_rd[hb] = mmav
                prev_exf3_rd[hb] = t_ab[hb]
            for half in range(2):
                pb = ("psE", "psF")[half]
                tev = E("vector", lambda v, s, half=half, pb=pb: v.tensor_copy(
                    out=H["attfm"][:].rearrange("p (n x) -> p n x", n=8)[:, half * 4:(half + 1) * 4, :],
                    in_=H[pb][:, 0:512].rearrange("p (n x) -> p n x", n=4)),
                    deps=tmax(prev_sm_rd[half * 2], prev_sm_rd[half * 2 + 1], prev_attfm_rd), inc="dve")
                prev_att_ps_rd[half] = tev
                t_att_ev.append(tev)
            t_att_done = tmax(t_att_ev[-1], t_att_ev[-2])
            if upto == "att2" and g == 0:
                E("gpsimd", lambda g_, s: g_.dma_start(out=x3_out[0:128, :], in_=H["x2sb"][:, 0:1024]),
                  deps=t_att_done, inc="st")
                E("gpsimd", lambda g_, s: g_.nop(), deps=[("st", ctx.cnt["st"])], inc=None)
                return
            t_r = []
            for dh in range(2):
                pq = ("psD", "psB")[dh]
                mm = None
                for k in range(8):
                    dep = tmax(t_att_done, prev_r32_rd[dh], prev_psD_rd[dh]) if k == 0 else ()
                    mm = E("tensor", lambda te, s, k=k, dh=dh, pq=pq: te.matmul(
                        out=H[pq][:, 0:512],
                        lhsT=H["attfm"][:, k * 128:(k + 1) * 128],
                        rhs=H["woutT"][:, k * 1024 + dh * 512:k * 1024 + (dh + 1) * 512],
                        start=(k == 0), stop=(k == 7)),
                        deps=(dep if k == 0 else ()), inc=("pe" if k == 7 else None))
                t_r.append(E("vector", lambda v, s, g=g, dh=dh, pq=pq: v.scalar_tensor_tensor(
                    out=H["r32"][:, (g % 2) * 1024 + dh * 512:(g % 2) * 1024 + (dh + 1) * 512],
                    in0=H["x2sb"][:, g * 1024 + dh * 512:g * 1024 + (dh + 1) * 512],
                    scalar=2.0, in1=H[pq][:, 0:512], op0=ALU.mult, op1=ALU.add),
                    deps=tmax(mm), inc="dve"))
            prev_attfm_rd = t_r[-1]
            prev_psD_rd[0] = tmax(prev_psD_rd[0], t_r[0])
            prev_psD_rd[1] = tmax(prev_psD_rd[1], t_r[1])
            c0 = 32 + (g % 2) * 16
            rb = (g % 2) * 1024
            t1 = E("vector", lambda v, s, rb=rb, c0=c0: v.bn_stats(
                out=H["mvar"][:, c0:c0 + 6], in_=H["r32"][:, rb:rb + 512]),
                deps=tmax(t_r[0], prev_mvar_act[0]), inc="dve")
            t1b = E("vector", lambda v, s, rb=rb, c0=c0: v.bn_stats(
                out=H["mvar"][:, c0 + 6:c0 + 12], in_=H["r32"][:, rb + 512:rb + 1024]),
                deps=tmax(t_r[1], t1), inc="dve")
            t2 = E("vector", lambda v, s, c0=c0: v.bn_aggr(
                out=H["mvar"][:, c0 + 12:c0 + 14], in_=H["mvar"][:, c0:c0 + 12]),
                deps=tmax(t1b), inc="dve")
            t3 = E("scalar", lambda sc, s, c0=c0: sc.activation(
                out=H["mvar"][:, c0 + 14:c0 + 15], in_=H["mvar"][:, c0 + 13:c0 + 14],
                func=AF.Sqrt, bias=H["epsc"][:, 0:1], scale=1.0), deps=tmax(t2), inc="act")
            t4 = E("vector", lambda v, s, c0=c0: v.reciprocal(
                out=H["mvar"][:, c0 + 15:c0 + 16], in_=H["mvar"][:, c0 + 14:c0 + 15]),
                deps=tmax(t3), inc="dve")
            t5 = E("vector", lambda v, s, rb=rb, c0=c0: v.tensor_scalar(
                out=H["x3t"][:, 0:1024], in0=H["r32"][:, rb:rb + 1024],
                scalar1=H["mvar"][:, c0 + 12:c0 + 13], scalar2=H["mvar"][:, c0 + 15:c0 + 16],
                op0=ALU.subtract, op1=ALU.mult),
                deps=tmax(t4, prev_store), inc="dve")
            prev_r32_rd[0] = t5
            prev_r32_rd[1] = t5
            st_t = E("gpsimd", lambda g_, s, sg=sg, g=g: g_.dma_start(
                out=x3_out[(sg * 4 + g) * 128:(sg * 4 + g + 1) * 128, :],
                in_=H["x3t"][:, 0:1024]), deps=tmax(t5), inc="st")
            prev_store = st_t
            prev_x2_rd[g] = t5 if False else tmax(t_r[-1], t5)
        prev_qk_rd = ("pe", ctx.cnt["pe"])

    E("gpsimd", lambda g, s: g.nop(), deps=[("st", ctx.cnt["st"])], inc=None)


def build_p3(nsg=8, upto='full'):
    nc = bass.Bass("TRN2", target_bir_lowering=False)
    IN = {}
    IN["x2"] = nc.dram_tensor("x2", [4096, D], F32, kind="ExternalInput").ap()
    IN["w_qkv"] = nc.dram_tensor("w_qkv", [3072, D], F32, kind="ExternalInput").ap()
    IN["w_out"] = nc.dram_tensor("w_out", [D, D], F32, kind="ExternalInput").ap()
    IN["ln_w"] = nc.dram_tensor("ln_w", [D], F32, kind="ExternalInput").ap()
    IN["ln_b"] = nc.dram_tensor("ln_b", [D], F32, kind="ExternalInput").ap()
    IN["mask4"] = nc.dram_tensor("mask4", [128, 512], F32, kind="ExternalInput").ap()
    IN["onesf"] = nc.dram_tensor("onesf", [1, 128], F32, kind="ExternalInput").ap()
    IN["ident"] = nc.dram_tensor("ident", [128, 128], BF16, kind="ExternalInput").ap()
    x3_out = nc.dram_tensor("x3o", [4096, D], F32, kind="ExternalOutput").ap()
    stack = ExitStack()
    with stack:
        H = alloc_p3(*real_allocators(nc, stack))
        ctx = Ctx()
        emit_p3(ctx, H, IN, x3_out, nsg=nsg, upto=upto)
        _finalize(ctx, nc, stack)
    return nc


def host_inputs_p3(x2_percore, inputs):
    ident = np.eye(128, dtype=np.float32).astype(BF)
    onesf = np.ones((1, 128), np.float32)
    m = np.full((128, 128), -1e9, np.float32)
    for p in range(8):
        m[p * 16:(p + 1) * 16, p * 16:(p + 1) * 16] = 0.0
    mask4 = np.ascontiguousarray(np.tile(m, (1, 4)))
    maps = []
    for c in range(N_CORES):
        maps.append({
            "x2": np.ascontiguousarray(x2_percore[c].reshape(4096, D)),
            "w_qkv": inputs["L1_attn_w_qkv"], "w_out": inputs["L1_attn_w_out"],
            "ln_w": inputs["L1_attn_ln_w"], "ln_b": inputs["L1_attn_ln_b"],
            "mask4": mask4, "onesf": onesf, "ident": ident,
        })
    return maps



NEXP4 = 8


def alloc_p4(sb, ps):
    H = {}
    H["yacc"] = sb("yacc", [128, 8 * 1024], F32)
    H["x3fm"] = sb("x3fm", [128, 8 * 1024], BF16)
    H["xtb"] = sb("xtb", [128, 2 * 1024], BF16)
    H["wgb"] = sb("wgb", [128, 2 * 4096], BF16)
    H["wub"] = sb("wub", [128, 2 * 4096], BF16)
    H["wdb"] = sb("wdb", [128, 2 * 4096], BF16)
    H["abf"] = sb("abf4", [128, 4 * 1024], BF16)
    H["gel"] = sb("gel", [128, 2 * 512], F32)
    H["wstg"] = sb("wstg", [128, 3 * 1024], F32)
    H["cw"] = sb("cw", [128, 8 * 8], F32)
    H["pbuf"] = sb("pbuf", [128, 64], F32)
    H["rstat"] = sb("rstat", [128, 16], F32)
    H["gwT"] = sb("gwT", [128, 8 * 8], F32)
    H["xf32"] = sb("xf32", [128, 1024], F32)
    H["identf"] = sb("identf_s", [128, 128], F32)
    H["repcorr"] = sb("repcorr", [128, 8], F32)
    H["repieps"] = sb("repieps", [128, 8], F32)
    H["onesf"] = sb("onesf_s", [1, 128], F32)
    H["identb"] = sb("identb", [128, 128], BF16)
    H["mvar"] = sb("mvar", [128, 32], F32)
    H["epsc"] = sb("epsc", [128, 1], F32)
    for nm in ("psA", "psB", "psC", "psD", "psE", "psF", "psG"):
        H[nm] = ps(nm, [128, 512], F32)
    H["ptr"] = ps("ptr", [128, 1024], BF16)
    H["ptr2"] = H["psG"][:].bitcast(BF16)
    return H


def emit_p4(ctx, H, IN, x4_out, dbg=None):
    E = ctx.emit
    idb = H["identb"]
    H = dict(H)
    H["x4t"] = H["gel"]
    state = {"xtb": 0, "xtb_rd": [None, None], "ptr_rd": None, "cv_prev": [None, None]}

    def ct_tile(load_tok, src_sb_ap_fn, evac_out_fn, war_deps=()):
        sl = state["xtb"]
        state["xtb"] = 1 - sl
        cv = E("scalar", lambda sc, s, sl=sl: sc.copy(
            out=H["xtb"][:, sl * 1024:(sl + 1) * 1024], in_=src_sb_ap_fn()),
            deps=tmax(load_tok, state["xtb_rd"][sl]), inc="act")
        tt = None
        for k in range(8):
            dep = tmax(cv, state["ptr_rd"], *war_deps) if k == 0 else ()
            tt = E("tensor", lambda te, s, sl=sl, k=k: te.transpose(
                out=H["ptr"][:, k * 128:(k + 1) * 128],
                in_=H["xtb"][:, sl * 1024 + k * 128:sl * 1024 + (k + 1) * 128],
                identity=idb[:]), deps=dep, inc=("pe" if k == 7 else None))
        state["xtb_rd"][sl] = tt
        ev = E("vector", lambda v, s: v.tensor_copy(
            out=evac_out_fn(), in_=H["ptr"][:].rearrange("p (k x) -> p k x", k=8)),
            deps=tmax(tt), inc="dve")
        state["ptr_rd"] = ev
        return cv, tt, ev

    t_c0 = E("sync", lambda s_, s: s_.dma_start(out=H["onesf"][0:1, :], in_=IN["onesf"][:]), inc="ld")
    t_c1 = E("sync", lambda s_, s: s_.dma_start(out=H["identb"][:], in_=IN["ident"][:]), inc="ld")
    t_c1f = E("sync", lambda s_, s: s_.dma_start(out=H["identf"][:], in_=IN["identf"][:]), inc="ld")
    t_c2 = E("sync", lambda s_, s: s_.dma_start(out=H["wstg"][0:1, 2048:2056], in_=IN["correction_bias"][:]), inc="ld")
    t_c3 = E("sync", lambda s_, s: s_.dma_start(out=H["wstg"][0:1, 2056:2064], in_=IN["ieps"][:]), inc="ld")
    t_eps = E("vector", lambda v, s: v.memset(H["epsc"][:], 1e-5), inc="dve")
    t_rep = []
    for i, (dst, off) in enumerate((("repcorr", 2048), ("repieps", 2056))):
        pq = ("psA", "psB")[i % 2]
        t = E("tensor", lambda te, s, pq=pq, off=off: te.matmul(
            out=H[pq][:, 0:8], lhsT=H["onesf"][0:1, :],
            rhs=H["wstg"][0:1, off:off + 8], start=True, stop=True),
            deps=tmax(t_c0, t_c2, t_c3), inc="pe")
        t_rep.append(E("vector", lambda v, s, pq=pq, dst=dst: v.tensor_copy(
            out=H[dst][:], in_=H[pq][:, 0:8]), deps=tmax(t), inc="dve"))
    t_repd = tmax(*t_rep)
    t_c4 = E("sync", lambda s_, s: s_.dma_start(out=H["wstg"][0:8, 0:1024], in_=IN["gate_w"][:, :]),
             deps=(), inc="ld")
    t_gwTl = []
    for k in range(8):
        t = E("tensor", lambda te, s, k=k: te.transpose(
            out=H["psG"][:, 0:8], in_=H["wstg"][0:8, k * 128:(k + 1) * 128],
            identity=H["identf"][0:8, 0:8]),
            deps=tmax(t_c4, t_c1f, t_repd, t_gwTl[-1] if t_gwTl else None), inc="pe")
        t_gwTl.append(E("vector", lambda v, s, k=k: v.tensor_copy(
            out=H["gwT"][:, k * 8:(k + 1) * 8], in_=H["psG"][:, 0:8]),
            deps=tmax(t), inc="dve"))
    t_gwT = tmax(*t_gwTl)
    state["cv_prev"] = [tmax(t_gwT), tmax(t_gwT)]

    prev_store = [None]
    prev_yacc_rd = {i: None for i in range(8)}
    prev_mvar_act = [None, None]
    slot_free = {}
    slot_stg = {}
    for kind in ("wgb", "wub", "wdb"):
        for sl in range(2):
            slot_free[(kind, sl)] = None
            slot_stg[(kind, sl)] = None
    prev_cw_rd = None
    prev_pG_rd = None
    prev_fc1_tail = None
    abf_rd = None

    def load_slot(kind, e, ST, sl, nrows, ncols, src_ap, stage_ap):
        base = sl * 4096
        dst = H[kind]
        if ST == 0:
            evs = []
            for fr in range(nrows):
                wsl = fr % 2
                tl = E("sync", lambda s_, s, fr=fr: s_.dma_start(
                    out=H["wstg"][:, wsl * 1024:wsl * 1024 + ncols] if False else H["wstg"][:, (fr % 2) * 1024:(fr % 2) * 1024 + ncols],
                    in_=src_ap[fr * 128:(fr + 1) * 128, :]),
                    deps=tmax(state["cv_prev"][wsl]), inc="ld")
                cv = E("scalar", lambda sc, s, fr=fr, base=base: sc.copy(
                    out=dst[:, base + fr * ncols:base + (fr + 1) * ncols],
                    in_=H["wstg"][:, (fr % 2) * 1024:(fr % 2) * 1024 + ncols]),
                    deps=tmax(tl, slot_free[(kind, sl)] if fr == 0 else None,
                              slot_stg[(kind, sl)] if fr == 0 else None), inc="act")
                state["cv_prev"][wsl] = cv
                evs.append(cv)
            ready = tmax(*evs)
            stg = E("scalar", lambda sc, s, base=base: sc.dma_start(
                out=stage_ap, in_=dst[:, base:base + nrows * ncols]),
                deps=ready, inc="ld2")
            return ready, stg
        else:
            t = E("sync", lambda s_, s, base=base: s_.dma_start(
                out=dst[:, base:base + nrows * ncols], in_=stage_ap),
                deps=tmax(slot_free[(kind, sl)], slot_stg[(kind, sl)]), inc="ld")
            return tmax(t), None

    for ST in range(NST):
        T0 = ST * 8
        t_fm = []
        t_x3 = []
        for i in range(8):
            tl = E("sync", lambda s_, s, T0=T0, i=i: s_.dma_start(
                out=H["yacc"][:, i * 1024:(i + 1) * 1024],
                in_=IN["x3"][(T0 + i) * 128:(T0 + i + 1) * 128, :]),
                deps=tmax(prev_yacc_rd[i]), inc="ld")
            t_x3.append(tl)
            cv, tr, ev = ct_tile(
                tl, (lambda i=i: H["yacc"][:, i * 1024:(i + 1) * 1024]),
                (lambda i=i: H["x3fm"][:].rearrange("p (k x) -> p k x", k=8)[:, :, i * 128:(i + 1) * 128]))
            t_fm.append(ev)
        t_fm_done = tmax(*t_fm)

        t_cw = []
        prev_xf32_rd = None
        for tc in range(8):
            tts = []
            for k in range(8):
                dep = tmax(t_x3[tc], prev_pG_rd, prev_xf32_rd,
                           t_gwT) if k == 0 else (tmax(tts[-1]) if k == 4 else ())
                t = E("tensor", lambda te, s, tc=tc, k=k: te.transpose(
                    out=H["psG"][:, (k % 4) * 128:(k % 4 + 1) * 128],
                    in_=H["yacc"][:, tc * 1024 + k * 128:tc * 1024 + (k + 1) * 128],
                    identity=H["identf"][:]),
                    deps=dep, inc=("pe" if (k % 4) == 3 else None))
                if (k % 4) == 3:
                    tts.append(E("vector", lambda v, s, k=k: v.tensor_copy(
                        out=H["xf32"][:, (k - 3) * 128:(k + 1) * 128], in_=H["psG"][:, 0:512]),
                        deps=tmax(t), inc="dve"))
            t_xf = tmax(*tts)
            mm = None
            for k in range(8):
                dep = tmax(t_xf) if k == 0 else ()
                mm = E("tensor", lambda te, s, tc=tc, k=k: te.matmul(
                    out=H["psG"][:, 0:8],
                    lhsT=H["xf32"][:, k * 128:(k + 1) * 128],
                    rhs=H["gwT"][:, k * 8:(k + 1) * 8],
                    start=(k == 0), stop=(k == 7)),
                    deps=dep, inc=("pe" if k == 7 else None))
            prev_xf32_rd = mm
            r = H["rstat"]
            prv = t_cw[-1] if t_cw else None
            t1 = E("scalar", lambda sc, s: sc.activation(
                out=H["pbuf"][:, 0:8], in_=H["psG"][:, 0:8], func=AF.Sigmoid,
                bias=0.0, scale=1.0), deps=tmax(mm, prv), inc="act")
            prev_pG_rd = t1
            t2 = E("vector", lambda v, s: v.tensor_tensor(
                out=H["pbuf"][:, 8:16], in0=H["pbuf"][:, 0:8], in1=H["repcorr"][:],
                op=ALU.add), deps=tmax(t1, t_repd, prv), inc="dve")
            t3 = E("vector", lambda v, s: v.tensor_tensor(
                out=r[:, 0:4], in0=H["pbuf"][:, 8:16:2], in1=H["pbuf"][:, 9:16:2],
                op=ALU.add), deps=tmax(t2), inc="dve")
            t4 = E("vector", lambda v, s: v.reduce_max(
                out=r[:, 4:5], in_=r[:, 0:4], axis=AX.X), deps=tmax(t3), inc="dve")
            t5 = E("vector", lambda v, s: v.tensor_scalar(
                out=H["pbuf"][:, 16:20], in0=r[:, 0:4], scalar1=r[:, 4:5],
                scalar2=None, op0=ALU.is_ge), deps=tmax(t4), inc="dve")
            t6 = E("vector", lambda v, s: v.scalar_tensor_tensor(
                out=H["pbuf"][:, 20:24], in0=H["pbuf"][:, 16:20], scalar=-1e6,
                in1=r[:, 0:4], op0=ALU.mult, op1=ALU.add), deps=tmax(t5), inc="dve")
            t7 = E("vector", lambda v, s: v.reduce_max(
                out=r[:, 5:6], in_=H["pbuf"][:, 20:24], axis=AX.X), deps=tmax(t6), inc="dve")
            t8 = E("vector", lambda v, s: v.tensor_scalar(
                out=H["pbuf"][:, 16:20], in0=r[:, 0:4], scalar1=r[:, 5:6],
                scalar2=None, op0=ALU.is_ge), deps=tmax(t7), inc="dve")
            t9a = E("vector", lambda v, s: v.tensor_copy(
                out=H["pbuf"][:, 24:32:2], in_=H["pbuf"][:, 16:20]), deps=tmax(t8), inc="dve")
            t9b = E("vector", lambda v, s: v.tensor_copy(
                out=H["pbuf"][:, 25:32:2], in_=H["pbuf"][:, 16:20]), deps=tmax(t8), inc="dve")
            t10 = E("vector", lambda v, s: v.tensor_tensor(
                out=H["pbuf"][:, 32:40], in0=H["pbuf"][:, 8:16], in1=H["pbuf"][:, 24:32],
                op=ALU.mult), deps=tmax(t9a, t9b), inc="dve")
            t11 = E("vector", lambda v, s: v.tensor_tensor(
                out=H["pbuf"][:, 32:40], in0=H["pbuf"][:, 32:40], in1=H["repieps"][:],
                op=ALU.add), deps=tmax(t10), inc="dve")
            cur = 32
            tprev = t11
            for it in range(3):
                ta = E("vector", lambda v, s, cur=cur, it=it: v.reduce_max(
                    out=r[:, 6 + it:7 + it], in_=H["pbuf"][:, cur:cur + 8], axis=AX.X),
                    deps=tmax(tprev), inc="dve")
                tb = E("vector", lambda v, s, cur=cur, it=it: v.tensor_scalar(
                    out=H["pbuf"][:, 40:48], in0=H["pbuf"][:, cur:cur + 8],
                    scalar1=r[:, 6 + it:7 + it], scalar2=None, op0=ALU.is_ge),
                    deps=tmax(ta), inc="dve")
                tc_ = E("vector", lambda v, s, cur=cur: v.scalar_tensor_tensor(
                    out=H["pbuf"][:, 48:56], in0=H["pbuf"][:, 40:48], scalar=-1e6,
                    in1=H["pbuf"][:, cur:cur + 8], op0=ALU.mult, op1=ALU.add),
                    deps=tmax(tb), inc="dve")
                td = E("vector", lambda v, s: v.tensor_copy(
                    out=H["pbuf"][:, 32:40] if False else H["pbuf"][:, 56:64],
                    in_=H["pbuf"][:, 48:56]), deps=tmax(tc_), inc="dve")
                cur = 56
                tprev = td
            tm4 = E("vector", lambda v, s, cur=cur: v.reduce_max(
                out=r[:, 9:10], in_=H["pbuf"][:, cur:cur + 8], axis=AX.X),
                deps=tmax(tprev), inc="dve")
            tsel = E("vector", lambda v, s: v.tensor_scalar(
                out=H["pbuf"][:, 40:48], in0=H["pbuf"][:, 32:40], scalar1=r[:, 9:10],
                scalar2=None, op0=ALU.is_ge), deps=tmax(tm4), inc="dve")
            ttw = E("vector", lambda v, s: v.tensor_tensor(
                out=H["pbuf"][:, 48:56], in0=H["pbuf"][:, 0:8], in1=H["pbuf"][:, 40:48],
                op=ALU.mult), deps=tmax(tsel), inc="dve")
            tsum = E("vector", lambda v, s: v.reduce_sum(
                out=r[:, 10:11], in_=H["pbuf"][:, 48:56], axis=AX.X), deps=tmax(ttw), inc="dve")
            tad = E("vector", lambda v, s: v.tensor_scalar(
                out=r[:, 11:12], in0=r[:, 10:11], scalar1=1e-20, scalar2=None,
                op0=ALU.add), deps=tmax(tsum), inc="dve")
            trc = E("vector", lambda v, s: v.reciprocal(
                out=r[:, 12:13], in_=r[:, 11:12]), deps=tmax(tad), inc="dve")
            tcw = E("vector", lambda v, s, tc=tc: v.tensor_scalar(
                out=H["cw"][:, tc * 8:(tc + 1) * 8], in0=H["pbuf"][:, 48:56],
                scalar1=r[:, 12:13], scalar2=2.5, op0=ALU.mult, op1=ALU.mult),
                deps=tmax(trc, prev_cw_rd), inc="dve")
            t_cw.append(tcw)
        t_cw_done = tmax(*t_cw)
        if dbg is not None and ST == 0:
            E("gpsimd", lambda g, s: g.dma_start(out=dbg["cw"][:], in_=H["cw"][:]),
              deps=t_cw_done, inc="st")

        t_y_prev2 = [prev_yacc_rd[6], prev_yacc_rd[7]]
        t_y_last = None
        for e in range(NEXP4 + 1):
            shared = (e == NEXP4)
            sl = e % 2
            if shared:
                wg_r, wg_s = load_slot("wgb", e, ST, sl, 8, 512, IN["Wg_s"], IN["wgs_st"][:])
                wu_r, wu_s = load_slot("wub", e, ST, sl, 8, 512, IN["Wu_s"], IN["wus_st"][:])
                wd_r, wd_s = load_slot("wdb", e, ST, sl, 4, 1024, IN["Wd_s"], IN["wds_st"][:])
            else:
                wg_r, wg_s = load_slot("wgb", e, ST, sl, 8, 512, IN["Wg"][e], IN["wg_st"][e])
                wu_r, wu_s = load_slot("wub", e, ST, sl, 8, 512, IN["Wu"][e], IN["wu_st"][e])
                wd_r, wd_s = load_slot("wdb", e, ST, sl, 4, 1024, IN["Wd"][e], IN["wd_st"][e])
            b = sl * 4096
            t_a = []
            for ic in range(4):
                for th in range(2):
                    gi = ic * 2 + th
                    pg = ("psA", "psC")[gi % 2]
                    px = ("psB", "psD")[gi % 2]
                    mmg = None
                    for k in range(8):
                        dep = tmax(wg_r, t_fm_done,
                                   t_a[gi - 2][2] if gi >= 2 else prev_fc1_tail) if k == 0 else ()
                        mmg = E("tensor", lambda te, s, ic=ic, th=th, k=k, pg=pg, b=b: te.matmul(
                            out=H[pg][:, 0:512],
                            lhsT=H["wgb"][:, b + k * 512 + ic * 128:b + k * 512 + (ic + 1) * 128],
                            rhs=H["x3fm"][:, k * 1024 + th * 512:k * 1024 + (th + 1) * 512],
                            start=(k == 0), stop=(k == 7)),
                            deps=dep, inc=("pe" if k == 7 else None))
                    tg = E("scalar", lambda sc, s, th=th, pg=pg: sc.activation(
                        out=H["gel"][:, th * 512:(th + 1) * 512], in_=H[pg][:, 0:512],
                        func=AF.Silu, bias=0.0, scale=1.0),
                        deps=tmax(mmg, t_a[gi - 2][1] if gi >= 2 else None), inc="act")
                    mmx = None
                    for k in range(8):
                        dep = tmax(wu_r, t_a[gi - 2][1] if gi >= 2 else prev_fc1_tail) if k == 0 else ()
                        mmx = E("tensor", lambda te, s, ic=ic, th=th, k=k, px=px, b=b: te.matmul(
                            out=H[px][:, 0:512],
                            lhsT=H["wub"][:, b + k * 512 + ic * 128:b + k * 512 + (ic + 1) * 128],
                            rhs=H["x3fm"][:, k * 1024 + th * 512:k * 1024 + (th + 1) * 512],
                            start=(k == 0), stop=(k == 7)),
                            deps=dep, inc=("pe" if k == 7 else None))
                    ta1 = E("vector", lambda v, s, ic=ic, th=th, px=px: v.tensor_tensor(
                        out=H["abf"][:, ic * 1024 + th * 512:ic * 1024 + (th + 1) * 512],
                        in0=H[px][:, 0:512], in1=H["gel"][:, th * 512:(th + 1) * 512],
                        op=ALU.mult), deps=tmax(mmx, tg, abf_rd), inc="dve")
                    t_a.append((ta1, ta1, tg))
            t_a_done = tmax(*[t[0] for t in t_a])
            for tc in range(8):
                for dh in range(2):
                    gi = tc * 2 + dh
                    pp = ("psE", "psF")[gi % 2]
                    mm = None
                    for ic in range(4):
                        dep = tmax(wd_r, t_a_done, t_y_prev2[gi % 2]) if ic == 0 else ()
                        mm = E("tensor", lambda te, s, tc=tc, dh=dh, ic=ic, pp=pp, b=b: te.matmul(
                            out=H[pp][:, 0:512],
                            lhsT=H["abf"][:, ic * 1024 + tc * 128:ic * 1024 + (tc + 1) * 128],
                            rhs=H["wdb"][:, b + ic * 1024 + dh * 512:b + ic * 1024 + (dh + 1) * 512],
                            start=(ic == 0), stop=(ic == 3)),
                            deps=dep, inc=("pe" if ic == 3 else None))
                    if shared:
                        ev = E("vector", lambda v, s, tc=tc, dh=dh, pp=pp: v.tensor_tensor(
                            out=H["yacc"][:, tc * 1024 + dh * 512:tc * 1024 + (dh + 1) * 512],
                            in0=H[pp][:, 0:512],
                            in1=H["yacc"][:, tc * 1024 + dh * 512:tc * 1024 + (dh + 1) * 512],
                            op=ALU.add), deps=tmax(mm, t_x3[tc]), inc="dve")
                    else:
                        ev = E("vector", lambda v, s, tc=tc, dh=dh, pp=pp, e=e: v.scalar_tensor_tensor(
                            out=H["yacc"][:, tc * 1024 + dh * 512:tc * 1024 + (dh + 1) * 512],
                            in0=H[pp][:, 0:512], scalar=H["cw"][:, tc * 8 + e:tc * 8 + e + 1],
                            in1=H["yacc"][:, tc * 1024 + dh * 512:tc * 1024 + (dh + 1) * 512],
                            op0=ALU.mult, op1=ALU.add),
                            deps=tmax(mm, t_cw_done, t_x3[tc]), inc="dve")
                    t_y_prev2[gi % 2] = ev
                    t_y_last = ev
            prev_cw_rd = t_y_last
            for kind, stg in (("wgb", wg_s), ("wub", wu_s), ("wdb", wd_s)):
                slot_stg[(kind, sl)] = stg
            slot_free[("wgb", sl)] = t_a[-1][0]
            slot_free[("wub", sl)] = t_a[-1][0]
            slot_free[("wdb", sl)] = t_y_last
            abf_rd = t_y_last
            prev_fc1_tail = tmax(t_a[-2][0], t_a[-1][0], t_a[-2][2], t_a[-1][2])

        for tc in range(8):
            c0 = (tc % 2) * 16
            t1 = E("vector", lambda v, s, tc=tc, c0=c0: v.bn_stats(
                out=H["mvar"][:, c0:c0 + 6], in_=H["yacc"][:, tc * 1024:tc * 1024 + 512]),
                deps=tmax(t_y_last, prev_mvar_act[tc % 2]), inc="dve")
            t1b = E("vector", lambda v, s, tc=tc, c0=c0: v.bn_stats(
                out=H["mvar"][:, c0 + 6:c0 + 12], in_=H["yacc"][:, tc * 1024 + 512:tc * 1024 + 1024]),
                deps=tmax(t1), inc="dve")
            t2 = E("vector", lambda v, s, c0=c0: v.bn_aggr(
                out=H["mvar"][:, c0 + 12:c0 + 14], in_=H["mvar"][:, c0:c0 + 12]),
                deps=tmax(t1b), inc="dve")
            t3 = E("scalar", lambda sc, s, c0=c0: sc.activation(
                out=H["mvar"][:, c0 + 14:c0 + 15], in_=H["mvar"][:, c0 + 13:c0 + 14],
                func=AF.Sqrt, bias=H["epsc"][:, 0:1], scale=1.0),
                deps=tmax(t2, t_eps), inc="act")
            prev_mvar_act[tc % 2] = t3
            t4 = E("vector", lambda v, s, c0=c0: v.reciprocal(
                out=H["mvar"][:, c0 + 15:c0 + 16], in_=H["mvar"][:, c0 + 14:c0 + 15]),
                deps=tmax(t3), inc="dve")
            t5 = E("vector", lambda v, s, tc=tc, c0=c0: v.tensor_scalar(
                out=H["x4t"][:, 0:1024], in0=H["yacc"][:, tc * 1024:(tc + 1) * 1024],
                scalar1=H["mvar"][:, c0 + 12:c0 + 13], scalar2=H["mvar"][:, c0 + 15:c0 + 16],
                op0=ALU.subtract, op1=ALU.mult),
                deps=tmax(t4, prev_store[0]), inc="dve")
            st_t = E("gpsimd", lambda g, s, T0=T0, tc=tc: g.dma_start(
                out=x4_out[(T0 + tc) * 128:(T0 + tc + 1) * 128, :],
                in_=H["x4t"][:, 0:1024]), deps=tmax(t5), inc="st")
            prev_store[0] = st_t
            prev_yacc_rd[tc] = t5
        prev_fc1_tail = tmax(prev_fc1_tail, prev_store[0])

    E("gpsimd", lambda g, s: g.nop(), deps=[("st", ctx.cnt["st"])], inc=None)


def build_p4(debug=False):
    nc = bass.Bass("TRN2", target_bir_lowering=False)
    IN = {}
    IN["x3"] = nc.dram_tensor("x3", [4096, D], F32, kind="ExternalInput").ap()
    IN["gate_w"] = nc.dram_tensor("gate_w", [NEXP4, D], F32, kind="ExternalInput").ap()
    IN["correction_bias"] = nc.dram_tensor("correction_bias", [NEXP4], F32, kind="ExternalInput").ap()
    IN["Wg"] = nc.dram_tensor("Wg", [NEXP4, D, 512], F32, kind="ExternalInput").ap()
    IN["Wu"] = nc.dram_tensor("Wu", [NEXP4, D, 512], F32, kind="ExternalInput").ap()
    IN["Wd"] = nc.dram_tensor("Wd", [NEXP4, 512, D], F32, kind="ExternalInput").ap()
    IN["Wg_s"] = nc.dram_tensor("Wg_s", [D, 512], F32, kind="ExternalInput").ap()
    IN["Wu_s"] = nc.dram_tensor("Wu_s", [D, 512], F32, kind="ExternalInput").ap()
    IN["Wd_s"] = nc.dram_tensor("Wd_s", [512, D], F32, kind="ExternalInput").ap()
    IN["ieps"] = nc.dram_tensor("ieps", [NEXP4], F32, kind="ExternalInput").ap()
    IN["onesf"] = nc.dram_tensor("onesf", [1, 128], F32, kind="ExternalInput").ap()
    IN["ident"] = nc.dram_tensor("ident", [128, 128], BF16, kind="ExternalInput").ap()
    IN["identf"] = nc.dram_tensor("identf", [128, 128], F32, kind="ExternalInput").ap()
    IN["wg_st"] = nc.dram_tensor("wg_st", [NEXP4, 128, 4096], BF16, kind="Internal").ap()
    IN["wu_st"] = nc.dram_tensor("wu_st", [NEXP4, 128, 4096], BF16, kind="Internal").ap()
    IN["wd_st"] = nc.dram_tensor("wd_st", [NEXP4, 128, 4096], BF16, kind="Internal").ap()
    IN["wgs_st"] = nc.dram_tensor("wgs_st", [128, 4096], BF16, kind="Internal").ap()
    IN["wus_st"] = nc.dram_tensor("wus_st", [128, 4096], BF16, kind="Internal").ap()
    IN["wds_st"] = nc.dram_tensor("wds_st", [128, 4096], BF16, kind="Internal").ap()
    x4_out = nc.dram_tensor("x4o", [4096, D], F32, kind="ExternalOutput").ap()
    dbg = None
    if debug:
        dbg = {"cw": nc.dram_tensor("dbg_cw", [128, 64], F32, kind="ExternalOutput").ap()}
    stack = ExitStack()
    with stack:
        H = alloc_p4(*real_allocators(nc, stack))
        ctx = Ctx()
        emit_p4(ctx, H, IN, x4_out, dbg=dbg)
        _finalize(ctx, nc, stack)
    return nc


def host_inputs_p4(x3_percore, inputs):
    ident = np.eye(128, dtype=np.float32).astype(BF)
    onesf = np.ones((1, 128), np.float32)
    ieps = (-np.arange(8) * 1e-30).astype(np.float32)
    maps = []
    for c in range(N_CORES):
        maps.append({
            "x3": np.ascontiguousarray(x3_percore[c].reshape(4096, D)),
            "gate_w": inputs["L1_ffn_gate_w"], "correction_bias": inputs["L1_ffn_correction_bias"],
            "Wg": inputs["L1_ffn_Wg"], "Wu": inputs["L1_ffn_Wu"], "Wd": inputs["L1_ffn_Wd"],
            "Wg_s": inputs["L1_ffn_Wg_s"], "Wu_s": inputs["L1_ffn_Wu_s"], "Wd_s": inputs["L1_ffn_Wd_s"],
            "ieps": ieps, "onesf": onesf, "ident": ident,
            "identf": np.eye(128, dtype=np.float32),
        })
    return maps




def real_allocators(nc, stack):
    def sb(name, shape, dt):
        return stack.enter_context(nc.sbuf_tensor(name, shape, dt))
    def ps(name, shape, dt):
        return stack.enter_context(nc.psum_tensor(name, shape, dt))
    return sb, ps


class PhaseArena:

    def __init__(self, nc):
        self.nc = nc
        self.cur = 0
        self.peak = 0
        self.tensors = []

    def reset(self):
        self.cur = 0

    def sbuf(self, name, shape, dt):
        import functools
        per_part = 1
        for d in shape[1:]:
            per_part *= d
        per_part *= mybir.dt.size(dt)
        addr = (self.cur + 31) // 32 * 32
        self.cur = addr + per_part
        self.peak = max(self.peak, self.cur)
        h = self.nc.virtual_sbuf_tensor(name, shape, dt)
        self.tensors.append((h, addr))
        return h

    def materialize(self, stack):
        backing = stack.enter_context(
            self.nc.sbuf_tensor("arena", [128, self.peak], mybir.dt.uint8))
        base = self.nc.lookup_mloc(backing).addr
        for h, addr in self.tensors:
            self.nc._tensor(h.name, h.shape, h.dtype, type="SB")
            mloc = self.nc.lookup_mloc(h)
            mloc.allocated = True
            mloc.addr = base + addr


def phase_barrier(ctx):
    toks = [(s, ctx.cnt[s]) for s in SEM_ENG]
    marks = {}
    for eng, sem in (("sync", "ld"), ("scalar", "act"), ("vector", "dve"),
                     ("tensor", "pe"), ("gpsimd", "pool")):
        pass
    for eng in ENGINES:
        ctx.emit(eng, lambda api, s: api.nop(), deps=list(toks), inc=None)


def build_fused():
    nc = bass.Bass("TRN2", target_bir_lowering=False)
    dram = lambda name, shape, kind="ExternalInput", dt=F32: nc.dram_tensor(
        name, shape, dt, kind=kind).ap()
    IN1 = {}
    IN1["xo"] = dram("xo", [B, SEG, D])
    IN1["xofm"] = dram("xofm", [B, D, SEG])
    IN1["xdfm"] = dram("xdfm", [B, D, NSLOT * 128])
    for nm in ("wq", "wk", "wv", "wo"):
        IN1[nm] = dram(nm, [D, D])
    for nm in ("bq", "bk", "bv", "bo"):
        IN1[nm] = dram(nm, [D])
    IN1["wmask"] = dram("wmask", [1, 4])
    IN1["onesf"] = dram("onesf", [1, 128])
    IN1["ident"] = dram("ident", [128, 128], dt=BF16)
    IN2 = {}
    _x1t = nc.dram_tensor("x1m", [OWN, B, D], F32, kind="Internal")
    IN2["x1"] = _x1t.ap().rearrange("a b d -> (a b) d")
    IN2["gate_w"] = dram("gate_w2", [NEXP, D])
    IN2["fc1_w"] = dram("fc1_w", [NEXP, 2048, D])
    IN2["fc1_b"] = dram("fc1_b", [NEXP, 2048])
    IN2["geglu_mult"] = dram("geglu_mult", [NEXP, D])
    IN2["fc2_w"] = dram("fc2_w", [NEXP, D, D])
    IN2["fc2_b"] = dram("fc2_b", [NEXP, D])
    IN2["s_fc1_w"] = dram("s_fc1_w", [4096, D])
    IN2["s_fc1_b"] = dram("s_fc1_b", [4096])
    IN2["s_geglu_mult"] = dram("s_geglu_mult", [2048])
    IN2["s_fc2_w"] = dram("s_fc2_w", [D, 2048])
    IN2["s_fc2_b"] = dram("s_fc2_b", [D])
    IN2["ident"] = IN1["ident"]
    IN2["identf"] = dram("identf", [128, 128])
    IN2["fc1T_st"] = dram("fc1T_st", [NEXP, 128, SLOT1], kind="Internal", dt=BF16)
    IN2["fc2T_st"] = dram("fc2T_st", [NEXP, 128, SLOT2], kind="Internal", dt=BF16)
    IN2["sfc1T_st"] = dram("sfc1T_st", [128, 2 * SLOT1], kind="Internal", dt=BF16)
    IN2["sfc2T_st"] = dram("sfc2T_st", [128, 2 * SLOT2], kind="Internal", dt=BF16)
    IN3 = {}
    IN3["x2"] = dram("x2m", [4096, D], kind="Internal")
    IN3["w_qkv"] = dram("w_qkv", [3072, D])
    IN3["w_out"] = dram("w_out", [D, D])
    IN3["ln_w"] = dram("ln_w", [D])
    IN3["ln_b"] = dram("ln_b", [D])
    IN3["mask4"] = dram("mask4", [128, 512])
    IN3["onesf"] = IN1["onesf"]
    IN3["ident"] = IN1["ident"]
    IN4 = {}
    IN4["x3"] = dram("x3m", [4096, D], kind="Internal")
    IN4["gate_w"] = dram("gate_w4", [NEXP4, D])
    IN4["correction_bias"] = dram("correction_bias", [NEXP4])
    IN4["Wg"] = dram("Wg", [NEXP4, D, 512])
    IN4["Wu"] = dram("Wu", [NEXP4, D, 512])
    IN4["Wd"] = dram("Wd", [NEXP4, 512, D])
    IN4["Wg_s"] = dram("Wg_s", [D, 512])
    IN4["Wu_s"] = dram("Wu_s", [D, 512])
    IN4["Wd_s"] = dram("Wd_s", [512, D])
    IN4["ieps"] = dram("ieps", [NEXP4])
    IN4["onesf"] = IN1["onesf"]
    IN4["ident"] = IN1["ident"]
    IN4["identf"] = IN2["identf"]
    IN4["wg_st"] = dram("wg_st", [NEXP4, 128, 4096], kind="Internal", dt=BF16)
    IN4["wu_st"] = dram("wu_st", [NEXP4, 128, 4096], kind="Internal", dt=BF16)
    IN4["wd_st"] = dram("wd_st", [NEXP4, 128, 4096], kind="Internal", dt=BF16)
    IN4["wgs_st"] = dram("wgs_st", [128, 4096], kind="Internal", dt=BF16)
    IN4["wus_st"] = dram("wus_st", [128, 4096], kind="Internal", dt=BF16)
    IN4["wds_st"] = dram("wds_st", [128, 4096], kind="Internal", dt=BF16)
    x1_m = _x1t.ap()
    x2_m = IN3["x2"]
    x3_m = IN4["x3"]
    x4_out = dram("x4o", [4096, D], kind="ExternalOutput")

    stack = ExitStack()
    with stack:
        arena = PhaseArena(nc)
        pshared = {}

        def ps(name, shape, dt):
            if name not in pshared:
                pshared[name] = stack.enter_context(nc.psum_tensor(name, shape, dt))
            return pshared[name]

        Hs = []
        for allocf in (alloc_p1, alloc_p2, alloc_p3, alloc_p4):
            arena.reset()
            Hs.append(allocf(arena.sbuf, ps))
        arena.materialize(stack)
        ctx = Ctx()
        emit_p1(ctx, Hs[0], IN1, x1_m)
        phase_barrier(ctx)
        emit_p2(ctx, Hs[1], IN2, x2_m)
        phase_barrier(ctx)
        emit_p3(ctx, Hs[2], IN3, x3_m)
        phase_barrier(ctx)
        emit_p4(ctx, Hs[3], IN4, x4_out)
        _finalize(ctx, nc, stack)
    return nc


def host_inputs_fused(inputs):
    x = inputs["L0_attn_x"]
    dummy = np.zeros((4096, D), np.float32)
    m1 = host_inputs_p1(x, inputs)
    m2 = host_inputs_p2([dummy] * N_CORES, inputs)
    m3 = host_inputs_p3([dummy] * N_CORES, inputs)
    m4 = host_inputs_p4([dummy] * N_CORES, inputs)
    maps = []
    for c in range(N_CORES):
        d = dict(m1[c])
        for k, v in m2[c].items():
            if k == "x1":
                continue
            d["gate_w2" if k == "gate_w" else k] = v
        for k, v in m3[c].items():
            if k == "x2":
                continue
            d[k] = v
        for k, v in m4[c].items():
            if k == "x3":
                continue
            d["gate_w4" if k == "gate_w" else k] = v
        maps.append(d)
    return maps


def _finalize(ctx, nc, stack):
    sems = {}
    for s in SEM_ENG:
        sems[s] = stack.enter_context(nc.semaphore(s))
    with nc.Block() as block:
        @block.sync
        def _(api):
            ctx.replay("sync", api, sems)

        @block.scalar
        def _(api):
            ctx.replay("scalar", api, sems)

        @block.vector
        def _(api):
            ctx.replay("vector", api, sems)

        @block.tensor
        def _(api):
            ctx.replay("tensor", api, sems)

        @block.gpsimd
        def _(api):
            ctx.replay("gpsimd", api, sems)


def build_p1(debug=False):
    nc = bass.Bass("TRN2", target_bir_lowering=False)
    IN = {}
    IN["xo"] = nc.dram_tensor("xo", [B, SEG, D], F32, kind="ExternalInput").ap()
    IN["xofm"] = nc.dram_tensor("xofm", [B, D, SEG], F32, kind="ExternalInput").ap()
    IN["xdfm"] = nc.dram_tensor("xdfm", [B, D, NSLOT * 128], F32, kind="ExternalInput").ap()
    for nm, shape in (("wq", [D, D]), ("wk", [D, D]), ("wv", [D, D]), ("wo", [D, D])):
        IN[nm] = nc.dram_tensor(nm, shape, F32, kind="ExternalInput").ap()
    for nm in ("bq", "bk", "bv", "bo"):
        IN[nm] = nc.dram_tensor(nm, [D], F32, kind="ExternalInput").ap()
    IN["wmask"] = nc.dram_tensor("wmask", [1, 4], F32, kind="ExternalInput").ap()
    IN["onesf"] = nc.dram_tensor("onesf", [1, 128], F32, kind="ExternalInput").ap()
    IN["ident"] = nc.dram_tensor("ident", [128, 128], BF16, kind="ExternalInput").ap()
    x1_out = nc.dram_tensor("x1o", [OWN, B, D], F32, kind="ExternalOutput").ap()
    dbg = None
    if debug:
        dbg = {}
        for nm, w in (("qfm", 8 * 256), ("kfm", 8 * 512), ("kdfm", 8 * 512),
                      ("vsb", 4 * 1024), ("vdsb", 4 * 1024), ("att", 8 * 256),
                      ("xfm", 8 * 512), ("xnfm", 8 * 512), ("r32", 2048)):
            dt = F32 if nm == "r32" else BF16
            dbg[nm] = nc.dram_tensor("dbg_" + nm, [128, w], dt, kind="ExternalOutput").ap()
    stack = ExitStack()
    with stack:
        H = alloc_p1(*real_allocators(nc, stack))
        ctx = Ctx()
        emit_p1(ctx, H, IN, x1_out, dbg=dbg)
        _finalize(ctx, nc, stack)
    return nc


def host_inputs_p1(x, inputs):
    ident = np.eye(128, dtype=np.float32).astype(BF)
    onesf = np.ones((1, 128), np.float32)
    maps = []
    for c in range(N_CORES):
        s0, h0 = c // 2, c % 2
        xo = np.empty((B, SEG, D), np.float32)
        xo[:, 0:OWN] = x[:, s0 * SEG + h0 * OWN: s0 * SEG + (h0 + 1) * OWN]
        xo[:, OWN:SEG] = x[:, s0 * SEG + (1 - h0) * OWN: s0 * SEG + (2 - h0) * OWN]
        xd = np.zeros((B, NSLOT * 128, D), np.float32)
        wm = np.zeros((1, 4), np.float32)
        for j, off in enumerate((-2, -1, 1, 2)):
            n = s0 + off
            if 0 <= n < 4:
                xd[:, j * 128:(j + 1) * 128] = x[:, n * SEG: (n + 1) * SEG: 4]
                wm[0, j] = 0.25
        maps.append({
            "xo": xo,
            "xofm": np.ascontiguousarray(xo.transpose(0, 2, 1)),
            "xdfm": np.ascontiguousarray(xd.transpose(0, 2, 1)),
            "wq": inputs["L0_attn_Wq"], "bq": inputs["L0_attn_bq"],
            "wk": inputs["L0_attn_Wk"], "bk": inputs["L0_attn_bk"],
            "wv": inputs["L0_attn_Wv"], "bv": inputs["L0_attn_bv"],
            "wo": inputs["L0_attn_Wo"], "bo": inputs["L0_attn_bo"],
            "wmask": wm, "onesf": onesf, "ident": ident,
        })
    return maps


FUSED = True


def kernel(**inputs):
    inputs = {k: np.ascontiguousarray(np.asarray(v)) for k, v in inputs.items()}
    if FUSED:
        nc = build_fused()
        maps = host_inputs_fused(inputs)
        res = run_bass_kernel_spmd(nc, maps, core_ids=list(range(N_CORES)))
        out = np.empty((B, L, D), np.float32)
        for c in range(N_CORES):
            out[:, c * OWN:(c + 1) * OWN, :] = res.results[c]["x4o"].reshape(
                OWN, B, D).transpose(1, 0, 2)
        return out
    x = inputs["L0_attn_x"]
    nc1 = build_p1()
    maps1 = host_inputs_p1(x, inputs)
    res1 = run_bass_kernel_spmd(nc1, maps1, core_ids=list(range(N_CORES)))
    x1_pc = [res1.results[c]["x1o"] for c in range(N_CORES)]
    nc2 = build_p2()
    maps2 = host_inputs_p2(x1_pc, inputs)
    res2 = run_bass_kernel_spmd(nc2, maps2, core_ids=list(range(N_CORES)))
    x2_pc = [res2.results[c]["x2o"] for c in range(N_CORES)]
    nc3 = build_p3()
    maps3 = host_inputs_p3(x2_pc, inputs)
    res3 = run_bass_kernel_spmd(nc3, maps3, core_ids=list(range(N_CORES)))
    x3_pc = [res3.results[c]["x3o"] for c in range(N_CORES)]
    nc4 = build_p4()
    maps4 = host_inputs_p4(x3_pc, inputs)
    res4 = run_bass_kernel_spmd(nc4, maps4, core_ids=list(range(N_CORES)))
    out = np.empty((B, L, D), np.float32)
    for c in range(N_CORES):
        out[:, c * OWN:(c + 1) * OWN, :] = res4.results[c]["x4o"].reshape(
            OWN, B, D).transpose(1, 0, 2)
    return out


# version 14
# speedup vs baseline: 1.0598x; 1.0041x over previous
import numpy as np
import ml_dtypes
import concourse.bass as bass
import concourse.mybir as mybir
from concourse.bass_utils import run_bass_kernel_spmd
from concourse.alu_op_type import AluOpType as ALU
from contextlib import ExitStack

F32 = mybir.dt.float32
BF16 = mybir.dt.bfloat16
AF = mybir.ActivationFunctionType
AX = mybir.AxisListType

BF = ml_dtypes.bfloat16
N_CORES = 8
B = 16
L = 2048
D = 1024
SEG = 512
OWN = 256
NSLOT = 4

ENGINES = ("sync", "scalar", "vector", "tensor", "gpsimd")
SEM_ENG = {"ld": "sync", "ld2": "scalar", "gld": "gpsimd", "gw": "gpsimd", "st": "gpsimd",
           "pe": "tensor", "act": "scalar", "dve": "vector", "pool": "gpsimd"}
DMA_SEMS = {"ld", "ld2", "gld", "gw", "st"}


class Ctx:

    def __init__(self):
        self.q = {e: [] for e in ENGINES}
        self.cnt = {s: 0 for s in SEM_ENG}
        self.waited = {e: {s: 0 for s in SEM_ENG} for e in ENGINES}

    def emit(self, eng, fn, deps=(), inc=None):
        step = 16 if inc in DMA_SEMS else 1
        tok = None
        if inc is not None:
            self.cnt[inc] += step
            tok = (inc, self.cnt[inc])
        need = []
        w = self.waited[eng]
        best = {}
        for (s, c) in deps:
            if c is None or c <= 0:
                continue
            if eng == "tensor" and SEM_ENG[s] == "tensor":
                continue
            if c <= w[s]:
                continue
            if c > best.get(s, 0):
                best[s] = c
        for s, c in best.items():
            w[s] = c
            need.append((s, c))
        self.q[eng].append((tuple(need), fn, inc, step))
        return tok

    def replay(self, eng, api, sems):
        for need, fn, inc, step in self.q[eng]:
            for (s, c) in need:
                api.wait_ge(sems[s], c)
            ins = fn(api, sems)
            if inc is not None:
                ins.then_inc(sems[inc], step)


def tmax(*toks):
    out = {}
    def add(t):
        if t is None:
            return
        if isinstance(t, list):
            for u in t:
                add(u)
            return
        s, c = t
        if c is not None and c > out.get(s, 0):
            out[s] = c
    for t in toks:
        add(t)
    return [(s, c) for s, c in out.items()]



def alloc_p1(sb, ps):
    H = {}
    H["wqb"] = sb("wqb", [128, 8 * 1024], BF16)
    H["wkb"] = sb("wkb", [128, 8 * 1024], BF16)
    H["wvb"] = sb("wvb", [128, 8 * 1024], BF16)
    H["wob"] = sb("wob", [128, 8 * 1024], BF16)
    H["bqc"] = sb("bqc", [128, 8], F32)
    H["bkc"] = sb("bkc", [128, 8], F32)
    H["repbv"] = sb("repbv", [128, 1024], F32)
    H["bobf"] = sb("bobf", [1, 1024], BF16)
    H["onebf"] = sb("onebf", [1, 128], BF16)
    H["onesf"] = sb("onesf_s", [1, 128], F32)
    H["wmaskr"] = sb("wmaskr", [1, 4], F32)
    H["repwm"] = sb("repwm", [128, 4], F32)
    H["identb"] = sb("identb", [128, 128], BF16)
    H["xfm"] = sb("xfm", [128, 2 * (8 * 512)], BF16)
    H["xnfm"] = sb("xnfm", [128, 2 * (8 * 512)], BF16)
    H["qfm"] = sb("qfm", [128, 2 * (8 * 256)], BF16)
    H["kfm"] = sb("kfm", [128, 2 * (8 * 512)], BF16)
    H["kdfm"] = sb("kdfm", [128, 2 * (8 * 512)], BF16)
    H["vsb"] = sb("vsb", [128, 2 * (4 * 1024)], BF16)
    H["vdsb"] = sb("vdsb", [128, 2 * (4 * 1024)], BF16)
    H["exf"] = sb("exf", [128, 2048], F32)
    H["smx"] = sb("smx", [128, 64], F32)
    H["abf"] = sb("abf", [128, 2048], BF16)
    H["aTl"] = sb("aTl", [128, 2 * 1024], BF16)
    H["aTc"] = sb("aTc", [128, 2 * 1024], BF16)
    H["att"] = sb("att", [128, 8 * 256], BF16)
    H["r32"] = sb("r32", [128, 2 * 1024], F32)
    H["mvar"] = sb("mvar", [128, 32], F32)
    H["epsc"] = sb("epsc", [128, 1], F32)
    for nm in ("psA", "psB", "psC", "psD", "psE", "psF", "psG"):
        H[nm] = ps(nm, [128, 512], F32)
    H["ptr"] = ps("ptr", [128, 1024], BF16)
    H["ptr2"] = H["psG"][:].bitcast(BF16)
    return H


def emit_p1(ctx, H, IN, x1_out, dbg=None):
    E = ctx.emit
    idb = H["identb"]

    t_wld = []
    for nm, src in (("wqb", "wq"), ("wkb", "wk"), ("wvb", "wv"), ("wob", "wo")):
        for k in range(8):
            t_wld.append(E("gpsimd", lambda g, s, nm=nm, src=src, k=k: g.dma_start(
                out=H[nm][:, k * 1024:(k + 1) * 1024], in_=IN[src][k * 128:(k + 1) * 128, :]),
                inc="gw"))
    t_b = []
    for nm, src in (("bqc", "bq"), ("bkc", "bk")):
        for m in range(8):
            t_b.append(E("sync", lambda s_, s, nm=nm, src=src, m=m: s_.dma_start(
                out=H[nm][:, m:m + 1],
                in_=IN[src][m * 128:(m + 1) * 128].rearrange("(p x) -> p x", x=1)),
                inc="ld"))
    t_b.append(E("sync", lambda s_, s: s_.dma_start(out=H["r32"][0:1, 0:1024], in_=IN["bv"][:]), inc="ld"))
    t_b.append(E("sync", lambda s_, s: s_.dma_start(out=H["onesf"][0:1, :], in_=IN["onesf"][:]), inc="ld"))
    t_b.append(E("sync", lambda s_, s: s_.dma_start(out=H["wmaskr"][0:1, :], in_=IN["wmask"][:]), inc="ld"))
    t_b.append(E("sync", lambda s_, s: s_.dma_start(out=H["identb"][:], in_=IN["ident"][:]), inc="ld"))
    t_bld = t_b[-1]

    jobs = [("psA", "repbv", 0), ("psB", "repbv", 512)]
    t_rep = []
    for i, (pq, dst, off) in enumerate(jobs):
        t = E("tensor", lambda te, s, pq=pq, off=off: te.matmul(
            out=H[pq][:, 0:512], lhsT=H["onesf"][0:1, :],
            rhs=H["r32"][0:1, off:off + 512], start=True, stop=True),
            deps=tmax(t_bld), inc="pe")
        t_rep.append(E("vector", lambda v, s, pq=pq, dst=dst, off=off: v.tensor_copy(
            out=H[dst][:, off % 1024:off % 1024 + 512], in_=H[pq][:, 0:512]),
            deps=tmax(t), inc="dve"))
    t_rep.append(E("gpsimd", lambda g, s: g.dma_start(out=H["bobf"][0:1, :], in_=IN["bo"][:]), inc="gw"))
    t_rep.append(E("vector", lambda v, s: v.memset(H["onebf"][0:1, :], 1.0), inc="dve"))
    t_rep = t_rep + [t_rep[-1]]
    t = E("tensor", lambda te, s: te.matmul(
        out=H["psA"][:, 0:4], lhsT=H["onesf"][0:1, :], rhs=H["wmaskr"][0:1, :],
        start=True, stop=True), deps=tmax(t_bld, t_rep[0], t_rep[1]), inc="pe")
    t_rep.append(E("vector", lambda v, s: v.tensor_copy(
        out=H["repwm"][:], in_=H["psA"][:, 0:4]), deps=tmax(t), inc="dve"))
    t_wready = tmax(t_wld[-1], t_b[7], t_b[15], *t_rep)
    t_eps = E("vector", lambda v, s: v.memset(H["epsc"][:], 1e-5), inc="dve")

    last_xof_use = [None]
    last_xdf_use = [None]
    prev_ptr_user = [None, None]
    prev_aTl_rd = [None, None]
    prev_aTc_rd = [None, None]
    prev_store = {0: None, 1: None}
    prev_mvar_act = {0: None, 1: None}
    prev_abf_user = {0: None, 1: None}
    prev_exf_user = {0: None, 1: None}
    prev_smx_user = {0: None, 1: None}
    prev_exp = {}
    t_avdone = {"psF": None, "psG": None}
    psq = ("psA", "psB")

    QW, KW, VW = 8 * 256, 8 * 512, 8 * 512
    XW = 8 * 512
    VSW = 4 * 1024

    def load_x(b, dep_tok):
        bp = b % 2
        toks = []
        for src_nm, dst, W in (("xofm", "xfm", XW), ("xdfm", "xnfm", XW)):
            for k in range(8):
                toks.append(E("gpsimd", lambda g, s, b=b, k=k, src_nm=src_nm, dst=dst, W=W, bp=bp: g.dma_start(
                    out=H[dst][:, bp * W + k * 512:bp * W + (k + 1) * 512],
                    in_=IN[src_nm][b, k * 128:(k + 1) * 128, :]),
                    deps=tmax(dep_tok) if not toks else (), inc="gld"))
        return toks[-1]

    def stage_B(b, t_xready, first):
        bp = b % 2
        t_q = []
        psq = ("psA", "psB")
        for m in range(8):
            pq = psq[m % 2]
            dep = tmax(t_xready, t_q[m - 2] if m >= 2 else projB_war[m % 2])
            if first and m == 0:
                dep = dep + t_wready
            mm = None
            for k in range(8):
                mm = E("tensor", lambda te, s, m=m, k=k, pq=pq, bp=bp: te.matmul(
                    out=H[pq][:, 0:256],
                    lhsT=H["wqb"][:, k * 1024 + m * 128:k * 1024 + (m + 1) * 128],
                    rhs=H["xfm"][:, bp * XW + k * 512:bp * XW + k * 512 + 256],
                    start=(k == 0), stop=(k == 7)),
                    deps=(dep if k == 0 else ()), inc=("pe" if k == 7 else None))
            t_q.append(E("vector", lambda v, s, m=m, pq=pq, bp=bp: v.tensor_scalar(
                out=H["qfm"][:, bp * QW + m * 256:bp * QW + (m + 1) * 256], in0=H[pq][:, 0:256],
                scalar1=H["bqc"][:, m:m + 1], scalar2=0.125, op0=ALU.add, op1=ALU.mult),
                deps=tmax(mm), inc="dve"))
            projB_war[m % 2] = t_q[-1]
            yield None
        t_k = []
        for m in range(8):
            pq = psq[m % 2]
            dep = tmax(t_k[m - 2] if m >= 2 else projB_war[m % 2])
            mm = None
            for k in range(8):
                mm = E("tensor", lambda te, s, m=m, k=k, pq=pq, bp=bp: te.matmul(
                    out=H[pq][:, 0:512],
                    lhsT=H["wkb"][:, k * 1024 + m * 128:k * 1024 + (m + 1) * 128],
                    rhs=H["xfm"][:, bp * XW + k * 512:bp * XW + (k + 1) * 512],
                    start=(k == 0), stop=(k == 7)),
                    deps=(dep if k == 0 else ()), inc=("pe" if k == 7 else None))
            t_k.append(E("vector", lambda v, s, m=m, pq=pq, bp=bp: v.tensor_scalar(
                out=H["kfm"][:, bp * KW + m * 512:bp * KW + (m + 1) * 512], in0=H[pq][:, 0:512],
                scalar1=H["bkc"][:, m:m + 1], scalar2=None, op0=ALU.add),
                deps=tmax(mm), inc="dve"))
            projB_war[m % 2] = t_k[-1]
            yield None
        t_kd = []
        for m in range(8):
            pq = psq[m % 2]
            dep = tmax(t_kd[m - 2] if m >= 2 else projB_war[m % 2])
            mm = None
            for k in range(8):
                mm = E("tensor", lambda te, s, m=m, k=k, pq=pq, bp=bp: te.matmul(
                    out=H[pq][:, 0:512],
                    lhsT=H["wkb"][:, k * 1024 + m * 128:k * 1024 + (m + 1) * 128],
                    rhs=H["xnfm"][:, bp * XW + k * 512:bp * XW + (k + 1) * 512],
                    start=(k == 0), stop=(k == 7)),
                    deps=(dep if k == 0 else ()), inc=("pe" if k == 7 else None))
            t_kd.append(E("vector", lambda v, s, m=m, pq=pq, bp=bp: v.tensor_scalar(
                out=H["kdfm"][:, bp * KW + m * 512:bp * KW + (m + 1) * 512], in0=H[pq][:, 0:512],
                scalar1=H["bkc"][:, m:m + 1], scalar2=None, op0=ALU.add),
                deps=tmax(mm), inc="dve"))
            projB_war[m % 2] = t_kd[-1]
            yield None
        t_v = []
        for srcb, dst in (("xfm", "vsb"), ("xnfm", "vdsb")):
            for tc in range(4):
                for dh in range(2):
                    i = len(t_v)
                    pq = psq[i % 2]
                    dep = tmax(t_v[i - 2] if i >= 2 else projB_war[i % 2])
                    mm = None
                    for k in range(8):
                        mm = E("tensor", lambda te, s, srcb=srcb, tc=tc, dh=dh, k=k, pq=pq, bp=bp: te.matmul(
                            out=H[pq][:, 0:512],
                            lhsT=H[srcb][:, bp * XW + k * 512 + tc * 128:bp * XW + k * 512 + (tc + 1) * 128],
                            rhs=H["wvb"][:, k * 1024 + dh * 512:k * 1024 + (dh + 1) * 512],
                            start=(k == 0), stop=(k == 7)),
                            deps=(dep if k == 0 else ()), inc=("pe" if k == 7 else None))
                    t_v.append(E("vector", lambda v, s, dst=dst, tc=tc, dh=dh, pq=pq, bp=bp: v.tensor_tensor(
                        out=H[dst][:, bp * VSW + tc * 1024 + dh * 512:bp * VSW + tc * 1024 + (dh + 1) * 512],
                        in0=H[pq][:, 0:512], in1=H["repbv"][:, dh * 512:(dh + 1) * 512],
                        op=ALU.add), deps=tmax(mm), inc="dve"))
                    projB_war[i % 2] = t_v[-1]
                    yield None
        proj_done[b % 2] = tmax(t_q[-1], t_k[-1], t_kd[-1], *t_v)

    projB_war = [None, None]
    proj_done = [None, None]
    prev_exp = {}
    t_avdone = {"psF": None}
    t_lb_tok = load_x(0, None)
    genB = stage_B(0, tmax(t_lb_tok), True)
    for _ in genB:
        pass
    next_gen = None
    for b in range(B):
        bp = b % 2
        t_ld = []
        for i in range(2):
            t_ld.append(E("sync", lambda s_, s, b=b, i=i: s_.dma_start(
                out=H["r32"][:, i * 1024:(i + 1) * 1024],
                in_=IN["xo"][b, i * 128:(i + 1) * 128, :]),
                deps=tmax(*last_xof_use, t_rep[0], t_rep[1],
                          prev_store[0], prev_store[1]), inc="ld"))
        if b + 1 < B:
            tok = load_x(b + 1, proj_done[(b + 1) % 2])
            next_gen = stage_B(b + 1, tmax(tok), False)
        else:
            next_gen = iter(())
        t_proj_done = proj_done[bp]

        t_att_evac = []
        for h in range(16):
            hp, ho = h // 2, (h % 2) * 64
            par = h % 2
            eB, aB, sB = par * 1024, par * 1024, par * 32
            pav = "psF"
            ptrh = ("ptr", "ptr2")[par]
            pl01 = (("psD", "psE") if par == 0 else ("psD", "psE"))
            pl01 = (("psD", "psE"))
            for _ in range(2):
                try:
                    next(next_gen)
                except StopIteration:
                    pass
            t_sc = []
            for qc, pl in ((0, pl01[0]), (1, pl01[1])):
                dep = tmax(t_proj_done, prev_exp.get(("c", pl)), prev_exp.get(("l", pl)))
                t_sc.append(E("tensor", lambda te, s, hp=hp, ho=ho, qc=qc, pl=pl, bp=bp: te.matmul(
                    out=H[pl][:, 0:512],
                    lhsT=H["qfm"][ho:ho + 64, bp * QW + hp * 256 + qc * 128:bp * QW + hp * 256 + (qc + 1) * 128],
                    rhs=H["kfm"][ho:ho + 64, bp * KW + hp * 512:bp * KW + (hp + 1) * 512],
                    start=True, stop=True), deps=dep, inc="pe"))
            t_abf = []
            for qc, pl in ((0, pl01[0]), (1, pl01[1])):
                t1 = E("vector", lambda v, s, qc=qc, pl=pl, sB=sB: v.reduce_max(
                    out=H["smx"][:, sB + qc:sB + qc + 1], in_=H[pl][:, 0:512],
                    axis=AX.X, negate=True),
                    deps=tmax(t_sc[qc], prev_smx_user[par]), inc="dve")
                t2 = E("scalar", lambda sc, s, qc=qc, pl=pl, sB=sB, eB=eB: sc.activation(
                    out=H["exf"][:, eB + qc * 512:eB + (qc + 1) * 512], in_=H[pl][:, 0:512],
                    func=AF.Exp, bias=H["smx"][:, sB + qc:sB + qc + 1], scale=1.0,
                    accum_out=H["smx"][:, sB + 2 + qc:sB + 3 + qc]),
                    deps=tmax(t1, prev_exf_user[par]), inc="act")
                prev_exp[("l", pl)] = t2
                t3 = E("vector", lambda v, s, qc=qc, sB=sB: v.reciprocal(
                    out=H["smx"][:, sB + 4 + qc:sB + 5 + qc],
                    in_=H["smx"][:, sB + 2 + qc:sB + 3 + qc]),
                    deps=tmax(t2), inc="dve")
                t_abf.append(E("scalar", lambda sc, s, qc=qc, sB=sB, eB=eB, aB=aB: sc.activation(
                    out=H["abf"][:, aB + qc * 512:aB + (qc + 1) * 512],
                    in_=H["exf"][:, eB + qc * 512:eB + (qc + 1) * 512],
                    func=AF.Copy, scale=H["smx"][:, sB + 4 + qc:sB + 5 + qc], bias=0.0),
                    deps=tmax(t2, t3, prev_abf_user[par]), inc="act"))
            for _ in range(1):
                try:
                    next(next_gen)
                except StopIteration:
                    pass
            tt = None
            for qc in range(2):
                for kc in range(4):
                    dep = tmax(t_abf[0], t_abf[1], prev_ptr_user[par]) if (qc == 0 and kc == 0) else ()
                    tt = E("tensor", lambda te, s, qc=qc, kc=kc, aB=aB, ptrh=ptrh: te.transpose(
                        out=H[ptrh][:, kc * 256 + qc * 128:kc * 256 + (qc + 1) * 128],
                        in_=H["abf"][:, aB + qc * 512 + kc * 128:aB + qc * 512 + (kc + 1) * 128],
                        identity=idb[:]),
                        deps=dep, inc=("pe" if (qc == 1 and kc == 3) else None))
            t_aTl = E("vector", lambda v, s, ptrh=ptrh, aB=aB: v.tensor_copy(
                out=H["aTl"][:, aB:aB + 1024], in_=H[ptrh][:]),
                deps=tmax(tt, prev_aTl_rd[par]), inc="dve")
            prev_ptr_user[par] = t_aTl
            mmav = None
            for kc in range(4):
                dep = tmax(t_aTl, t_avdone[pav]) if kc == 0 else ()
                mmav = E("tensor", lambda te, s, h=h, ho=ho, kc=kc, pav=pav, aB=aB, bp=bp: te.matmul(
                    out=H[pav][ho:ho + 64, 0:256],
                    lhsT=H["vsb"][:, bp * VSW + kc * 1024 + h * 64:bp * VSW + kc * 1024 + (h + 1) * 64],
                    rhs=H["aTl"][:, aB + kc * 256:aB + (kc + 1) * 256],
                    start=(kc == 0), stop=False), deps=dep, inc=None)
            for qc, pl in ((0, pl01[0]), (1, pl01[1])):
                mm = None
                dep0 = tmax(prev_exp.get(("l", pl)))
                for j in range(4):
                    mm = E("tensor", lambda te, s, hp=hp, ho=ho, qc=qc, j=j, pl=pl, bp=bp: te.matmul(
                        out=H[pl][:, j * 128:(j + 1) * 128],
                        lhsT=H["qfm"][ho:ho + 64, bp * QW + hp * 256 + qc * 128:bp * QW + hp * 256 + (qc + 1) * 128],
                        rhs=H["kdfm"][ho:ho + 64, bp * KW + hp * 512 + j * 128:bp * KW + hp * 512 + (j + 1) * 128],
                        start=True, stop=True),
                        deps=(dep0 if j == 0 else ()), inc=("pe" if j == 3 else None))
                t1 = E("vector", lambda v, s, qc=qc, pl=pl, sB=sB: v.tensor_reduce(
                    out=H["smx"][:, sB + 8 + qc * 4:sB + 12 + qc * 4],
                    in_=H[pl][:, 0:512].rearrange("p (j x) -> p j x", j=4),
                    axis=AX.X, op=ALU.max, negate=True),
                    deps=tmax(mm), inc="dve")
                texp = None
                for j in range(4):
                    texp = E("scalar", lambda sc, s, qc=qc, j=j, pl=pl, sB=sB, eB=eB: sc.activation(
                        out=H["exf"][:, eB + qc * 512 + j * 128:eB + qc * 512 + (j + 1) * 128],
                        in_=H[pl][:, j * 128:(j + 1) * 128], func=AF.Exp,
                        bias=H["smx"][:, sB + 8 + qc * 4 + j:sB + 9 + qc * 4 + j], scale=1.0,
                        accum_out=H["smx"][:, sB + 16 + qc * 4 + j:sB + 17 + qc * 4 + j]),
                        deps=tmax(t1, t_abf[qc]) if j == 0 else (), inc=("act" if j == 3 else None))
                prev_exp[("c", pl)] = texp
                t2 = E("vector", lambda v, s, qc=qc, sB=sB: v.reciprocal(
                    out=H["smx"][:, sB + 24 + qc * 4:sB + 28 + qc * 4],
                    in_=H["smx"][:, sB + 16 + qc * 4:sB + 20 + qc * 4]),
                    deps=tmax(texp), inc="dve")
                t3 = E("vector", lambda v, s, qc=qc, sB=sB: v.tensor_tensor(
                    out=H["smx"][:, sB + 24 + qc * 4:sB + 28 + qc * 4],
                    in0=H["smx"][:, sB + 24 + qc * 4:sB + 28 + qc * 4], in1=H["repwm"][:, 0:4],
                    op=ALU.mult), deps=tmax(t2), inc="dve")
                for j in range(4):
                    E("scalar", lambda sc, s, qc=qc, j=j, sB=sB, eB=eB, aB=aB: sc.activation(
                        out=H["abf"][:, aB + qc * 512 + j * 128:aB + qc * 512 + (j + 1) * 128],
                        in_=H["exf"][:, eB + qc * 512 + j * 128:eB + qc * 512 + (j + 1) * 128],
                        func=AF.Copy, scale=H["smx"][:, sB + 24 + qc * 4 + j:sB + 25 + qc * 4 + j],
                        bias=0.0),
                        deps=tmax(t3, texp, t_aTl), inc="act")
            t_cabf = ("act", ctx.cnt["act"])
            for _ in range(1):
                try:
                    next(next_gen)
                except StopIteration:
                    pass
            tt = None
            for qc in range(2):
                for j in range(4):
                    dep = tmax(t_cabf, prev_ptr_user[par]) if (qc == 0 and j == 0) else ()
                    tt = E("tensor", lambda te, s, qc=qc, j=j, aB=aB, ptrh=ptrh: te.transpose(
                        out=H[ptrh][:, j * 256 + qc * 128:j * 256 + (qc + 1) * 128],
                        in_=H["abf"][:, aB + qc * 512 + j * 128:aB + qc * 512 + (j + 1) * 128],
                        identity=idb[:]),
                        deps=dep, inc=("pe" if (qc == 1 and j == 3) else None))
            t_aTc = E("vector", lambda v, s, ptrh=ptrh, aB=aB: v.tensor_copy(
                out=H["aTc"][:, aB:aB + 1024], in_=H[ptrh][:]),
                deps=tmax(tt, prev_aTc_rd[par]), inc="dve")
            prev_ptr_user[par] = t_aTc
            prev_abf_user[par] = t_aTc
            prev_exf_user[par] = t_cabf
            prev_smx_user[par] = t_cabf
            for j in range(4):
                mmav = E("tensor", lambda te, s, h=h, ho=ho, j=j, pav=pav, aB=aB, bp=bp: te.matmul(
                    out=H[pav][ho:ho + 64, 0:256],
                    lhsT=H["vdsb"][:, bp * VSW + j * 1024 + h * 64:bp * VSW + j * 1024 + (h + 1) * 64],
                    rhs=H["aTc"][:, aB + j * 256:aB + (j + 1) * 256],
                    start=False, stop=(j == 3)),
                    deps=tmax(t_aTc) if j == 0 else (), inc=("pe" if j == 3 else None))
            prev_aTl_rd[par] = mmav
            prev_aTc_rd[par] = mmav
            if h % 2 == 1:
                tev = E("vector", lambda v, s, h=h, pav=pav: v.tensor_copy(
                    out=H["att"][:, (h // 2) * 256:(h // 2 + 1) * 256], in_=H[pav][:, 0:256]),
                    deps=tmax(mmav), inc="dve")
                t_att_evac.append(tev)
                t_avdone[pav] = tev

        for _ in next_gen:
            pass

        t_r = []
        psd = ("psD", "psE")
        for tc in range(2):
            for dh in range(2):
                gi = tc * 2 + dh
                pq = psd[gi % 2]
                dep = tmax(*t_att_evac, t_r[-2] if len(t_r) >= 2 else
                           tmax(prev_exp.get(("l", "psD")), prev_exp.get(("l", "psE")),
                                prev_exp.get(("c", "psD")), prev_exp.get(("c", "psE"))))
                for k in range(8):
                    E("tensor", lambda te, s, tc=tc, dh=dh, k=k, pq=pq: te.matmul(
                        out=H[pq][:, 0:512],
                        lhsT=H["att"][:, k * 256 + tc * 128:k * 256 + (tc + 1) * 128],
                        rhs=H["wob"][:, k * 1024 + dh * 512:k * 1024 + (dh + 1) * 512],
                        start=(k == 0), stop=False),
                        deps=(dep if k == 0 else ()), inc=None)
                mm = E("tensor", lambda te, s, dh=dh, pq=pq: te.matmul(
                    out=H[pq][:, 0:512], lhsT=H["onebf"][0:1, :],
                    rhs=H["bobf"][0:1, dh * 512:(dh + 1) * 512],
                    start=False, stop=True), deps=(), inc="pe")
                t_r.append(E("vector", lambda v, s, tc=tc, dh=dh, pq=pq: v.tensor_tensor(
                    out=H["r32"][:, tc * 1024 + dh * 512:tc * 1024 + (dh + 1) * 512],
                    in0=H[pq][:, 0:512],
                    in1=H["r32"][:, tc * 1024 + dh * 512:tc * 1024 + (dh + 1) * 512],
                    op=ALU.add), deps=tmax(mm, t_ld[tc]), inc="dve"))
        prev_exp[("l", "psD")] = tmax(prev_exp.get(("l", "psD")), t_r[-2], t_r[-1])
        prev_exp[("l", "psE")] = tmax(prev_exp.get(("l", "psE")), t_r[-2], t_r[-1])
        t_store = []
        for tc in range(2):
            c0 = tc * 16
            t1 = E("vector", lambda v, s, tc=tc, c0=c0: v.bn_stats(
                out=H["mvar"][:, c0:c0 + 6], in_=H["r32"][:, tc * 1024:tc * 1024 + 512]),
                deps=tmax(t_r[tc * 2], prev_mvar_act[tc]), inc="dve")
            t1b = E("vector", lambda v, s, tc=tc, c0=c0: v.bn_stats(
                out=H["mvar"][:, c0 + 6:c0 + 12], in_=H["r32"][:, tc * 1024 + 512:tc * 1024 + 1024]),
                deps=tmax(t_r[tc * 2 + 1], t1), inc="dve")
            t2 = E("vector", lambda v, s, tc=tc, c0=c0: v.bn_aggr(
                out=H["mvar"][:, c0 + 12:c0 + 14], in_=H["mvar"][:, c0:c0 + 12]),
                deps=tmax(t1b), inc="dve")
            t3 = E("scalar", lambda sc, s, c0=c0: sc.activation(
                out=H["mvar"][:, c0 + 14:c0 + 15], in_=H["mvar"][:, c0 + 13:c0 + 14],
                func=AF.Sqrt, bias=H["epsc"][:, 0:1], scale=1.0), deps=tmax(t2, t_eps), inc="act")
            prev_mvar_act[tc] = t3
            t4 = E("vector", lambda v, s, c0=c0: v.reciprocal(
                out=H["mvar"][:, c0 + 15:c0 + 16], in_=H["mvar"][:, c0 + 14:c0 + 15]),
                deps=tmax(t3), inc="dve")
            t5 = E("vector", lambda v, s, tc=tc, c0=c0: v.tensor_scalar(
                out=H["r32"][:, tc * 1024:(tc + 1) * 1024],
                in0=H["r32"][:, tc * 1024:(tc + 1) * 1024],
                scalar1=H["mvar"][:, c0 + 12:c0 + 13], scalar2=H["mvar"][:, c0 + 15:c0 + 16],
                op0=ALU.subtract, op1=ALU.mult),
                deps=tmax(t4), inc="dve")
            last_xof_use.append(t5)
            t_store.append(E("gpsimd", lambda g, s, b=b, tc=tc: g.dma_start(
                out=x1_out[tc * 128:(tc + 1) * 128, b, :],
                in_=H["r32"][:, tc * 1024:(tc + 1) * 1024]),
                deps=tmax(t5), inc="st"))
            prev_store[tc] = t_store[tc]
        last_xof_use = [t_r[-1]] + last_xof_use[-2:]

    E("gpsimd", lambda g, s: g.nop(), deps=[("st", ctx.cnt["st"])], inc=None)



NEXP = 15
NST = 4
SLOT1 = 16384
SLOT2 = 8192


def alloc_p2(sb, ps):
    H = {}
    H["yacc"] = sb("yacc", [128, 8 * 1024], F32)
    H["x1fm"] = sb("x1fm", [128, 8 * 1024], BF16)
    H["xtb"] = sb("xtb", [128, 2 * 1024], BF16)
    H["fc1T"] = sb("fc1T", [128, 2 * SLOT1], BF16)
    H["fc2T"] = sb("fc2T", [128, 2 * SLOT2], BF16)
    H["abf"] = sb("abf2", [128, 16 * 1024], BF16)
    H["gel"] = sb("gel", [128, 2 * 512], F32)
    H["wstg"] = sb("wstg", [128, 3 * 1024], F32)
    H["comb"] = sb("comb", [128, 8 * 16], F32)
    H["combT"] = sb("combT", [15, 1024], BF16)
    H["cbt"] = sb("cbt", [128, 16], BF16)
    H["pbuf"] = sb("pbuf", [128, 64], F32)
    H["rstat"] = sb("rstat", [128, 8], F32)
    H["gwT"] = sb("gwT", [128, 8 * 16], F32)
    H["identf"] = sb("identf_s", [128, 128], F32)
    H["fc1b"] = sb("fc1b", [128, 240], F32)
    H["geglu"] = sb("geglu", [128, 120], F32)
    H["fc2brow"] = sb("fc2brow", [15, 1024], BF16)
    H["sfc1b"] = sb("sfc1b", [128, 32], F32)
    H["sgeglu"] = sb("sgeglu", [128, 16], F32)
    H["repsb"] = sb("repsb", [128, 1024], F32)
    H["identb"] = sb("identb", [128, 128], BF16)
    H["mvar"] = sb("mvar", [128, 32], F32)
    H["epsc"] = sb("epsc", [128, 1], F32)
    for nm in ("psA", "psB", "psC", "psD", "psE", "psF", "psG"):
        H[nm] = ps(nm, [128, 512], F32)
    H["ptr"] = ps("ptr", [128, 1024], BF16)
    H["ptr2"] = H["psG"][:].bitcast(BF16)
    return H


def emit_p2(ctx, H, IN, x2_out):
    E = ctx.emit
    idb = H["identb"]
    H = dict(H)
    H["x2t"] = H["gel"]
    state = {"xtb": 0, "xtb_rd": [None, None], "ptr_rd": None, "cv_prev": [None, None]}

    def ct_tile(load_tok, src_sb_ap_fn, evac_out_fn, war_deps=()):
        sl = state["xtb"]
        state["xtb"] = 1 - sl
        cv = E("scalar", lambda sc, s, sl=sl: sc.copy(
            out=H["xtb"][:, sl * 1024:(sl + 1) * 1024], in_=src_sb_ap_fn()),
            deps=tmax(load_tok, state["xtb_rd"][sl]), inc="act")
        tt = None
        for k in range(8):
            dep = tmax(cv, state["ptr_rd"], *war_deps) if k == 0 else ()
            tt = E("tensor", lambda te, s, sl=sl, k=k: te.transpose(
                out=H["ptr"][:, k * 128:(k + 1) * 128],
                in_=H["xtb"][:, sl * 1024 + k * 128:sl * 1024 + (k + 1) * 128],
                identity=idb[:]), deps=dep, inc=("pe" if k == 7 else None))
        state["xtb_rd"][sl] = tt
        ev = E("vector", lambda v, s: v.tensor_copy(
            out=evac_out_fn(), in_=H["ptr"][:].rearrange("p (k x) -> p k x", k=8)),
            deps=tmax(tt), inc="dve")
        state["ptr_rd"] = ev
        return cv, tt, ev

    t_c0 = E("vector", lambda v, s: v.memset(H["gel"][0:1, 0:128], 1.0), inc="dve")
    t_c1 = E("sync", lambda s_, s: s_.dma_start(out=H["identb"][:], in_=IN["ident"][:]), inc="ld")
    t_c1f = E("sync", lambda s_, s: s_.dma_start(out=H["identf"][:], in_=IN["identf"][:]), inc="ld")
    t_c2 = E("sync", lambda s_, s: s_.dma_start(out=H["wstg"][0:1, 2048:3072], in_=IN["s_fc2_b"][:]), inc="ld")
    t_rep = []
    for half in range(2):
        t = E("tensor", lambda te, s, half=half: te.matmul(
            out=H["psA"][:, 0:512], lhsT=H["gel"][0:1, 0:128],
            rhs=H["wstg"][0:1, 2048 + half * 512:2048 + (half + 1) * 512], start=True, stop=True),
            deps=tmax(t_c0, t_c2, t_rep[-1] if t_rep else None), inc="pe")
        t_rep.append(E("vector", lambda v, s, half=half: v.tensor_copy(
            out=H["repsb"][:, half * 512:(half + 1) * 512], in_=H["psA"][:, 0:512]),
            deps=tmax(t), inc="dve"))
    t_c3 = E("sync", lambda s_, s: s_.dma_start(out=H["wstg"][0:15, 2048:3072], in_=IN["gate_w"][:, :]),
             deps=tmax(t_rep[-1], ("pe", ctx.cnt["pe"])), inc="ld")
    t_c4 = E("sync", lambda s_, s: s_.dma_start(out=H["wstg"][0:15, 0:1024], in_=IN["fc2_b"][:, :]), inc="ld")
    t_cv4 = E("scalar", lambda sc, s: sc.copy(out=H["fc2brow"][0:15, :], in_=H["wstg"][0:15, 0:1024]),
              deps=tmax(t_c4), inc="act")
    t_bc = []
    for e in range(NEXP):
        for c in range(16):
            t_bc.append(E("sync", lambda s_, s, e=e, c=c: s_.dma_start(
                out=H["fc1b"][:, e * 16 + c:e * 16 + c + 1],
                in_=IN["fc1_b"][e, c * 128:(c + 1) * 128].rearrange("(p x) -> p x", x=1)), inc="ld"))
        for c in range(8):
            t_bc.append(E("sync", lambda s_, s, e=e, c=c: s_.dma_start(
                out=H["geglu"][:, e * 8 + c:e * 8 + c + 1],
                in_=IN["geglu_mult"][e, c * 128:(c + 1) * 128].rearrange("(p x) -> p x", x=1)), inc="ld"))
    for c in range(32):
        t_bc.append(E("sync", lambda s_, s, c=c: s_.dma_start(
            out=H["sfc1b"][:, c:c + 1],
            in_=IN["s_fc1_b"][c * 128:(c + 1) * 128].rearrange("(p x) -> p x", x=1)), inc="ld"))
    for c in range(16):
        t_bc.append(E("sync", lambda s_, s, c=c: s_.dma_start(
            out=H["sgeglu"][:, c:c + 1],
            in_=IN["s_geglu_mult"][c * 128:(c + 1) * 128].rearrange("(p x) -> p x", x=1)), inc="ld"))
    t_biases = t_bc[-1]
    t_eps = E("vector", lambda v, s: v.memset(H["epsc"][:], 1e-5), inc="dve")
    t_gwTl = []
    for k in range(8):
        t = E("tensor", lambda te, s, k=k: te.transpose(
            out=H["psG"][:, 0:15], in_=H["wstg"][0:15, 2048 + k * 128:2048 + (k + 1) * 128],
            identity=H["identf"][0:15, 0:15]),
            deps=tmax(t_c3, t_c1f, t_rep[-1], t_gwTl[-1] if t_gwTl else None), inc="pe")
        t_gwTl.append(E("vector", lambda v, s, k=k: v.tensor_copy(
            out=H["gwT"][:, k * 16:k * 16 + 15], in_=H["psG"][:, 0:15]),
            deps=tmax(t), inc="dve"))
    t_gwT = tmax(*t_gwTl)

    prev_store = [None, None]
    prev_yacc_rd = {i: None for i in range(8)}
    prev_mvar_act = [None, None]
    slot_free = {("fc1T", 0): None, ("fc1T", 1): None, ("fc2T", 0): None, ("fc2T", 1): None}
    slot_stg = {("fc1T", 0): None, ("fc1T", 1): None, ("fc2T", 0): None, ("fc2T", 1): None}
    prev_comb_rd = None
    prev_combT_rd = None
    prev_pG_rd = None
    prev_fc1_tail = None
    abf_rd = None
    state["cv_prev"][0] = tmax(t_cv4)

    def load_weight_slot(kind, e_or_none, ST, sl, nrows, src_fn, stage_ap_fn, slot_cols,
                         extra_war=None):
        base = sl * (SLOT1 if kind == "fc1T" else SLOT2)
        war0 = extra_war if extra_war is not None else tmax(
            slot_free[(kind, sl)], slot_stg[(kind, sl)])
        dst = H[kind]
        if ST == 0:
            evs = []
            ld_prev = [None, None]
            for fr in range(nrows):
                wsl = fr % 2
                tl = E("sync", lambda s_, s, fr=fr: s_.dma_start(
                    out=H["wstg"][:, (fr % 2) * 1024:(fr % 2) * 1024 + 1024],
                    in_=src_fn(fr)),
                    deps=tmax(state["cv_prev"][wsl]), inc="ld")
                cv, tr, ev = ct_tile(
                    tl,
                    (lambda fr=fr: H["wstg"][:, (fr % 2) * 1024:(fr % 2) * 1024 + 1024]),
                    (lambda base=base, fr=fr: dst[:, base:base + slot_cols].rearrange(
                        "p (k x) -> p k x", k=8)[:, :, fr * 128:(fr + 1) * 128]),
                    war_deps=(war0 if fr == 0 else ()))
                state["cv_prev"][wsl] = cv
                evs.append(ev)
            ready = tmax(*evs)
            stg = E("scalar", lambda sc, s, base=base: sc.dma_start(
                out=stage_ap_fn(), in_=dst[:, base:base + slot_cols]),
                deps=ready, inc="ld2")
            return ready, stg
        else:
            t = E("sync", lambda s_, s, base=base: s_.dma_start(
                out=dst[:, base:base + slot_cols], in_=stage_ap_fn()),
                deps=war0, inc="ld")
            return tmax(t), None

    for ST in range(NST):
        T0 = ST * 8
        t_fm = []
        t_yinit = []
        t_x1l = []
        for i in range(8):
            tl = E("sync", lambda s_, s, T0=T0, i=i: s_.dma_start(
                out=H["yacc"][:, i * 1024:(i + 1) * 1024],
                in_=IN["x1"][(T0 + i) * 128:(T0 + i + 1) * 128, :]),
                deps=tmax(prev_yacc_rd[i]), inc="ld")
            t_x1l.append(tl)
            cv, tr, ev = ct_tile(
                tl, (lambda i=i: H["yacc"][:, i * 1024:(i + 1) * 1024]),
                (lambda i=i: H["x1fm"][:].rearrange("p (k x) -> p k x", k=8)[:, :, i * 128:(i + 1) * 128]))
            t_fm.append(ev)
        t_fm_done = tmax(*t_fm)

        t_comb = []
        t_combT_ev = []
        prev_xf32_rd = None
        for tc in range(8):
            tts = []
            for k in range(8):
                dep = tmax(t_x1l[tc], prev_pG_rd, prev_xf32_rd,
                           t_gwT) if k == 0 else (tmax(tts[-1]) if k == 4 else ())
                t = E("tensor", lambda te, s, tc=tc, k=k: te.transpose(
                    out=H["psG"][:, (k % 4) * 128:(k % 4 + 1) * 128],
                    in_=H["yacc"][:, tc * 1024 + k * 128:tc * 1024 + (k + 1) * 128],
                    identity=H["identf"][:]),
                    deps=dep, inc=("pe" if (k % 4) == 3 else None))
                if (k % 4) == 3:
                    tts.append(E("vector", lambda v, s, k=k: v.tensor_copy(
                        out=H["wstg"][:, 2048 + (k - 3) * 128:2048 + (k + 1) * 128],
                        in_=H["psG"][:, 0:512]),
                        deps=tmax(t), inc="dve"))
            t_xf = tmax(*tts)
            t_yinit.append(E("vector", lambda v, s, tc=tc: v.tensor_tensor(
                out=H["yacc"][:, tc * 1024:(tc + 1) * 1024],
                in0=H["yacc"][:, tc * 1024:(tc + 1) * 1024],
                in1=H["repsb"][:, 0:1024], op=ALU.add),
                deps=tmax(t_xf, t_rep[-1]), inc="dve"))
            mm = None
            for k in range(8):
                dep = tmax(t_xf) if k == 0 else ()
                mm = E("tensor", lambda te, s, tc=tc, k=k: te.matmul(
                    out=H["psG"][:, 0:15],
                    lhsT=H["wstg"][:, 2048 + k * 128:2048 + (k + 1) * 128],
                    rhs=H["gwT"][:, k * 16:k * 16 + 15],
                    start=(k == 0), stop=(k == 7)),
                    deps=dep, inc=("pe" if k == 7 else None))
            prev_xf32_rd = mm
            r = H["rstat"]
            prv = t_comb[-1] if t_comb else None
            t1 = E("vector", lambda v, s: v.reduce_max(
                out=r[:, 0:1], in_=H["psG"][:, 0:15], axis=AX.X, negate=True),
                deps=tmax(mm, prv), inc="dve")
            t2 = E("scalar", lambda sc, s: sc.activation(
                out=H["pbuf"][:, 0:15], in_=H["psG"][:, 0:15], func=AF.Exp,
                bias=r[:, 0:1], scale=1.0), deps=tmax(t1), inc="act")
            prev_pG_rd = t2
            t3 = E("vector", lambda v, s: v.reduce_max(
                out=r[:, 1:2], in_=H["pbuf"][:, 0:15], axis=AX.X), deps=tmax(t2), inc="dve")
            t4 = E("vector", lambda v, s: v.tensor_scalar(
                out=H["pbuf"][:, 16:31], in0=H["pbuf"][:, 0:15], scalar1=r[:, 1:2],
                scalar2=None, op0=ALU.is_ge), deps=tmax(t3), inc="dve")
            t5 = E("vector", lambda v, s: v.scalar_tensor_tensor(
                out=H["pbuf"][:, 16:31], in0=H["pbuf"][:, 16:31], scalar=-2.0,
                in1=H["pbuf"][:, 0:15], op0=ALU.mult, op1=ALU.add), deps=tmax(t4), inc="dve")
            t6 = E("vector", lambda v, s: v.reduce_max(
                out=r[:, 2:3], in_=H["pbuf"][:, 16:31], axis=AX.X), deps=tmax(t5), inc="dve")
            t7 = E("vector", lambda v, s: v.tensor_scalar(
                out=H["pbuf"][:, 32:47], in0=H["pbuf"][:, 16:31], scalar1=r[:, 2:3],
                scalar2=None, op0=ALU.is_ge), deps=tmax(t6), inc="dve")
            t8 = E("vector", lambda v, s: v.scalar_tensor_tensor(
                out=H["pbuf"][:, 32:47], in0=H["pbuf"][:, 32:47], scalar=-2.0,
                in1=H["pbuf"][:, 16:31], op0=ALU.mult, op1=ALU.add), deps=tmax(t7), inc="dve")
            t9 = E("vector", lambda v, s: v.reduce_max(
                out=r[:, 3:4], in_=H["pbuf"][:, 32:47], axis=AX.X), deps=tmax(t8), inc="dve")
            t10 = E("vector", lambda v, s: v.tensor_scalar(
                out=H["pbuf"][:, 48:63], in0=H["pbuf"][:, 0:15], scalar1=r[:, 3:4],
                scalar2=None, op0=ALU.is_ge), deps=tmax(t9), inc="dve")
            t11 = E("vector", lambda v, s: v.tensor_tensor(
                out=H["pbuf"][:, 48:63], in0=H["pbuf"][:, 48:63], in1=H["pbuf"][:, 0:15],
                op=ALU.mult), deps=tmax(t10), inc="dve")
            t12 = E("vector", lambda v, s: v.reduce_sum(
                out=r[:, 4:5], in_=H["pbuf"][:, 48:63], axis=AX.X), deps=tmax(t11), inc="dve")
            t13 = E("vector", lambda v, s: v.reciprocal(
                out=r[:, 5:6], in_=r[:, 4:5]), deps=tmax(t12), inc="dve")
            t14 = E("vector", lambda v, s, tc=tc: v.tensor_scalar(
                out=H["comb"][:, tc * 16:tc * 16 + 15], in0=H["pbuf"][:, 48:63],
                scalar1=r[:, 5:6], scalar2=None, op0=ALU.mult),
                deps=tmax(t13, prev_comb_rd), inc="dve")
            t_comb.append(t14)
            c1 = E("scalar", lambda sc, s, tc=tc: sc.copy(
                out=H["cbt"][:, 0:15], in_=H["comb"][:, tc * 16:tc * 16 + 15]),
                deps=tmax(t14, t_combT_ev[-1] if t_combT_ev else None), inc="act")
            c2 = E("tensor", lambda te, s: te.transpose(
                out=H["ptr"][0:15, 0:128], in_=H["cbt"][:, 0:15], identity=idb[:]),
                deps=tmax(c1, state["ptr_rd"], prev_combT_rd), inc="pe")
            ev = E("vector", lambda v, s, tc=tc: v.tensor_copy(
                out=H["combT"][0:15, tc * 128:(tc + 1) * 128], in_=H["ptr"][0:15, 0:128]),
                deps=tmax(c2), inc="dve")
            state["ptr_rd"] = ev
            t_combT_ev.append(ev)
        t_comb_done = tmax(*t_comb)
        t_combT_done = tmax(*t_combT_ev)

        t_y_prev2 = [prev_yacc_rd[6], prev_yacc_rd[7]]
        t_y_last = None
        for e in range(NEXP):
            sl = e % 2
            w1_ready, w1_stg = load_weight_slot(
                "fc1T", e, ST, sl, 16,
                (lambda fr, e=e: IN["fc1_w"][e, fr * 128:(fr + 1) * 128, :]),
                (lambda e=e: IN["fc1T_st"][e]), SLOT1)
            w2_ready, w2_stg = load_weight_slot(
                "fc2T", e, ST, sl, 8,
                (lambda dr, e=e: IN["fc2_w"][e, dr * 128:(dr + 1) * 128, :]),
                (lambda e=e: IN["fc2T_st"][e]), SLOT2)
            b1, b2 = sl * SLOT1, sl * SLOT2
            t_a = []
            for fc in range(8):
                for th in range(2):
                    gi = fc * 2 + th
                    pg = ("psA", "psC")[gi % 2]
                    px = ("psB", "psD")[gi % 2]
                    mmg = None
                    for k in range(8):
                        dep = tmax(w1_ready, t_fm_done,
                                   t_a[gi - 2][2] if gi >= 2 else prev_fc1_tail) if k == 0 else ()
                        mmg = E("tensor", lambda te, s, fc=fc, th=th, k=k, pg=pg, b1=b1: te.matmul(
                            out=H[pg][:, 0:512],
                            lhsT=H["fc1T"][:, b1 + k * 2048 + 1024 + fc * 128:b1 + k * 2048 + 1024 + (fc + 1) * 128],
                            rhs=H["x1fm"][:, k * 1024 + th * 512:k * 1024 + (th + 1) * 512],
                            start=(k == 0), stop=(k == 7)),
                            deps=dep, inc=("pe" if k == 7 else None))
                    tg = E("scalar", lambda sc, s, th=th, pg=pg, e=e, fc=fc: sc.activation(
                        out=H["gel"][:, th * 512:(th + 1) * 512], in_=H[pg][:, 0:512],
                        func=AF.Gelu, bias=H["fc1b"][:, e * 16 + 8 + fc:e * 16 + 9 + fc], scale=1.0),
                        deps=tmax(mmg, t_a[gi - 2][1] if gi >= 2 else None), inc="act")
                    mmx = None
                    for k in range(8):
                        dep = tmax(t_a[gi - 2][1] if gi >= 2 else prev_fc1_tail) if k == 0 else ()
                        mmx = E("tensor", lambda te, s, fc=fc, th=th, k=k, px=px, b1=b1: te.matmul(
                            out=H[px][:, 0:512],
                            lhsT=H["fc1T"][:, b1 + k * 2048 + fc * 128:b1 + k * 2048 + (fc + 1) * 128],
                            rhs=H["x1fm"][:, k * 1024 + th * 512:k * 1024 + (th + 1) * 512],
                            start=(k == 0), stop=(k == 7)),
                            deps=dep, inc=("pe" if k == 7 else None))
                    ta1 = E("vector", lambda v, s, th=th, px=px, e=e, fc=fc: v.scalar_tensor_tensor(
                        out=H["gel"][:, th * 512:(th + 1) * 512], in0=H[px][:, 0:512],
                        scalar=H["fc1b"][:, e * 16 + fc:e * 16 + 1 + fc],
                        in1=H["gel"][:, th * 512:(th + 1) * 512],
                        op0=ALU.add, op1=ALU.mult),
                        deps=tmax(mmx, tg, t_biases), inc="dve")
                    ta2 = E("vector", lambda v, s, fc=fc, th=th, e=e: v.tensor_scalar(
                        out=H["abf"][:, fc * 1024 + th * 512:fc * 1024 + (th + 1) * 512],
                        in0=H["gel"][:, th * 512:(th + 1) * 512],
                        scalar1=H["geglu"][:, e * 8 + fc:e * 8 + 1 + fc], scalar2=None,
                        op0=ALU.mult), deps=tmax(ta1, abf_rd), inc="dve")
                    t_a.append((ta2, ta1, tg))
            t_a_done = tmax(*[t[0] for t in t_a])
            for tc in range(8):
                for dh in range(2):
                    gi = tc * 2 + dh
                    pp = ("psE", "psF")[gi % 2]
                    mm = None
                    for fc in range(8):
                        dep = tmax(w2_ready, t_a_done, t_y_prev2[gi % 2]) if fc == 0 else ()
                        mm = E("tensor", lambda te, s, tc=tc, dh=dh, fc=fc, pp=pp, b2=b2: te.matmul(
                            out=H[pp][:, 0:512],
                            lhsT=H["abf"][:, fc * 1024 + tc * 128:fc * 1024 + (tc + 1) * 128],
                            rhs=H["fc2T"][:, b2 + fc * 1024 + dh * 512:b2 + fc * 1024 + (dh + 1) * 512],
                            start=(fc == 0), stop=(fc == 7)),
                            deps=dep, inc=("pe" if fc == 7 else None))
                    ev = E("vector", lambda v, s, tc=tc, dh=dh, pp=pp, e=e: v.scalar_tensor_tensor(
                        out=H["yacc"][:, tc * 1024 + dh * 512:tc * 1024 + (dh + 1) * 512],
                        in0=H[pp][:, 0:512], scalar=H["comb"][:, tc * 16 + e:tc * 16 + e + 1],
                        in1=H["yacc"][:, tc * 1024 + dh * 512:tc * 1024 + (dh + 1) * 512],
                        op0=ALU.mult, op1=ALU.add),
                        deps=tmax(mm, t_comb_done, t_yinit[tc]), inc="dve")
                    t_y_prev2[gi % 2] = ev
                    t_y_last = ev
            prev_comb_rd = t_y_last
            slot_free[("fc1T", sl)] = t_a[-1][0]
            slot_free[("fc2T", sl)] = t_y_last
            slot_stg[("fc1T", sl)] = w1_stg
            slot_stg[("fc2T", sl)] = w2_stg
            abf_rd = t_y_last
            prev_fc1_tail = tmax(t_a[-2][1], t_a[-1][1], t_a[-2][2], t_a[-1][2])

        sl = NEXP % 2
        ws1_ready, ws1_stg = load_weight_slot(
            "fc1T", None, ST, 0, 32,
            (lambda fr: IN["s_fc1_w"][fr * 128:(fr + 1) * 128, :]),
            (lambda: IN["sfc1T_st"][:]), 2 * SLOT1,
            extra_war=tmax(slot_free[("fc1T", 0)], slot_free[("fc1T", 1)],
                           slot_stg[("fc1T", 0)], slot_stg[("fc1T", 1)]))
        b1 = 0
        t_a = []
        for fc in range(16):
            for th in range(2):
                gi = fc * 2 + th
                pg = ("psA", "psC")[gi % 2]
                px = ("psB", "psD")[gi % 2]
                mmg = None
                for k in range(8):
                    dep = tmax(ws1_ready, t_fm_done,
                               t_a[gi - 2][2] if gi >= 2 else prev_fc1_tail) if k == 0 else ()
                    mmg = E("tensor", lambda te, s, fc=fc, th=th, k=k, pg=pg, b1=b1: te.matmul(
                        out=H[pg][:, 0:512],
                        lhsT=H["fc1T"][:, b1 + k * 4096 + 2048 + fc * 128:b1 + k * 4096 + 2048 + (fc + 1) * 128],
                        rhs=H["x1fm"][:, k * 1024 + th * 512:k * 1024 + (th + 1) * 512],
                        start=(k == 0), stop=(k == 7)),
                        deps=dep, inc=("pe" if k == 7 else None))
                tg = E("scalar", lambda sc, s, th=th, pg=pg, fc=fc: sc.activation(
                    out=H["gel"][:, th * 512:(th + 1) * 512], in_=H[pg][:, 0:512],
                    func=AF.Gelu, bias=H["sfc1b"][:, 16 + fc:17 + fc], scale=1.0),
                    deps=tmax(mmg, t_a[gi - 2][1] if gi >= 2 else None), inc="act")
                mmx = None
                for k in range(8):
                    dep = tmax(t_a[gi - 2][1] if gi >= 2 else prev_fc1_tail) if k == 0 else ()
                    mmx = E("tensor", lambda te, s, fc=fc, th=th, k=k, px=px, b1=b1: te.matmul(
                        out=H[px][:, 0:512],
                        lhsT=H["fc1T"][:, b1 + k * 4096 + fc * 128:b1 + k * 4096 + (fc + 1) * 128],
                        rhs=H["x1fm"][:, k * 1024 + th * 512:k * 1024 + (th + 1) * 512],
                        start=(k == 0), stop=(k == 7)),
                        deps=dep, inc=("pe" if k == 7 else None))
                ta1 = E("vector", lambda v, s, th=th, px=px, fc=fc: v.scalar_tensor_tensor(
                    out=H["gel"][:, th * 512:(th + 1) * 512], in0=H[px][:, 0:512],
                    scalar=H["sfc1b"][:, fc:fc + 1],
                    in1=H["gel"][:, th * 512:(th + 1) * 512],
                    op0=ALU.add, op1=ALU.mult), deps=tmax(mmx, tg), inc="dve")
                ta2 = E("vector", lambda v, s, fc=fc, th=th: v.tensor_scalar(
                    out=H["abf"][:, fc * 1024 + th * 512:fc * 1024 + (th + 1) * 512],
                    in0=H["gel"][:, th * 512:(th + 1) * 512],
                    scalar1=H["sgeglu"][:, fc:fc + 1], scalar2=None,
                    op0=ALU.mult), deps=tmax(ta1, abf_rd), inc="dve")
                t_a.append((ta2, ta1, tg))
        t_a_done = tmax(*[t[0] for t in t_a])
        ws2_ready, ws2_stg = None, None
        if ST == 0:
            evs = []
            for dr in range(8):
                for half in range(2):
                    wsl = (dr * 2 + half) % 2
                    tl = E("sync", lambda s_, s, dr=dr, half=half, wsl=wsl: s_.dma_start(
                        out=H["wstg"][:, wsl * 1024:wsl * 1024 + 1024],
                        in_=IN["s_fc2_w"][dr * 128:(dr + 1) * 128, half * 1024:(half + 1) * 1024]),
                        deps=tmax(state["cv_prev"][wsl]), inc="ld")
                    war = (tmax(slot_free[("fc2T", 0)], slot_free[("fc2T", 1)],
                                slot_stg[("fc2T", 0)], slot_stg[("fc2T", 1)])
                           if (dr == 0 and half == 0) else ())
                    cv, tr, ev = ct_tile(
                        tl,
                        (lambda wsl=wsl: H["wstg"][:, wsl * 1024:wsl * 1024 + 1024]),
                        (lambda dr=dr, half=half: H["fc2T"][:].rearrange(
                            "p (k x) -> p k x", k=16)[:, half * 8:(half + 1) * 8, dr * 128:(dr + 1) * 128]),
                        war_deps=war)
                    state["cv_prev"][wsl] = cv
                    evs.append(ev)
            ws2_ready = tmax(*evs)
            ws2_stg = E("scalar", lambda sc, s: sc.dma_start(
                out=IN["sfc2T_st"][:], in_=H["fc2T"][:, 0:2 * SLOT2]),
                deps=ws2_ready, inc="ld2")
        else:
            t = E("sync", lambda s_, s: s_.dma_start(
                out=H["fc2T"][:, 0:2 * SLOT2], in_=IN["sfc2T_st"][:]),
                deps=tmax(slot_free[("fc2T", 0)], slot_free[("fc2T", 1)],
                          slot_stg[("fc2T", 0)], slot_stg[("fc2T", 1)]), inc="ld")
            ws2_ready = tmax(t)
        t_y_final = {}
        for tc in range(8):
            for dh in range(2):
                gi = tc * 2 + dh
                pp = ("psE", "psF")[gi % 2]
                mm = None
                for fc in range(16):
                    dep = tmax(ws2_ready, t_a_done, t_y_prev2[gi % 2], t_combT_done, t_cv4) if fc == 0 else ()
                    mm = E("tensor", lambda te, s, tc=tc, dh=dh, fc=fc, pp=pp: te.matmul(
                        out=H[pp][:, 0:512],
                        lhsT=H["abf"][:, fc * 1024 + tc * 128:fc * 1024 + (tc + 1) * 128],
                        rhs=H["fc2T"][:, fc * 1024 + dh * 512:fc * 1024 + (dh + 1) * 512],
                        start=(fc == 0), stop=False),
                        deps=dep, inc=None)
                mm = E("tensor", lambda te, s, tc=tc, dh=dh, pp=pp: te.matmul(
                    out=H[pp][:, 0:512],
                    lhsT=H["combT"][0:15, tc * 128:(tc + 1) * 128],
                    rhs=H["fc2brow"][0:15, dh * 512:(dh + 1) * 512],
                    start=False, stop=True), deps=(), inc="pe")
                ev = E("vector", lambda v, s, tc=tc, dh=dh, pp=pp: v.tensor_tensor(
                    out=H["yacc"][:, tc * 1024 + dh * 512:tc * 1024 + (dh + 1) * 512],
                    in0=H[pp][:, 0:512],
                    in1=H["yacc"][:, tc * 1024 + dh * 512:tc * 1024 + (dh + 1) * 512],
                    op=ALU.add), deps=tmax(mm), inc="dve")
                t_y_prev2[gi % 2] = ev
                t_y_final[tc] = ev
        prev_combT_rd = ("pe", ctx.cnt["pe"])
        slot_free[("fc1T", 0)] = t_a[-1][0]
        slot_free[("fc1T", 1)] = t_a[-1][0]
        slot_free[("fc2T", 0)] = t_y_final[7]
        slot_free[("fc2T", 1)] = t_y_final[7]
        slot_stg[("fc1T", 0)] = ws1_stg
        slot_stg[("fc1T", 1)] = ws1_stg
        slot_stg[("fc2T", 0)] = ws2_stg
        slot_stg[("fc2T", 1)] = ws2_stg
        abf_rd = ("pe", ctx.cnt["pe"])
        prev_fc1_tail = tmax(t_a[-2][1], t_a[-1][1], t_a[-2][2], t_a[-1][2])

        for tc in range(8):
            c0 = (tc % 2) * 16
            xsl = tc % 2
            t1 = E("vector", lambda v, s, tc=tc, c0=c0: v.bn_stats(
                out=H["mvar"][:, c0:c0 + 6], in_=H["yacc"][:, tc * 1024:tc * 1024 + 512]),
                deps=tmax(t_y_final[tc], prev_mvar_act[tc % 2]), inc="dve")
            t1b = E("vector", lambda v, s, tc=tc, c0=c0: v.bn_stats(
                out=H["mvar"][:, c0 + 6:c0 + 12], in_=H["yacc"][:, tc * 1024 + 512:tc * 1024 + 1024]),
                deps=tmax(t1), inc="dve")
            t2 = E("vector", lambda v, s, c0=c0: v.bn_aggr(
                out=H["mvar"][:, c0 + 12:c0 + 14], in_=H["mvar"][:, c0:c0 + 12]),
                deps=tmax(t1b), inc="dve")
            t3 = E("scalar", lambda sc, s, c0=c0: sc.activation(
                out=H["mvar"][:, c0 + 14:c0 + 15], in_=H["mvar"][:, c0 + 13:c0 + 14],
                func=AF.Sqrt, bias=H["epsc"][:, 0:1], scale=1.0),
                deps=tmax(t2, t_eps), inc="act")
            prev_mvar_act[tc % 2] = t3
            t4 = E("vector", lambda v, s, c0=c0: v.reciprocal(
                out=H["mvar"][:, c0 + 15:c0 + 16], in_=H["mvar"][:, c0 + 14:c0 + 15]),
                deps=tmax(t3), inc="dve")
            t5 = E("vector", lambda v, s, tc=tc, c0=c0: v.tensor_scalar(
                out=H["x2t"][:, 0:1024],
                in0=H["yacc"][:, tc * 1024:(tc + 1) * 1024],
                scalar1=H["mvar"][:, c0 + 12:c0 + 13], scalar2=H["mvar"][:, c0 + 15:c0 + 16],
                op0=ALU.subtract, op1=ALU.mult),
                deps=tmax(t4, prev_store[0]), inc="dve")
            st_t = E("gpsimd", lambda g, s, T0=T0, tc=tc: g.dma_start(
                out=x2_out[(T0 + tc) * 128:(T0 + tc + 1) * 128, :],
                in_=H["x2t"][:, 0:1024]),
                deps=tmax(t5), inc="st")
            prev_store[0] = st_t
            prev_yacc_rd[tc] = t5
        prev_fc1_tail = tmax(prev_fc1_tail, prev_store[0])

    E("gpsimd", lambda g, s: g.nop(), deps=[("st", ctx.cnt["st"])], inc=None)


def build_p2():
    nc = bass.Bass("TRN2", target_bir_lowering=False)
    IN = {}
    IN["x1"] = nc.dram_tensor("x1", [4096, D], F32, kind="ExternalInput").ap()
    IN["gate_w"] = nc.dram_tensor("gate_w", [NEXP, D], F32, kind="ExternalInput").ap()
    IN["fc1_w"] = nc.dram_tensor("fc1_w", [NEXP, 2048, D], F32, kind="ExternalInput").ap()
    IN["fc1_b"] = nc.dram_tensor("fc1_b", [NEXP, 2048], F32, kind="ExternalInput").ap()
    IN["geglu_mult"] = nc.dram_tensor("geglu_mult", [NEXP, D], F32, kind="ExternalInput").ap()
    IN["fc2_w"] = nc.dram_tensor("fc2_w", [NEXP, D, D], F32, kind="ExternalInput").ap()
    IN["fc2_b"] = nc.dram_tensor("fc2_b", [NEXP, D], F32, kind="ExternalInput").ap()
    IN["s_fc1_w"] = nc.dram_tensor("s_fc1_w", [4096, D], F32, kind="ExternalInput").ap()
    IN["s_fc1_b"] = nc.dram_tensor("s_fc1_b", [4096], F32, kind="ExternalInput").ap()
    IN["s_geglu_mult"] = nc.dram_tensor("s_geglu_mult", [2048], F32, kind="ExternalInput").ap()
    IN["s_fc2_w"] = nc.dram_tensor("s_fc2_w", [D, 2048], F32, kind="ExternalInput").ap()
    IN["s_fc2_b"] = nc.dram_tensor("s_fc2_b", [D], F32, kind="ExternalInput").ap()
    IN["onesf"] = nc.dram_tensor("onesf", [1, 128], F32, kind="ExternalInput").ap()
    IN["ident"] = nc.dram_tensor("ident", [128, 128], BF16, kind="ExternalInput").ap()
    IN["identf"] = nc.dram_tensor("identf", [128, 128], F32, kind="ExternalInput").ap()
    IN["fc1T_st"] = nc.dram_tensor("fc1T_st", [NEXP, 128, SLOT1], BF16, kind="Internal").ap()
    IN["fc2T_st"] = nc.dram_tensor("fc2T_st", [NEXP, 128, SLOT2], BF16, kind="Internal").ap()
    IN["sfc1T_st"] = nc.dram_tensor("sfc1T_st", [128, 2 * SLOT1], BF16, kind="Internal").ap()
    IN["sfc2T_st"] = nc.dram_tensor("sfc2T_st", [128, 2 * SLOT2], BF16, kind="Internal").ap()
    x2_out = nc.dram_tensor("x2o", [4096, D], F32, kind="ExternalOutput").ap()
    stack = ExitStack()
    with stack:
        H = alloc_p2(*real_allocators(nc, stack))
        ctx = Ctx()
        emit_p2(ctx, H, IN, x2_out)
        _finalize(ctx, nc, stack)
    return nc


def host_inputs_p2(x1_percore, inputs):
    ident = np.eye(128, dtype=np.float32).astype(BF)
    onesf = np.ones((1, 128), np.float32)
    maps = []
    for c in range(N_CORES):
        maps.append({
            "x1": np.ascontiguousarray(x1_percore[c].reshape(4096, D)),
            "gate_w": inputs["L0_ffn_gate_w"], "fc1_w": inputs["L0_ffn_fc1_w"],
            "fc1_b": inputs["L0_ffn_fc1_b"], "geglu_mult": inputs["L0_ffn_geglu_mult"],
            "fc2_w": inputs["L0_ffn_fc2_w"], "fc2_b": inputs["L0_ffn_fc2_b"],
            "s_fc1_w": inputs["L0_ffn_s_fc1_w"], "s_fc1_b": inputs["L0_ffn_s_fc1_b"],
            "s_geglu_mult": inputs["L0_ffn_s_geglu_mult"], "s_fc2_w": inputs["L0_ffn_s_fc2_w"],
            "s_fc2_b": inputs["L0_ffn_s_fc2_b"], "onesf": onesf, "ident": ident,
            "identf": np.eye(128, dtype=np.float32),
        })
    return maps




def alloc_p3(sb, ps):
    H = {}
    H["wqT"] = sb("wqT", [128, 8 * 1024], BF16)
    H["wkT"] = sb("wkT", [128, 8 * 1024], BF16)
    H["wvT"] = sb("wvT", [128, 8 * 1024], BF16)
    H["woutT"] = sb("woutT", [128, 8 * 1024], BF16)
    H["x2sb"] = sb("x2sb", [128, 4 * 1024], F32)
    H["xin"] = sb("xin", [128, 2 * 1024], F32)
    H["xtb"] = sb("xtb", [128, 2 * 1024], BF16)
    H["xinfm"] = sb("xinfm", [128, 8 * 512], BF16)
    H["qfm"] = sb("qfm", [128, 8 * 512], BF16)
    H["kfm"] = sb("kfm", [128, 2 * (8 * 512)], BF16)
    H["vsb"] = sb("vsb", [128, 2 * (4 * 1024)], BF16)
    H["smf"] = sb("smf", [128, 2048], F32)
    H["exf3"] = sb("exf3", [128, 2048], F32)
    H["aT"] = sb("aT", [128, 2048], BF16)
    H["aT2"] = sb("aT2", [128, 2048], BF16)
    H["attfm"] = sb("attfm", [128, 8 * 128], BF16)
    H["replnw"] = sb("replnw", [128, 1024], F32)
    H["replnb"] = sb("replnb", [128, 1024], F32)
    H["mask4"] = sb("mask4_s", [128, 512], F32)
    H["r32"] = sb("r32", [128, 2 * 1024], F32)
    H["x3t"] = sb("x3t", [128, 1024], F32)
    H["lrow"] = sb("lrow", [1, 2048], F32)
    H["onesf"] = sb("onesf_s", [1, 128], F32)
    H["identb"] = sb("identb", [128, 128], BF16)
    H["wstg"] = sb("wstg", [128, 2 * 1024], F32)
    H["mvar"] = sb("mvar", [128, 64], F32)
    H["smx"] = sb("smx", [128, 64], F32)
    H["epsc"] = sb("epsc", [128, 1], F32)
    for nm in ("psA", "psB", "psC", "psD", "psE", "psF", "psG"):
        H[nm] = ps(nm, [128, 512], F32)
    H["ptr"] = ps("ptr", [128, 1024], BF16)
    H["ptr2"] = H["psG"][:].bitcast(BF16)
    return H


def emit_p3(ctx, H, IN, x3_out, nsg=8, upto='full'):
    E = ctx.emit
    idb = H["identb"]
    state = {"xtb": 0, "xtb_rd": [None, None], "ptr_rd": None, "cv_prev": [None, None]}

    def ct_tile(load_tok, src_sb_ap_fn, evac_out_fn, war_deps=()):
        sl = state["xtb"]
        state["xtb"] = 1 - sl
        cv = E("scalar", lambda sc, s, sl=sl: sc.copy(
            out=H["xtb"][:, sl * 1024:(sl + 1) * 1024], in_=src_sb_ap_fn()),
            deps=tmax(load_tok, state["xtb_rd"][sl]), inc="act")
        tt = None
        for k in range(8):
            dep = tmax(cv, state["ptr_rd"], *war_deps) if k == 0 else ()
            tt = E("tensor", lambda te, s, sl=sl, k=k: te.transpose(
                out=H["ptr"][:, k * 128:(k + 1) * 128],
                in_=H["xtb"][:, sl * 1024 + k * 128:sl * 1024 + (k + 1) * 128],
                identity=idb[:]), deps=dep, inc=("pe" if k == 7 else None))
        state["xtb_rd"][sl] = tt
        ev = E("vector", lambda v, s: v.tensor_copy(
            out=evac_out_fn(), in_=H["ptr"][:].rearrange("p (k x) -> p k x", k=8)),
            deps=tmax(tt), inc="dve")
        state["ptr_rd"] = ev
        return cv, tt, ev

    t_c0 = E("sync", lambda s_, s: s_.dma_start(out=H["onesf"][0:1, :], in_=IN["onesf"][:]), inc="ld")
    t_c1 = E("sync", lambda s_, s: s_.dma_start(out=H["identb"][:], in_=IN["ident"][:]), inc="ld")
    t_c2 = E("sync", lambda s_, s: s_.dma_start(out=H["mask4"][:], in_=IN["mask4"][:]), inc="ld")
    t_c3 = E("sync", lambda s_, s: s_.dma_start(out=H["lrow"][0:1, 0:1024], in_=IN["ln_w"][:]), inc="ld")
    t_c4 = E("sync", lambda s_, s: s_.dma_start(out=H["lrow"][0:1, 1024:2048], in_=IN["ln_b"][:]), inc="ld")
    t_eps = E("vector", lambda v, s: v.memset(H["epsc"][:], 1e-5), inc="dve")
    t_rep = []
    for i, (dst, off) in enumerate((("replnw", 0), ("replnw", 512), ("replnb", 1024), ("replnb", 1536))):
        pq = ("psA", "psB")[i % 2]
        war = t_rep[i - 2] if i >= 2 else None
        t = E("tensor", lambda te, s, pq=pq, off=off: te.matmul(
            out=H[pq][:, 0:512], lhsT=H["onesf"][0:1, :],
            rhs=H["lrow"][0:1, off:off + 512], start=True, stop=True),
            deps=tmax(t_c0, t_c3, t_c4, war), inc="pe")
        t_rep.append(E("vector", lambda v, s, pq=pq, dst=dst, off=off: v.tensor_copy(
            out=H[dst][:, off % 1024:off % 1024 + 512], in_=H[pq][:, 0:512]),
            deps=tmax(t), inc="dve"))
    t_repd = tmax(*t_rep)
    w_ev = []
    DSTS = ("wqT", "wkT", "wvT")
    for r in range(24):
        tl = E("sync", lambda s_, s, r=r: s_.dma_start(
            out=H["wstg"][:, (r % 2) * 1024:(r % 2 + 1) * 1024],
            in_=IN["w_qkv"][r * 128:(r + 1) * 128, :]),
            deps=tmax(state["cv_prev"][r % 2]), inc="ld")
        sl = state["xtb"]
        state["xtb"] = 1 - sl
        cv = E("scalar", lambda sc, s, sl=sl, r=r: sc.copy(
            out=H["xtb"][:, sl * 1024:(sl + 1) * 1024],
            in_=H["wstg"][:, (r % 2) * 1024:(r % 2 + 1) * 1024]),
            deps=tmax(tl, state["xtb_rd"][sl]), inc="act")
        tt = None
        for k in range(8):
            dep = tmax(cv, state["ptr_rd"],
                       (tmax(t_c1, t_rep[-1]) if r == 0 else None)) if k == 0 else ()
            tt = E("tensor", lambda te, s, sl=sl, k=k: te.transpose(
                out=H["ptr"][:, k * 128:(k + 1) * 128],
                in_=H["xtb"][:, sl * 1024 + k * 128:sl * 1024 + (k + 1) * 128],
                identity=idb[:]), deps=dep, inc=("pe" if k == 7 else None))
        state["xtb_rd"][sl] = tt
        evl = None
        for half in range(2):
            j = 2 * r + half
            n, sel = j // 3, j % 3
            dst = DSTS[sel]
            evl = E("vector", lambda v, s, dst=dst, n=n, half=half: v.tensor_copy(
                out=H[dst][:].rearrange("p (k n c) -> p k n c", k=8, n=16)[:, :, n, :],
                in_=H["ptr"][:].rearrange("p (k x) -> p k x", k=8)[:, :, half * 64:(half + 1) * 64]),
                deps=tmax(tt) if half == 0 else (), inc="dve")
        state["ptr_rd"] = evl
        state["cv_prev"][r % 2] = cv
        w_ev.append(evl)
    for r in range(8):
        tl = E("sync", lambda s_, s, r=r: s_.dma_start(
            out=H["wstg"][:, (r % 2) * 1024:(r % 2 + 1) * 1024],
            in_=IN["w_out"][r * 128:(r + 1) * 128, :]),
            deps=tmax(state["cv_prev"][r % 2]), inc="ld")
        cv, tt, ev = ct_tile(
            tl, (lambda r=r: H["wstg"][:, (r % 2) * 1024:(r % 2 + 1) * 1024]),
            (lambda r=r: H["woutT"][:].rearrange("p (k x) -> p k x", k=8)[:, :, r * 128:(r + 1) * 128]))
        state["cv_prev"][r % 2] = cv
        w_ev.append(ev)
    t_w_done = tmax(*w_ev)

    prev_x2_rd = [None] * 4
    prev_qk_rd = None
    prev_sm_rd = {}
    prev_store = None
    prev_mvar_act = [None, None]
    prev_attfm_rd = None
    prev_att_ps_rd = [None, None]
    prev_psD_rd = [None] * 4
    prev_exf3_rd = [None] * 4
    prev_smf_rd = [None] * 4
    prev_aT_rd = [None] * 4
    prev_aT2_rd = [None] * 4
    prev_ptr3_rd = [None, None]
    prev_r32_rd = [None, None]
    prev_x3t_rd = None

    if upto == "w":
        E("gpsimd", lambda g, s: g.dma_start(out=x3_out[0:128, :], in_=H["wqT"][:, 0:1024].bitcast(F32) if False else H["replnw"][:, 0:1024]), deps=tmax(t_w_done, t_repd), inc="st")
        E("gpsimd", lambda g, s: g.nop(), deps=[("st", ctx.cnt["st"])], inc=None)
        return
    for sg in range(nsg):
        t_x2 = []
        t_fm = []
        for i in range(4):
            tl = E("sync", lambda s_, s, sg=sg, i=i: s_.dma_start(
                out=H["x2sb"][:, i * 1024:(i + 1) * 1024],
                in_=IN["x2"][(sg * 4 + i) * 128:(sg * 4 + i + 1) * 128, :]),
                deps=tmax(prev_x2_rd[i]), inc="ld")
            t_x2.append(tl)
            c0 = (i % 2) * 16
            t1 = E("vector", lambda v, s, i=i, c0=c0: v.bn_stats(
                out=H["mvar"][:, c0:c0 + 6], in_=H["x2sb"][:, i * 1024:i * 1024 + 512]),
                deps=tmax(tl, prev_mvar_act[i % 2]), inc="dve")
            t1b = E("vector", lambda v, s, i=i, c0=c0: v.bn_stats(
                out=H["mvar"][:, c0 + 6:c0 + 12], in_=H["x2sb"][:, i * 1024 + 512:i * 1024 + 1024]),
                deps=tmax(t1), inc="dve")
            t2 = E("vector", lambda v, s, c0=c0: v.bn_aggr(
                out=H["mvar"][:, c0 + 12:c0 + 14], in_=H["mvar"][:, c0:c0 + 12]),
                deps=tmax(t1b), inc="dve")
            t3 = E("scalar", lambda sc, s, c0=c0: sc.activation(
                out=H["mvar"][:, c0 + 14:c0 + 15], in_=H["mvar"][:, c0 + 13:c0 + 14],
                func=AF.Sqrt, bias=H["epsc"][:, 0:1], scale=1.0), deps=tmax(t2, t_eps), inc="act")
            prev_mvar_act[i % 2] = t3
            t4 = E("vector", lambda v, s, c0=c0: v.reciprocal(
                out=H["mvar"][:, c0 + 15:c0 + 16], in_=H["mvar"][:, c0 + 14:c0 + 15]),
                deps=tmax(t3), inc="dve")
            xsl = i % 2
            t5 = E("vector", lambda v, s, i=i, c0=c0, xsl=xsl: v.tensor_scalar(
                out=H["xin"][:, xsl * 1024:(xsl + 1) * 1024],
                in0=H["x2sb"][:, i * 1024:(i + 1) * 1024],
                scalar1=H["mvar"][:, c0 + 12:c0 + 13], scalar2=H["mvar"][:, c0 + 15:c0 + 16],
                op0=ALU.subtract, op1=ALU.mult), deps=tmax(t4), inc="dve")
            t6 = E("vector", lambda v, s, xsl=xsl: v.tensor_tensor(
                out=H["xin"][:, xsl * 1024:(xsl + 1) * 1024],
                in0=H["xin"][:, xsl * 1024:(xsl + 1) * 1024], in1=H["replnw"][:, 0:1024],
                op=ALU.mult), deps=tmax(t5, t_repd), inc="dve")
            t7 = E("vector", lambda v, s, xsl=xsl: v.tensor_tensor(
                out=H["xin"][:, xsl * 1024:(xsl + 1) * 1024],
                in0=H["xin"][:, xsl * 1024:(xsl + 1) * 1024], in1=H["replnb"][:, 0:1024],
                op=ALU.add), deps=tmax(t6), inc="dve")
            cv, tt, ev = ct_tile(
                t7, (lambda xsl=xsl: H["xin"][:, xsl * 1024:(xsl + 1) * 1024]),
                (lambda i=i: H["xinfm"][:].rearrange("p (k x) -> p k x", k=8)[:, :, i * 128:(i + 1) * 128]),
                war_deps=(tmax(t_w_done) if (sg == 0 and i == 0) else ()))
            t_fm.append(ev)
        t_fm_done = tmax(*t_fm)

        t_q = []
        t_k = []
        psq = ("psA", "psB")
        for np_ in range(8):
            for which, dst, tlist in (("q", "qfm", t_q), ("k", "kfm", t_k)):
                qsel = 0 if which == "q" else 1
                i = len(t_q) + len(t_k)
                pq = psq[i % 2]
                dep = tmax(t_fm_done, tlist[np_ - 1] if np_ >= 1 else prev_qk_rd,
                           t_q[np_ - 1] if (which == "k" and np_ >= 1) else None,
                           (t_q[-1] if (which == "k" and t_q) else None))
                wsrc = "wqT" if qsel == 0 else "wkT"
                mm = None
                for k in range(8):
                    mm = E("tensor", lambda te, s, np_=np_, k=k, pq=pq, wsrc=wsrc: te.matmul(
                        out=H[pq][:, 0:512],
                        lhsT=H[wsrc][:, k * 1024 + np_ * 128:k * 1024 + (np_ + 1) * 128],
                        rhs=H["xinfm"][:, k * 512:(k + 1) * 512],
                        start=(k == 0), stop=(k == 7)),
                        deps=(dep if k == 0 else ()), inc=("pe" if k == 7 else None))
                tlist.append(E("vector", lambda v, s, np_=np_, dst=dst, pq=pq, which=which: v.tensor_scalar(
                    out=H[dst][:, np_ * 512:(np_ + 1) * 512], in0=H[pq][:, 0:512],
                    scalar1=(0.03125 if which == "q" else 1.0), scalar2=None, op0=ALU.mult),
                    deps=tmax(mm), inc="dve"))
        t_v = []
        for tc in range(4):
            for vh in range(2):
                i = len(t_v)
                pq = psq[i % 2]
                dep = tmax(t_fm_done, t_v[i - 2] if i >= 2 else t_k[-1],
                           t_k[-2] if i < 2 else None)
                mm = None
                for k in range(8):
                    mm = E("tensor", lambda te, s, tc=tc, vh=vh, k=k, pq=pq: te.matmul(
                        out=H[pq][:, 0:512],
                        lhsT=H["xinfm"][:, k * 512 + tc * 128:k * 512 + (tc + 1) * 128],
                        rhs=H["wvT"][:, k * 1024 + vh * 512:k * 1024 + (vh + 1) * 512],
                        start=(k == 0), stop=(k == 7)),
                        deps=(dep if k == 0 else ()), inc=("pe" if k == 7 else None))
                t_v.append(E("vector", lambda v, s, tc=tc, vh=vh, pq=pq: v.tensor_copy(
                    out=H["vsb"][:, tc * 1024 + vh * 512:tc * 1024 + (vh + 1) * 512],
                    in_=H[pq][:, 0:512]), deps=tmax(mm), inc="dve"))
        t_qkv_done = tmax(t_q[-1], t_k[-1], *t_v)
        if upto == "qkv":
            E("gpsimd", lambda g, s: g.dma_start(out=x3_out[0:128, :], in_=H["x2sb"][:, 0:1024]), deps=t_qkv_done, inc="st")
            E("gpsimd", lambda g, s: g.nop(), deps=[("st", ctx.cnt["st"])], inc=None)
            return

        t_att_ev = []
        for g in range(4):
            SCB = ("psD", "psB", "psA", "psC")
            t_mm = [None] * 4
            t_msk = [None] * 4
            t_max = [None] * 4
            t_exp = [None] * 4
            t_rcp = [None] * 4
            t_ab = [None] * 4
            t_tr = [None] * 4
            t_ev = [None] * 4
            for pidx, js in enumerate(((0, 2), (1, 3))):
                if pidx == 1:
                    E("tensor", lambda te, s: te.transpose(
                        out=H["ptr"][:, 512:640], in_=idb[:], identity=idb[:]),
                        deps=(), inc=None)
                for hb in range(4):
                    psc = SCB[hb]
                    mm = None
                    for j in js:
                        n = hb * 4 + j
                        np_, ho = n // 2, (n % 2) * 64
                        dep = tmax(t_qkv_done, prev_psD_rd[hb]) if (pidx == 0 and j == js[0]) else ()
                        mm = E("tensor", lambda te, s, g=g, np_=np_, ho=ho, j=j, psc=psc: te.matmul(
                            out=H[psc][:, j * 128:(j + 1) * 128],
                            lhsT=H["qfm"][ho:ho + 64, np_ * 512 + g * 128:np_ * 512 + (g + 1) * 128],
                            rhs=H["kfm"][ho:ho + 64, np_ * 512 + g * 128:np_ * 512 + (g + 1) * 128],
                            start=True, stop=True),
                            deps=dep, inc=("pe" if (pidx == 1 and j == js[-1]) else None))
                    if pidx == 1:
                        t_mm[hb] = mm
            for hb in range(4):
                psc = SCB[hb]
                t_msk[hb] = E("vector", lambda v, s, psc=psc, hb=hb: v.tensor_tensor(
                    out=H["smf"][:, hb * 512:(hb + 1) * 512], in0=H[psc][:, 0:512],
                    in1=H["mask4"][:], op=ALU.add),
                    deps=tmax(t_mm[hb], t_c2, prev_smf_rd[hb]), inc="dve")
                prev_psD_rd[hb] = t_msk[hb]
            for hb in range(4):
                t_max[hb] = E("vector", lambda v, s, hb=hb: v.tensor_reduce(
                    out=H["smx"][:, hb * 4:hb * 4 + 4],
                    in_=H["smf"][:, hb * 512:(hb + 1) * 512].rearrange("p (j x) -> p j x", j=4),
                    axis=AX.X, op=ALU.max, negate=True), deps=tmax(t_msk[hb]), inc="dve")
            for hb in range(4):
                texp = None
                for j in range(4):
                    texp = E("scalar", lambda sc, s, hb=hb, j=j: sc.activation(
                        out=H["exf3"][:, hb * 512 + j * 128:hb * 512 + (j + 1) * 128],
                        in_=H["smf"][:, hb * 512 + j * 128:hb * 512 + (j + 1) * 128],
                        func=AF.Exp, bias=H["smx"][:, hb * 4 + j:hb * 4 + j + 1], scale=1.0,
                        accum_out=H["smx"][:, 16 + hb * 4 + j:17 + hb * 4 + j]),
                        deps=tmax(t_max[hb], prev_exf3_rd[hb]) if j == 0 else (),
                        inc=("act" if j == 3 else None))
                t_exp[hb] = texp
                prev_smf_rd[hb] = texp
            for hb in range(4):
                t_rcp[hb] = E("vector", lambda v, s, hb=hb: v.reciprocal(
                    out=H["smx"][:, 32 + hb * 4:36 + hb * 4],
                    in_=H["smx"][:, 16 + hb * 4:20 + hb * 4]), deps=tmax(t_exp[hb]), inc="dve")
            for hb in range(4):
                tab = None
                for j in range(4):
                    tab = E("scalar", lambda sc, s, hb=hb, j=j: sc.activation(
                        out=H["aT"][:, hb * 512 + j * 128:hb * 512 + (j + 1) * 128],
                        in_=H["exf3"][:, hb * 512 + j * 128:hb * 512 + (j + 1) * 128],
                        func=AF.Copy, scale=H["smx"][:, 32 + hb * 4 + j:33 + hb * 4 + j], bias=0.0),
                        deps=tmax(t_rcp[hb], prev_aT_rd[hb]) if j == 0 else (),
                        inc=("act" if j == 3 else None))
                t_ab[hb] = tab
            for hb in range(4):
                ptrh = ("ptr", "ptr2")[hb % 2]
                tt = None
                for j in range(4):
                    dep = tmax(t_ab[hb], prev_ptr3_rd[hb % 2]) if j == 0 else ()
                    tt = E("tensor", lambda te, s, hb=hb, j=j, ptrh=ptrh: te.transpose(
                        out=H[ptrh][:, j * 128:(j + 1) * 128],
                        in_=H["aT"][:, hb * 512 + j * 128:hb * 512 + (j + 1) * 128],
                        identity=idb[:]), deps=dep, inc=("pe" if j == 3 else None))
                t_tr[hb] = tt
                t_ev[hb] = E("vector", lambda v, s, hb=hb, ptrh=ptrh: v.tensor_copy(
                    out=H["aT2"][:, hb * 512:(hb + 1) * 512], in_=H[ptrh][:, 0:512]),
                    deps=tmax(tt, prev_aT2_rd[hb]), inc="dve")
                prev_ptr3_rd[hb % 2] = t_ev[hb]
            for hb in range(4):
                pb = ("psE", "psF")[hb // 2]
                mmav = None
                for j in range(4):
                    n = hb * 4 + j
                    ho = (n % 2) * 64
                    col = ((n // 2) % 4) * 128
                    dep = tmax(t_ev[hb], prev_att_ps_rd[hb // 2]) if j == 0 else ()
                    mmav = E("tensor", lambda te, s, g=g, n=n, j=j, ho=ho, col=col, pb=pb, hb=hb: te.matmul(
                        out=H[pb][ho:ho + 64, col:col + 128],
                        lhsT=H["vsb"][:, g * 1024 + n * 64:g * 1024 + (n + 1) * 64],
                        rhs=H["aT2"][:, hb * 512 + j * 128:hb * 512 + (j + 1) * 128],
                        start=True, stop=True),
                        deps=dep, inc=("pe" if j == 3 else None))
                prev_sm_rd[hb] = mmav
                prev_aT_rd[hb] = mmav
                prev_aT2_rd[hb] = mmav
                prev_exf3_rd[hb] = t_ab[hb]
            for half in range(2):
                pb = ("psE", "psF")[half]
                tev = E("vector", lambda v, s, half=half, pb=pb: v.tensor_copy(
                    out=H["attfm"][:].rearrange("p (n x) -> p n x", n=8)[:, half * 4:(half + 1) * 4, :],
                    in_=H[pb][:, 0:512].rearrange("p (n x) -> p n x", n=4)),
                    deps=tmax(prev_sm_rd[half * 2], prev_sm_rd[half * 2 + 1], prev_attfm_rd), inc="dve")
                prev_att_ps_rd[half] = tev
                t_att_ev.append(tev)
            t_att_done = tmax(t_att_ev[-1], t_att_ev[-2])
            if upto == "att2" and g == 0:
                E("gpsimd", lambda g_, s: g_.dma_start(out=x3_out[0:128, :], in_=H["x2sb"][:, 0:1024]),
                  deps=t_att_done, inc="st")
                E("gpsimd", lambda g_, s: g_.nop(), deps=[("st", ctx.cnt["st"])], inc=None)
                return
            t_r = []
            for dh in range(2):
                pq = ("psD", "psB")[dh]
                mm = None
                for k in range(8):
                    dep = tmax(t_att_done, prev_r32_rd[dh], prev_psD_rd[dh]) if k == 0 else ()
                    mm = E("tensor", lambda te, s, k=k, dh=dh, pq=pq: te.matmul(
                        out=H[pq][:, 0:512],
                        lhsT=H["attfm"][:, k * 128:(k + 1) * 128],
                        rhs=H["woutT"][:, k * 1024 + dh * 512:k * 1024 + (dh + 1) * 512],
                        start=(k == 0), stop=(k == 7)),
                        deps=(dep if k == 0 else ()), inc=("pe" if k == 7 else None))
                t_r.append(E("vector", lambda v, s, g=g, dh=dh, pq=pq: v.scalar_tensor_tensor(
                    out=H["r32"][:, (g % 2) * 1024 + dh * 512:(g % 2) * 1024 + (dh + 1) * 512],
                    in0=H["x2sb"][:, g * 1024 + dh * 512:g * 1024 + (dh + 1) * 512],
                    scalar=2.0, in1=H[pq][:, 0:512], op0=ALU.mult, op1=ALU.add),
                    deps=tmax(mm), inc="dve"))
            prev_attfm_rd = t_r[-1]
            prev_psD_rd[0] = tmax(prev_psD_rd[0], t_r[0])
            prev_psD_rd[1] = tmax(prev_psD_rd[1], t_r[1])
            c0 = 32 + (g % 2) * 16
            rb = (g % 2) * 1024
            t1 = E("vector", lambda v, s, rb=rb, c0=c0: v.bn_stats(
                out=H["mvar"][:, c0:c0 + 6], in_=H["r32"][:, rb:rb + 512]),
                deps=tmax(t_r[0], prev_mvar_act[0]), inc="dve")
            t1b = E("vector", lambda v, s, rb=rb, c0=c0: v.bn_stats(
                out=H["mvar"][:, c0 + 6:c0 + 12], in_=H["r32"][:, rb + 512:rb + 1024]),
                deps=tmax(t_r[1], t1), inc="dve")
            t2 = E("vector", lambda v, s, c0=c0: v.bn_aggr(
                out=H["mvar"][:, c0 + 12:c0 + 14], in_=H["mvar"][:, c0:c0 + 12]),
                deps=tmax(t1b), inc="dve")
            t3 = E("scalar", lambda sc, s, c0=c0: sc.activation(
                out=H["mvar"][:, c0 + 14:c0 + 15], in_=H["mvar"][:, c0 + 13:c0 + 14],
                func=AF.Sqrt, bias=H["epsc"][:, 0:1], scale=1.0), deps=tmax(t2), inc="act")
            t4 = E("vector", lambda v, s, c0=c0: v.reciprocal(
                out=H["mvar"][:, c0 + 15:c0 + 16], in_=H["mvar"][:, c0 + 14:c0 + 15]),
                deps=tmax(t3), inc="dve")
            t5 = E("vector", lambda v, s, rb=rb, c0=c0: v.tensor_scalar(
                out=H["x3t"][:, 0:1024], in0=H["r32"][:, rb:rb + 1024],
                scalar1=H["mvar"][:, c0 + 12:c0 + 13], scalar2=H["mvar"][:, c0 + 15:c0 + 16],
                op0=ALU.subtract, op1=ALU.mult),
                deps=tmax(t4, prev_store), inc="dve")
            prev_r32_rd[0] = t5
            prev_r32_rd[1] = t5
            st_t = E("gpsimd", lambda g_, s, sg=sg, g=g: g_.dma_start(
                out=x3_out[(sg * 4 + g) * 128:(sg * 4 + g + 1) * 128, :],
                in_=H["x3t"][:, 0:1024]), deps=tmax(t5), inc="st")
            prev_store = st_t
            prev_x2_rd[g] = t5 if False else tmax(t_r[-1], t5)
        prev_qk_rd = ("pe", ctx.cnt["pe"])

    E("gpsimd", lambda g, s: g.nop(), deps=[("st", ctx.cnt["st"])], inc=None)


def build_p3(nsg=8, upto='full'):
    nc = bass.Bass("TRN2", target_bir_lowering=False)
    IN = {}
    IN["x2"] = nc.dram_tensor("x2", [4096, D], F32, kind="ExternalInput").ap()
    IN["w_qkv"] = nc.dram_tensor("w_qkv", [3072, D], F32, kind="ExternalInput").ap()
    IN["w_out"] = nc.dram_tensor("w_out", [D, D], F32, kind="ExternalInput").ap()
    IN["ln_w"] = nc.dram_tensor("ln_w", [D], F32, kind="ExternalInput").ap()
    IN["ln_b"] = nc.dram_tensor("ln_b", [D], F32, kind="ExternalInput").ap()
    IN["mask4"] = nc.dram_tensor("mask4", [128, 512], F32, kind="ExternalInput").ap()
    IN["onesf"] = nc.dram_tensor("onesf", [1, 128], F32, kind="ExternalInput").ap()
    IN["ident"] = nc.dram_tensor("ident", [128, 128], BF16, kind="ExternalInput").ap()
    x3_out = nc.dram_tensor("x3o", [4096, D], F32, kind="ExternalOutput").ap()
    stack = ExitStack()
    with stack:
        H = alloc_p3(*real_allocators(nc, stack))
        ctx = Ctx()
        emit_p3(ctx, H, IN, x3_out, nsg=nsg, upto=upto)
        _finalize(ctx, nc, stack)
    return nc


def host_inputs_p3(x2_percore, inputs):
    ident = np.eye(128, dtype=np.float32).astype(BF)
    onesf = np.ones((1, 128), np.float32)
    m = np.full((128, 128), -1e9, np.float32)
    for p in range(8):
        m[p * 16:(p + 1) * 16, p * 16:(p + 1) * 16] = 0.0
    mask4 = np.ascontiguousarray(np.tile(m, (1, 4)))
    maps = []
    for c in range(N_CORES):
        maps.append({
            "x2": np.ascontiguousarray(x2_percore[c].reshape(4096, D)),
            "w_qkv": inputs["L1_attn_w_qkv"], "w_out": inputs["L1_attn_w_out"],
            "ln_w": inputs["L1_attn_ln_w"], "ln_b": inputs["L1_attn_ln_b"],
            "mask4": mask4, "onesf": onesf, "ident": ident,
        })
    return maps



NEXP4 = 8


def alloc_p4(sb, ps):
    H = {}
    H["yacc"] = sb("yacc", [128, 8 * 1024], F32)
    H["x3fm"] = sb("x3fm", [128, 8 * 1024], BF16)
    H["xtb"] = sb("xtb", [128, 2 * 1024], BF16)
    H["wgb"] = sb("wgb", [128, 2 * 4096], BF16)
    H["wub"] = sb("wub", [128, 2 * 4096], BF16)
    H["wdb"] = sb("wdb", [128, 2 * 4096], BF16)
    H["abf"] = sb("abf4", [128, 4 * 1024], BF16)
    H["gel"] = sb("gel", [128, 2 * 512], F32)
    H["wstg"] = sb("wstg", [128, 3 * 1024], F32)
    H["cw"] = sb("cw", [128, 8 * 8], F32)
    H["pbuf"] = sb("pbuf", [128, 64], F32)
    H["rstat"] = sb("rstat", [128, 16], F32)
    H["gwT"] = sb("gwT", [128, 8 * 8], F32)
    H["xf32"] = sb("xf32", [128, 1024], F32)
    H["identf"] = sb("identf_s", [128, 128], F32)
    H["repcorr"] = sb("repcorr", [128, 8], F32)
    H["repieps"] = sb("repieps", [128, 8], F32)
    H["onesf"] = sb("onesf_s", [1, 128], F32)
    H["identb"] = sb("identb", [128, 128], BF16)
    H["mvar"] = sb("mvar", [128, 32], F32)
    H["epsc"] = sb("epsc", [128, 1], F32)
    for nm in ("psA", "psB", "psC", "psD", "psE", "psF", "psG"):
        H[nm] = ps(nm, [128, 512], F32)
    H["ptr"] = ps("ptr", [128, 1024], BF16)
    H["ptr2"] = H["psG"][:].bitcast(BF16)
    return H


def emit_p4(ctx, H, IN, x4_out, dbg=None):
    E = ctx.emit
    idb = H["identb"]
    H = dict(H)
    H["x4t"] = H["gel"]
    state = {"xtb": 0, "xtb_rd": [None, None], "ptr_rd": None, "cv_prev": [None, None]}

    def ct_tile(load_tok, src_sb_ap_fn, evac_out_fn, war_deps=()):
        sl = state["xtb"]
        state["xtb"] = 1 - sl
        cv = E("scalar", lambda sc, s, sl=sl: sc.copy(
            out=H["xtb"][:, sl * 1024:(sl + 1) * 1024], in_=src_sb_ap_fn()),
            deps=tmax(load_tok, state["xtb_rd"][sl]), inc="act")
        tt = None
        for k in range(8):
            dep = tmax(cv, state["ptr_rd"], *war_deps) if k == 0 else ()
            tt = E("tensor", lambda te, s, sl=sl, k=k: te.transpose(
                out=H["ptr"][:, k * 128:(k + 1) * 128],
                in_=H["xtb"][:, sl * 1024 + k * 128:sl * 1024 + (k + 1) * 128],
                identity=idb[:]), deps=dep, inc=("pe" if k == 7 else None))
        state["xtb_rd"][sl] = tt
        ev = E("vector", lambda v, s: v.tensor_copy(
            out=evac_out_fn(), in_=H["ptr"][:].rearrange("p (k x) -> p k x", k=8)),
            deps=tmax(tt), inc="dve")
        state["ptr_rd"] = ev
        return cv, tt, ev

    t_c0 = E("sync", lambda s_, s: s_.dma_start(out=H["onesf"][0:1, :], in_=IN["onesf"][:]), inc="ld")
    t_c1 = E("sync", lambda s_, s: s_.dma_start(out=H["identb"][:], in_=IN["ident"][:]), inc="ld")
    t_c1f = E("sync", lambda s_, s: s_.dma_start(out=H["identf"][:], in_=IN["identf"][:]), inc="ld")
    t_c2 = E("sync", lambda s_, s: s_.dma_start(out=H["wstg"][0:1, 2048:2056], in_=IN["correction_bias"][:]), inc="ld")
    t_c3 = E("sync", lambda s_, s: s_.dma_start(out=H["wstg"][0:1, 2056:2064], in_=IN["ieps"][:]), inc="ld")
    t_eps = E("vector", lambda v, s: v.memset(H["epsc"][:], 1e-5), inc="dve")
    t_rep = []
    for i, (dst, off) in enumerate((("repcorr", 2048), ("repieps", 2056))):
        pq = ("psA", "psB")[i % 2]
        t = E("tensor", lambda te, s, pq=pq, off=off: te.matmul(
            out=H[pq][:, 0:8], lhsT=H["onesf"][0:1, :],
            rhs=H["wstg"][0:1, off:off + 8], start=True, stop=True),
            deps=tmax(t_c0, t_c2, t_c3), inc="pe")
        t_rep.append(E("vector", lambda v, s, pq=pq, dst=dst: v.tensor_copy(
            out=H[dst][:], in_=H[pq][:, 0:8]), deps=tmax(t), inc="dve"))
    t_repd = tmax(*t_rep)
    t_c4 = E("sync", lambda s_, s: s_.dma_start(out=H["wstg"][0:8, 0:1024], in_=IN["gate_w"][:, :]),
             deps=(), inc="ld")
    t_gwTl = []
    for k in range(8):
        t = E("tensor", lambda te, s, k=k: te.transpose(
            out=H["psG"][:, 0:8], in_=H["wstg"][0:8, k * 128:(k + 1) * 128],
            identity=H["identf"][0:8, 0:8]),
            deps=tmax(t_c4, t_c1f, t_repd, t_gwTl[-1] if t_gwTl else None), inc="pe")
        t_gwTl.append(E("vector", lambda v, s, k=k: v.tensor_copy(
            out=H["gwT"][:, k * 8:(k + 1) * 8], in_=H["psG"][:, 0:8]),
            deps=tmax(t), inc="dve"))
    t_gwT = tmax(*t_gwTl)
    state["cv_prev"] = [tmax(t_gwT), tmax(t_gwT)]

    prev_store = [None]
    prev_yacc_rd = {i: None for i in range(8)}
    prev_mvar_act = [None, None]
    slot_free = {}
    slot_stg = {}
    for kind in ("wgb", "wub", "wdb"):
        for sl in range(2):
            slot_free[(kind, sl)] = None
            slot_stg[(kind, sl)] = None
    prev_cw_rd = None
    prev_pG_rd = None
    prev_fc1_tail = None
    abf_rd = None

    def load_slot(kind, e, ST, sl, nrows, ncols, src_ap, stage_ap):
        base = sl * 4096
        dst = H[kind]
        if ST == 0:
            evs = []
            for fr in range(nrows):
                wsl = fr % 2
                tl = E("sync", lambda s_, s, fr=fr: s_.dma_start(
                    out=H["wstg"][:, wsl * 1024:wsl * 1024 + ncols] if False else H["wstg"][:, (fr % 2) * 1024:(fr % 2) * 1024 + ncols],
                    in_=src_ap[fr * 128:(fr + 1) * 128, :]),
                    deps=tmax(state["cv_prev"][wsl]), inc="ld")
                cv = E("scalar", lambda sc, s, fr=fr, base=base: sc.copy(
                    out=dst[:, base + fr * ncols:base + (fr + 1) * ncols],
                    in_=H["wstg"][:, (fr % 2) * 1024:(fr % 2) * 1024 + ncols]),
                    deps=tmax(tl, slot_free[(kind, sl)] if fr == 0 else None,
                              slot_stg[(kind, sl)] if fr == 0 else None), inc="act")
                state["cv_prev"][wsl] = cv
                evs.append(cv)
            ready = tmax(*evs)
            stg = E("scalar", lambda sc, s, base=base: sc.dma_start(
                out=stage_ap, in_=dst[:, base:base + nrows * ncols]),
                deps=ready, inc="ld2")
            return ready, stg
        else:
            t = E("sync", lambda s_, s, base=base: s_.dma_start(
                out=dst[:, base:base + nrows * ncols], in_=stage_ap),
                deps=tmax(slot_free[(kind, sl)], slot_stg[(kind, sl)]), inc="ld")
            return tmax(t), None

    for ST in range(NST):
        T0 = ST * 8
        t_fm = []
        t_x3 = []
        for i in range(8):
            tl = E("sync", lambda s_, s, T0=T0, i=i: s_.dma_start(
                out=H["yacc"][:, i * 1024:(i + 1) * 1024],
                in_=IN["x3"][(T0 + i) * 128:(T0 + i + 1) * 128, :]),
                deps=tmax(prev_yacc_rd[i]), inc="ld")
            t_x3.append(tl)
            cv, tr, ev = ct_tile(
                tl, (lambda i=i: H["yacc"][:, i * 1024:(i + 1) * 1024]),
                (lambda i=i: H["x3fm"][:].rearrange("p (k x) -> p k x", k=8)[:, :, i * 128:(i + 1) * 128]))
            t_fm.append(ev)
        t_fm_done = tmax(*t_fm)

        t_cw = []
        prev_xf32_rd = None
        for tc in range(8):
            tts = []
            for k in range(8):
                dep = tmax(t_x3[tc], prev_pG_rd, prev_xf32_rd,
                           t_gwT) if k == 0 else (tmax(tts[-1]) if k == 4 else ())
                t = E("tensor", lambda te, s, tc=tc, k=k: te.transpose(
                    out=H["psG"][:, (k % 4) * 128:(k % 4 + 1) * 128],
                    in_=H["yacc"][:, tc * 1024 + k * 128:tc * 1024 + (k + 1) * 128],
                    identity=H["identf"][:]),
                    deps=dep, inc=("pe" if (k % 4) == 3 else None))
                if (k % 4) == 3:
                    tts.append(E("vector", lambda v, s, k=k: v.tensor_copy(
                        out=H["xf32"][:, (k - 3) * 128:(k + 1) * 128], in_=H["psG"][:, 0:512]),
                        deps=tmax(t), inc="dve"))
            t_xf = tmax(*tts)
            mm = None
            for k in range(8):
                dep = tmax(t_xf) if k == 0 else ()
                mm = E("tensor", lambda te, s, tc=tc, k=k: te.matmul(
                    out=H["psG"][:, 0:8],
                    lhsT=H["xf32"][:, k * 128:(k + 1) * 128],
                    rhs=H["gwT"][:, k * 8:(k + 1) * 8],
                    start=(k == 0), stop=(k == 7)),
                    deps=dep, inc=("pe" if k == 7 else None))
            prev_xf32_rd = mm
            r = H["rstat"]
            prv = t_cw[-1] if t_cw else None
            t1 = E("scalar", lambda sc, s: sc.activation(
                out=H["pbuf"][:, 0:8], in_=H["psG"][:, 0:8], func=AF.Sigmoid,
                bias=0.0, scale=1.0), deps=tmax(mm, prv), inc="act")
            prev_pG_rd = t1
            t2 = E("vector", lambda v, s: v.tensor_tensor(
                out=H["pbuf"][:, 8:16], in0=H["pbuf"][:, 0:8], in1=H["repcorr"][:],
                op=ALU.add), deps=tmax(t1, t_repd, prv), inc="dve")
            t3 = E("vector", lambda v, s: v.tensor_tensor(
                out=r[:, 0:4], in0=H["pbuf"][:, 8:16:2], in1=H["pbuf"][:, 9:16:2],
                op=ALU.add), deps=tmax(t2), inc="dve")
            t4 = E("vector", lambda v, s: v.reduce_max(
                out=r[:, 4:5], in_=r[:, 0:4], axis=AX.X), deps=tmax(t3), inc="dve")
            t5 = E("vector", lambda v, s: v.tensor_scalar(
                out=H["pbuf"][:, 16:20], in0=r[:, 0:4], scalar1=r[:, 4:5],
                scalar2=None, op0=ALU.is_ge), deps=tmax(t4), inc="dve")
            t6 = E("vector", lambda v, s: v.scalar_tensor_tensor(
                out=H["pbuf"][:, 20:24], in0=H["pbuf"][:, 16:20], scalar=-1e6,
                in1=r[:, 0:4], op0=ALU.mult, op1=ALU.add), deps=tmax(t5), inc="dve")
            t7 = E("vector", lambda v, s: v.reduce_max(
                out=r[:, 5:6], in_=H["pbuf"][:, 20:24], axis=AX.X), deps=tmax(t6), inc="dve")
            t8 = E("vector", lambda v, s: v.tensor_scalar(
                out=H["pbuf"][:, 16:20], in0=r[:, 0:4], scalar1=r[:, 5:6],
                scalar2=None, op0=ALU.is_ge), deps=tmax(t7), inc="dve")
            t9a = E("vector", lambda v, s: v.tensor_copy(
                out=H["pbuf"][:, 24:32:2], in_=H["pbuf"][:, 16:20]), deps=tmax(t8), inc="dve")
            t9b = E("vector", lambda v, s: v.tensor_copy(
                out=H["pbuf"][:, 25:32:2], in_=H["pbuf"][:, 16:20]), deps=tmax(t8), inc="dve")
            t10 = E("vector", lambda v, s: v.tensor_tensor(
                out=H["pbuf"][:, 32:40], in0=H["pbuf"][:, 8:16], in1=H["pbuf"][:, 24:32],
                op=ALU.mult), deps=tmax(t9a, t9b), inc="dve")
            t11 = E("vector", lambda v, s: v.tensor_tensor(
                out=H["pbuf"][:, 32:40], in0=H["pbuf"][:, 32:40], in1=H["repieps"][:],
                op=ALU.add), deps=tmax(t10), inc="dve")
            cur = 32
            tprev = t11
            for it in range(3):
                ta = E("vector", lambda v, s, cur=cur, it=it: v.reduce_max(
                    out=r[:, 6 + it:7 + it], in_=H["pbuf"][:, cur:cur + 8], axis=AX.X),
                    deps=tmax(tprev), inc="dve")
                tb = E("vector", lambda v, s, cur=cur, it=it: v.tensor_scalar(
                    out=H["pbuf"][:, 40:48], in0=H["pbuf"][:, cur:cur + 8],
                    scalar1=r[:, 6 + it:7 + it], scalar2=None, op0=ALU.is_ge),
                    deps=tmax(ta), inc="dve")
                tc_ = E("vector", lambda v, s, cur=cur: v.scalar_tensor_tensor(
                    out=H["pbuf"][:, 48:56], in0=H["pbuf"][:, 40:48], scalar=-1e6,
                    in1=H["pbuf"][:, cur:cur + 8], op0=ALU.mult, op1=ALU.add),
                    deps=tmax(tb), inc="dve")
                td = E("vector", lambda v, s: v.tensor_copy(
                    out=H["pbuf"][:, 32:40] if False else H["pbuf"][:, 56:64],
                    in_=H["pbuf"][:, 48:56]), deps=tmax(tc_), inc="dve")
                cur = 56
                tprev = td
            tm4 = E("vector", lambda v, s, cur=cur: v.reduce_max(
                out=r[:, 9:10], in_=H["pbuf"][:, cur:cur + 8], axis=AX.X),
                deps=tmax(tprev), inc="dve")
            tsel = E("vector", lambda v, s: v.tensor_scalar(
                out=H["pbuf"][:, 40:48], in0=H["pbuf"][:, 32:40], scalar1=r[:, 9:10],
                scalar2=None, op0=ALU.is_ge), deps=tmax(tm4), inc="dve")
            ttw = E("vector", lambda v, s: v.tensor_tensor(
                out=H["pbuf"][:, 48:56], in0=H["pbuf"][:, 0:8], in1=H["pbuf"][:, 40:48],
                op=ALU.mult), deps=tmax(tsel), inc="dve")
            tsum = E("vector", lambda v, s: v.reduce_sum(
                out=r[:, 10:11], in_=H["pbuf"][:, 48:56], axis=AX.X), deps=tmax(ttw), inc="dve")
            tad = E("vector", lambda v, s: v.tensor_scalar(
                out=r[:, 11:12], in0=r[:, 10:11], scalar1=1e-20, scalar2=None,
                op0=ALU.add), deps=tmax(tsum), inc="dve")
            trc = E("vector", lambda v, s: v.reciprocal(
                out=r[:, 12:13], in_=r[:, 11:12]), deps=tmax(tad), inc="dve")
            tcw = E("vector", lambda v, s, tc=tc: v.tensor_scalar(
                out=H["cw"][:, tc * 8:(tc + 1) * 8], in0=H["pbuf"][:, 48:56],
                scalar1=r[:, 12:13], scalar2=2.5, op0=ALU.mult, op1=ALU.mult),
                deps=tmax(trc, prev_cw_rd), inc="dve")
            t_cw.append(tcw)
        t_cw_done = tmax(*t_cw)
        if dbg is not None and ST == 0:
            E("gpsimd", lambda g, s: g.dma_start(out=dbg["cw"][:], in_=H["cw"][:]),
              deps=t_cw_done, inc="st")

        t_y_prev2 = [prev_yacc_rd[6], prev_yacc_rd[7]]
        t_y_last = None
        for e in range(NEXP4 + 1):
            shared = (e == NEXP4)
            sl = e % 2
            if shared:
                wg_r, wg_s = load_slot("wgb", e, ST, sl, 8, 512, IN["Wg_s"], IN["wgs_st"][:])
                wu_r, wu_s = load_slot("wub", e, ST, sl, 8, 512, IN["Wu_s"], IN["wus_st"][:])
                wd_r, wd_s = load_slot("wdb", e, ST, sl, 4, 1024, IN["Wd_s"], IN["wds_st"][:])
            else:
                wg_r, wg_s = load_slot("wgb", e, ST, sl, 8, 512, IN["Wg"][e], IN["wg_st"][e])
                wu_r, wu_s = load_slot("wub", e, ST, sl, 8, 512, IN["Wu"][e], IN["wu_st"][e])
                wd_r, wd_s = load_slot("wdb", e, ST, sl, 4, 1024, IN["Wd"][e], IN["wd_st"][e])
            b = sl * 4096
            t_a = []
            for ic in range(4):
                for th in range(2):
                    gi = ic * 2 + th
                    pg = ("psA", "psC")[gi % 2]
                    px = ("psB", "psD")[gi % 2]
                    mmg = None
                    for k in range(8):
                        dep = tmax(wg_r, t_fm_done,
                                   t_a[gi - 2][2] if gi >= 2 else prev_fc1_tail) if k == 0 else ()
                        mmg = E("tensor", lambda te, s, ic=ic, th=th, k=k, pg=pg, b=b: te.matmul(
                            out=H[pg][:, 0:512],
                            lhsT=H["wgb"][:, b + k * 512 + ic * 128:b + k * 512 + (ic + 1) * 128],
                            rhs=H["x3fm"][:, k * 1024 + th * 512:k * 1024 + (th + 1) * 512],
                            start=(k == 0), stop=(k == 7)),
                            deps=dep, inc=("pe" if k == 7 else None))
                    tg = E("scalar", lambda sc, s, th=th, pg=pg: sc.activation(
                        out=H["gel"][:, th * 512:(th + 1) * 512], in_=H[pg][:, 0:512],
                        func=AF.Silu, bias=0.0, scale=1.0),
                        deps=tmax(mmg, t_a[gi - 2][1] if gi >= 2 else None), inc="act")
                    mmx = None
                    for k in range(8):
                        dep = tmax(wu_r, t_a[gi - 2][1] if gi >= 2 else prev_fc1_tail) if k == 0 else ()
                        mmx = E("tensor", lambda te, s, ic=ic, th=th, k=k, px=px, b=b: te.matmul(
                            out=H[px][:, 0:512],
                            lhsT=H["wub"][:, b + k * 512 + ic * 128:b + k * 512 + (ic + 1) * 128],
                            rhs=H["x3fm"][:, k * 1024 + th * 512:k * 1024 + (th + 1) * 512],
                            start=(k == 0), stop=(k == 7)),
                            deps=dep, inc=("pe" if k == 7 else None))
                    ta1 = E("vector", lambda v, s, ic=ic, th=th, px=px: v.tensor_tensor(
                        out=H["abf"][:, ic * 1024 + th * 512:ic * 1024 + (th + 1) * 512],
                        in0=H[px][:, 0:512], in1=H["gel"][:, th * 512:(th + 1) * 512],
                        op=ALU.mult), deps=tmax(mmx, tg, abf_rd), inc="dve")
                    t_a.append((ta1, ta1, tg))
            t_a_done = tmax(*[t[0] for t in t_a])
            for tc in range(8):
                for dh in range(2):
                    gi = tc * 2 + dh
                    pp = ("psE", "psF")[gi % 2]
                    mm = None
                    for ic in range(4):
                        dep = tmax(wd_r, t_a_done, t_y_prev2[gi % 2]) if ic == 0 else ()
                        mm = E("tensor", lambda te, s, tc=tc, dh=dh, ic=ic, pp=pp, b=b: te.matmul(
                            out=H[pp][:, 0:512],
                            lhsT=H["abf"][:, ic * 1024 + tc * 128:ic * 1024 + (tc + 1) * 128],
                            rhs=H["wdb"][:, b + ic * 1024 + dh * 512:b + ic * 1024 + (dh + 1) * 512],
                            start=(ic == 0), stop=(ic == 3)),
                            deps=dep, inc=("pe" if ic == 3 else None))
                    if shared:
                        ev = E("vector", lambda v, s, tc=tc, dh=dh, pp=pp: v.tensor_tensor(
                            out=H["yacc"][:, tc * 1024 + dh * 512:tc * 1024 + (dh + 1) * 512],
                            in0=H[pp][:, 0:512],
                            in1=H["yacc"][:, tc * 1024 + dh * 512:tc * 1024 + (dh + 1) * 512],
                            op=ALU.add), deps=tmax(mm, t_x3[tc]), inc="dve")
                    else:
                        ev = E("vector", lambda v, s, tc=tc, dh=dh, pp=pp, e=e: v.scalar_tensor_tensor(
                            out=H["yacc"][:, tc * 1024 + dh * 512:tc * 1024 + (dh + 1) * 512],
                            in0=H[pp][:, 0:512], scalar=H["cw"][:, tc * 8 + e:tc * 8 + e + 1],
                            in1=H["yacc"][:, tc * 1024 + dh * 512:tc * 1024 + (dh + 1) * 512],
                            op0=ALU.mult, op1=ALU.add),
                            deps=tmax(mm, t_cw_done, t_x3[tc]), inc="dve")
                    t_y_prev2[gi % 2] = ev
                    t_y_last = ev
            prev_cw_rd = t_y_last
            for kind, stg in (("wgb", wg_s), ("wub", wu_s), ("wdb", wd_s)):
                slot_stg[(kind, sl)] = stg
            slot_free[("wgb", sl)] = t_a[-1][0]
            slot_free[("wub", sl)] = t_a[-1][0]
            slot_free[("wdb", sl)] = t_y_last
            abf_rd = t_y_last
            prev_fc1_tail = tmax(t_a[-2][0], t_a[-1][0], t_a[-2][2], t_a[-1][2])

        for tc in range(8):
            c0 = (tc % 2) * 16
            t1 = E("vector", lambda v, s, tc=tc, c0=c0: v.bn_stats(
                out=H["mvar"][:, c0:c0 + 6], in_=H["yacc"][:, tc * 1024:tc * 1024 + 512]),
                deps=tmax(t_y_last, prev_mvar_act[tc % 2]), inc="dve")
            t1b = E("vector", lambda v, s, tc=tc, c0=c0: v.bn_stats(
                out=H["mvar"][:, c0 + 6:c0 + 12], in_=H["yacc"][:, tc * 1024 + 512:tc * 1024 + 1024]),
                deps=tmax(t1), inc="dve")
            t2 = E("vector", lambda v, s, c0=c0: v.bn_aggr(
                out=H["mvar"][:, c0 + 12:c0 + 14], in_=H["mvar"][:, c0:c0 + 12]),
                deps=tmax(t1b), inc="dve")
            t3 = E("scalar", lambda sc, s, c0=c0: sc.activation(
                out=H["mvar"][:, c0 + 14:c0 + 15], in_=H["mvar"][:, c0 + 13:c0 + 14],
                func=AF.Sqrt, bias=H["epsc"][:, 0:1], scale=1.0),
                deps=tmax(t2, t_eps), inc="act")
            prev_mvar_act[tc % 2] = t3
            t4 = E("vector", lambda v, s, c0=c0: v.reciprocal(
                out=H["mvar"][:, c0 + 15:c0 + 16], in_=H["mvar"][:, c0 + 14:c0 + 15]),
                deps=tmax(t3), inc="dve")
            t5 = E("vector", lambda v, s, tc=tc, c0=c0: v.tensor_scalar(
                out=H["x4t"][:, 0:1024], in0=H["yacc"][:, tc * 1024:(tc + 1) * 1024],
                scalar1=H["mvar"][:, c0 + 12:c0 + 13], scalar2=H["mvar"][:, c0 + 15:c0 + 16],
                op0=ALU.subtract, op1=ALU.mult),
                deps=tmax(t4, prev_store[0]), inc="dve")
            st_t = E("gpsimd", lambda g, s, T0=T0, tc=tc: g.dma_start(
                out=x4_out[(T0 + tc) * 128:(T0 + tc + 1) * 128, :],
                in_=H["x4t"][:, 0:1024]), deps=tmax(t5), inc="st")
            prev_store[0] = st_t
            prev_yacc_rd[tc] = t5
        prev_fc1_tail = tmax(prev_fc1_tail, prev_store[0])

    E("gpsimd", lambda g, s: g.nop(), deps=[("st", ctx.cnt["st"])], inc=None)


def build_p4(debug=False):
    nc = bass.Bass("TRN2", target_bir_lowering=False)
    IN = {}
    IN["x3"] = nc.dram_tensor("x3", [4096, D], F32, kind="ExternalInput").ap()
    IN["gate_w"] = nc.dram_tensor("gate_w", [NEXP4, D], F32, kind="ExternalInput").ap()
    IN["correction_bias"] = nc.dram_tensor("correction_bias", [NEXP4], F32, kind="ExternalInput").ap()
    IN["Wg"] = nc.dram_tensor("Wg", [NEXP4, D, 512], F32, kind="ExternalInput").ap()
    IN["Wu"] = nc.dram_tensor("Wu", [NEXP4, D, 512], F32, kind="ExternalInput").ap()
    IN["Wd"] = nc.dram_tensor("Wd", [NEXP4, 512, D], F32, kind="ExternalInput").ap()
    IN["Wg_s"] = nc.dram_tensor("Wg_s", [D, 512], F32, kind="ExternalInput").ap()
    IN["Wu_s"] = nc.dram_tensor("Wu_s", [D, 512], F32, kind="ExternalInput").ap()
    IN["Wd_s"] = nc.dram_tensor("Wd_s", [512, D], F32, kind="ExternalInput").ap()
    IN["ieps"] = nc.dram_tensor("ieps", [NEXP4], F32, kind="ExternalInput").ap()
    IN["onesf"] = nc.dram_tensor("onesf", [1, 128], F32, kind="ExternalInput").ap()
    IN["ident"] = nc.dram_tensor("ident", [128, 128], BF16, kind="ExternalInput").ap()
    IN["identf"] = nc.dram_tensor("identf", [128, 128], F32, kind="ExternalInput").ap()
    IN["wg_st"] = nc.dram_tensor("wg_st", [NEXP4, 128, 4096], BF16, kind="Internal").ap()
    IN["wu_st"] = nc.dram_tensor("wu_st", [NEXP4, 128, 4096], BF16, kind="Internal").ap()
    IN["wd_st"] = nc.dram_tensor("wd_st", [NEXP4, 128, 4096], BF16, kind="Internal").ap()
    IN["wgs_st"] = nc.dram_tensor("wgs_st", [128, 4096], BF16, kind="Internal").ap()
    IN["wus_st"] = nc.dram_tensor("wus_st", [128, 4096], BF16, kind="Internal").ap()
    IN["wds_st"] = nc.dram_tensor("wds_st", [128, 4096], BF16, kind="Internal").ap()
    x4_out = nc.dram_tensor("x4o", [4096, D], F32, kind="ExternalOutput").ap()
    dbg = None
    if debug:
        dbg = {"cw": nc.dram_tensor("dbg_cw", [128, 64], F32, kind="ExternalOutput").ap()}
    stack = ExitStack()
    with stack:
        H = alloc_p4(*real_allocators(nc, stack))
        ctx = Ctx()
        emit_p4(ctx, H, IN, x4_out, dbg=dbg)
        _finalize(ctx, nc, stack)
    return nc


def host_inputs_p4(x3_percore, inputs):
    ident = np.eye(128, dtype=np.float32).astype(BF)
    onesf = np.ones((1, 128), np.float32)
    ieps = (-np.arange(8) * 1e-30).astype(np.float32)
    maps = []
    for c in range(N_CORES):
        maps.append({
            "x3": np.ascontiguousarray(x3_percore[c].reshape(4096, D)),
            "gate_w": inputs["L1_ffn_gate_w"], "correction_bias": inputs["L1_ffn_correction_bias"],
            "Wg": inputs["L1_ffn_Wg"], "Wu": inputs["L1_ffn_Wu"], "Wd": inputs["L1_ffn_Wd"],
            "Wg_s": inputs["L1_ffn_Wg_s"], "Wu_s": inputs["L1_ffn_Wu_s"], "Wd_s": inputs["L1_ffn_Wd_s"],
            "ieps": ieps, "onesf": onesf, "ident": ident,
            "identf": np.eye(128, dtype=np.float32),
        })
    return maps




def real_allocators(nc, stack):
    def sb(name, shape, dt):
        return stack.enter_context(nc.sbuf_tensor(name, shape, dt))
    def ps(name, shape, dt):
        return stack.enter_context(nc.psum_tensor(name, shape, dt))
    return sb, ps


class PhaseArena:

    def __init__(self, nc):
        self.nc = nc
        self.cur = 0
        self.peak = 0
        self.tensors = []

    def reset(self):
        self.cur = 0

    def sbuf(self, name, shape, dt):
        import functools
        per_part = 1
        for d in shape[1:]:
            per_part *= d
        per_part *= mybir.dt.size(dt)
        addr = (self.cur + 31) // 32 * 32
        self.cur = addr + per_part
        self.peak = max(self.peak, self.cur)
        h = self.nc.virtual_sbuf_tensor(name, shape, dt)
        self.tensors.append((h, addr))
        return h

    def materialize(self, stack):
        backing = stack.enter_context(
            self.nc.sbuf_tensor("arena", [128, self.peak], mybir.dt.uint8))
        base = self.nc.lookup_mloc(backing).addr
        for h, addr in self.tensors:
            self.nc._tensor(h.name, h.shape, h.dtype, type="SB")
            mloc = self.nc.lookup_mloc(h)
            mloc.allocated = True
            mloc.addr = base + addr


def phase_barrier(ctx):
    toks = [(s, ctx.cnt[s]) for s in SEM_ENG]
    marks = {}
    for eng, sem in (("sync", "ld"), ("scalar", "act"), ("vector", "dve"),
                     ("tensor", "pe"), ("gpsimd", "pool")):
        pass
    for eng in ENGINES:
        ctx.emit(eng, lambda api, s: api.nop(), deps=list(toks), inc=None)


def build_fused():
    nc = bass.Bass("TRN2", target_bir_lowering=False)
    dram = lambda name, shape, kind="ExternalInput", dt=F32: nc.dram_tensor(
        name, shape, dt, kind=kind).ap()
    IN1 = {}
    IN1["xo"] = dram("xo", [B, SEG, D])
    IN1["xofm"] = dram("xofm", [B, D, SEG])
    IN1["xdfm"] = dram("xdfm", [B, D, NSLOT * 128])
    for nm in ("wq", "wk", "wv", "wo"):
        IN1[nm] = dram(nm, [D, D])
    for nm in ("bq", "bk", "bv", "bo"):
        IN1[nm] = dram(nm, [D])
    IN1["wmask"] = dram("wmask", [1, 4])
    IN1["onesf"] = dram("onesf", [1, 128])
    IN1["ident"] = dram("ident", [128, 128], dt=BF16)
    IN2 = {}
    _x1t = nc.dram_tensor("x1m", [OWN, B, D], F32, kind="Internal")
    IN2["x1"] = _x1t.ap().rearrange("a b d -> (a b) d")
    IN2["gate_w"] = dram("gate_w2", [NEXP, D])
    IN2["fc1_w"] = dram("fc1_w", [NEXP, 2048, D])
    IN2["fc1_b"] = dram("fc1_b", [NEXP, 2048])
    IN2["geglu_mult"] = dram("geglu_mult", [NEXP, D])
    IN2["fc2_w"] = dram("fc2_w", [NEXP, D, D])
    IN2["fc2_b"] = dram("fc2_b", [NEXP, D])
    IN2["s_fc1_w"] = dram("s_fc1_w", [4096, D])
    IN2["s_fc1_b"] = dram("s_fc1_b", [4096])
    IN2["s_geglu_mult"] = dram("s_geglu_mult", [2048])
    IN2["s_fc2_w"] = dram("s_fc2_w", [D, 2048])
    IN2["s_fc2_b"] = dram("s_fc2_b", [D])
    IN2["ident"] = IN1["ident"]
    IN2["identf"] = dram("identf", [128, 128])
    IN2["fc1T_st"] = dram("fc1T_st", [NEXP, 128, SLOT1], kind="Internal", dt=BF16)
    IN2["fc2T_st"] = dram("fc2T_st", [NEXP, 128, SLOT2], kind="Internal", dt=BF16)
    IN2["sfc1T_st"] = dram("sfc1T_st", [128, 2 * SLOT1], kind="Internal", dt=BF16)
    IN2["sfc2T_st"] = dram("sfc2T_st", [128, 2 * SLOT2], kind="Internal", dt=BF16)
    IN3 = {}
    IN3["x2"] = dram("x2m", [4096, D], kind="Internal")
    IN3["w_qkv"] = dram("w_qkv", [3072, D])
    IN3["w_out"] = dram("w_out", [D, D])
    IN3["ln_w"] = dram("ln_w", [D])
    IN3["ln_b"] = dram("ln_b", [D])
    IN3["mask4"] = dram("mask4", [128, 512])
    IN3["onesf"] = IN1["onesf"]
    IN3["ident"] = IN1["ident"]
    IN4 = {}
    IN4["x3"] = dram("x3m", [4096, D], kind="Internal")
    IN4["gate_w"] = dram("gate_w4", [NEXP4, D])
    IN4["correction_bias"] = dram("correction_bias", [NEXP4])
    IN4["Wg"] = dram("Wg", [NEXP4, D, 512])
    IN4["Wu"] = dram("Wu", [NEXP4, D, 512])
    IN4["Wd"] = dram("Wd", [NEXP4, 512, D])
    IN4["Wg_s"] = dram("Wg_s", [D, 512])
    IN4["Wu_s"] = dram("Wu_s", [D, 512])
    IN4["Wd_s"] = dram("Wd_s", [512, D])
    IN4["ieps"] = dram("ieps", [NEXP4])
    IN4["onesf"] = IN1["onesf"]
    IN4["ident"] = IN1["ident"]
    IN4["identf"] = IN2["identf"]
    IN4["wg_st"] = dram("wg_st", [NEXP4, 128, 4096], kind="Internal", dt=BF16)
    IN4["wu_st"] = dram("wu_st", [NEXP4, 128, 4096], kind="Internal", dt=BF16)
    IN4["wd_st"] = dram("wd_st", [NEXP4, 128, 4096], kind="Internal", dt=BF16)
    IN4["wgs_st"] = dram("wgs_st", [128, 4096], kind="Internal", dt=BF16)
    IN4["wus_st"] = dram("wus_st", [128, 4096], kind="Internal", dt=BF16)
    IN4["wds_st"] = dram("wds_st", [128, 4096], kind="Internal", dt=BF16)
    x1_m = _x1t.ap()
    x2_m = IN3["x2"]
    x3_m = IN4["x3"]
    x4_out = dram("x4o", [4096, D], kind="ExternalOutput")

    stack = ExitStack()
    with stack:
        arena = PhaseArena(nc)
        pshared = {}

        def ps(name, shape, dt):
            if name not in pshared:
                pshared[name] = stack.enter_context(nc.psum_tensor(name, shape, dt))
            return pshared[name]

        Hs = []
        for allocf in (alloc_p1, alloc_p2, alloc_p3, alloc_p4):
            arena.reset()
            Hs.append(allocf(arena.sbuf, ps))
        arena.materialize(stack)
        ctx = Ctx()
        emit_p1(ctx, Hs[0], IN1, x1_m)
        phase_barrier(ctx)
        emit_p2(ctx, Hs[1], IN2, x2_m)
        phase_barrier(ctx)
        emit_p3(ctx, Hs[2], IN3, x3_m)
        phase_barrier(ctx)
        emit_p4(ctx, Hs[3], IN4, x4_out)
        _finalize(ctx, nc, stack)
    return nc


def host_inputs_fused(inputs):
    x = inputs["L0_attn_x"]
    dummy = np.zeros((4096, D), np.float32)
    m1 = host_inputs_p1(x, inputs)
    m2 = host_inputs_p2([dummy] * N_CORES, inputs)
    m3 = host_inputs_p3([dummy] * N_CORES, inputs)
    m4 = host_inputs_p4([dummy] * N_CORES, inputs)
    maps = []
    for c in range(N_CORES):
        d = dict(m1[c])
        for k, v in m2[c].items():
            if k == "x1":
                continue
            d["gate_w2" if k == "gate_w" else k] = v
        for k, v in m3[c].items():
            if k == "x2":
                continue
            d[k] = v
        for k, v in m4[c].items():
            if k == "x3":
                continue
            d["gate_w4" if k == "gate_w" else k] = v
        maps.append(d)
    return maps


def _finalize(ctx, nc, stack):
    sems = {}
    for s in SEM_ENG:
        sems[s] = stack.enter_context(nc.semaphore(s))
    with nc.Block() as block:
        @block.sync
        def _(api):
            ctx.replay("sync", api, sems)

        @block.scalar
        def _(api):
            ctx.replay("scalar", api, sems)

        @block.vector
        def _(api):
            ctx.replay("vector", api, sems)

        @block.tensor
        def _(api):
            ctx.replay("tensor", api, sems)

        @block.gpsimd
        def _(api):
            ctx.replay("gpsimd", api, sems)


def build_p1(debug=False):
    nc = bass.Bass("TRN2", target_bir_lowering=False)
    IN = {}
    IN["xo"] = nc.dram_tensor("xo", [B, SEG, D], F32, kind="ExternalInput").ap()
    IN["xofm"] = nc.dram_tensor("xofm", [B, D, SEG], F32, kind="ExternalInput").ap()
    IN["xdfm"] = nc.dram_tensor("xdfm", [B, D, NSLOT * 128], F32, kind="ExternalInput").ap()
    for nm, shape in (("wq", [D, D]), ("wk", [D, D]), ("wv", [D, D]), ("wo", [D, D])):
        IN[nm] = nc.dram_tensor(nm, shape, F32, kind="ExternalInput").ap()
    for nm in ("bq", "bk", "bv", "bo"):
        IN[nm] = nc.dram_tensor(nm, [D], F32, kind="ExternalInput").ap()
    IN["wmask"] = nc.dram_tensor("wmask", [1, 4], F32, kind="ExternalInput").ap()
    IN["onesf"] = nc.dram_tensor("onesf", [1, 128], F32, kind="ExternalInput").ap()
    IN["ident"] = nc.dram_tensor("ident", [128, 128], BF16, kind="ExternalInput").ap()
    x1_out = nc.dram_tensor("x1o", [OWN, B, D], F32, kind="ExternalOutput").ap()
    dbg = None
    if debug:
        dbg = {}
        for nm, w in (("qfm", 8 * 256), ("kfm", 8 * 512), ("kdfm", 8 * 512),
                      ("vsb", 4 * 1024), ("vdsb", 4 * 1024), ("att", 8 * 256),
                      ("xfm", 8 * 512), ("xnfm", 8 * 512), ("r32", 2048)):
            dt = F32 if nm == "r32" else BF16
            dbg[nm] = nc.dram_tensor("dbg_" + nm, [128, w], dt, kind="ExternalOutput").ap()
    stack = ExitStack()
    with stack:
        H = alloc_p1(*real_allocators(nc, stack))
        ctx = Ctx()
        emit_p1(ctx, H, IN, x1_out, dbg=dbg)
        _finalize(ctx, nc, stack)
    return nc


def host_inputs_p1(x, inputs):
    ident = np.eye(128, dtype=np.float32).astype(BF)
    onesf = np.ones((1, 128), np.float32)
    maps = []
    for c in range(N_CORES):
        s0, h0 = c // 2, c % 2
        xo = np.empty((B, SEG, D), np.float32)
        xo[:, 0:OWN] = x[:, s0 * SEG + h0 * OWN: s0 * SEG + (h0 + 1) * OWN]
        xo[:, OWN:SEG] = x[:, s0 * SEG + (1 - h0) * OWN: s0 * SEG + (2 - h0) * OWN]
        xd = np.zeros((B, NSLOT * 128, D), np.float32)
        wm = np.zeros((1, 4), np.float32)
        for j, off in enumerate((-2, -1, 1, 2)):
            n = s0 + off
            if 0 <= n < 4:
                xd[:, j * 128:(j + 1) * 128] = x[:, n * SEG: (n + 1) * SEG: 4]
                wm[0, j] = 0.25
        maps.append({
            "xo": xo,
            "xofm": np.ascontiguousarray(xo.transpose(0, 2, 1)),
            "xdfm": np.ascontiguousarray(xd.transpose(0, 2, 1)),
            "wq": inputs["L0_attn_Wq"], "bq": inputs["L0_attn_bq"],
            "wk": inputs["L0_attn_Wk"], "bk": inputs["L0_attn_bk"],
            "wv": inputs["L0_attn_Wv"], "bv": inputs["L0_attn_bv"],
            "wo": inputs["L0_attn_Wo"], "bo": inputs["L0_attn_bo"],
            "wmask": wm, "onesf": onesf, "ident": ident,
        })
    return maps


FUSED = True


def kernel(**inputs):
    inputs = {k: np.ascontiguousarray(np.asarray(v)) for k, v in inputs.items()}
    if FUSED:
        nc = build_fused()
        maps = host_inputs_fused(inputs)
        res = run_bass_kernel_spmd(nc, maps, core_ids=list(range(N_CORES)))
        out = np.empty((B, L, D), np.float32)
        for c in range(N_CORES):
            out[:, c * OWN:(c + 1) * OWN, :] = res.results[c]["x4o"].reshape(
                OWN, B, D).transpose(1, 0, 2)
        return out
    x = inputs["L0_attn_x"]
    nc1 = build_p1()
    maps1 = host_inputs_p1(x, inputs)
    res1 = run_bass_kernel_spmd(nc1, maps1, core_ids=list(range(N_CORES)))
    x1_pc = [res1.results[c]["x1o"] for c in range(N_CORES)]
    nc2 = build_p2()
    maps2 = host_inputs_p2(x1_pc, inputs)
    res2 = run_bass_kernel_spmd(nc2, maps2, core_ids=list(range(N_CORES)))
    x2_pc = [res2.results[c]["x2o"] for c in range(N_CORES)]
    nc3 = build_p3()
    maps3 = host_inputs_p3(x2_pc, inputs)
    res3 = run_bass_kernel_spmd(nc3, maps3, core_ids=list(range(N_CORES)))
    x3_pc = [res3.results[c]["x3o"] for c in range(N_CORES)]
    nc4 = build_p4()
    maps4 = host_inputs_p4(x3_pc, inputs)
    res4 = run_bass_kernel_spmd(nc4, maps4, core_ids=list(range(N_CORES)))
    out = np.empty((B, L, D), np.float32)
    for c in range(N_CORES):
        out[:, c * OWN:(c + 1) * OWN, :] = res4.results[c]["x4o"].reshape(
            OWN, B, D).transpose(1, 0, 2)
    return out
